# version 11
# baseline (speedup 1.0000x reference)
import numpy as np
import ml_dtypes
from contextlib import ExitStack

import concourse.bass as bass
import concourse.mybir as mybir
from concourse.bass_utils import run_bass_kernel_spmd

F32 = mybir.dt.float32
BF16 = mybir.dt.bfloat16
AF = mybir.ActivationFunctionType
ALU = mybir.AluOpType
AX = mybir.AxisListType

D = 4096
S = 4096
BATCH = 2
NT = BATCH * S
H = 32
DH = 128
HL = 4
NC = 8
OC = 512
TOKB = NT // NC
SCALE = 1.0 / DH
EPS = 1e-5
NEG_BIG = -30000.0

DEBUG = 0

KB = 1024


def _alibi_slopes(n_heads):
    import math
    cp2 = 2 ** math.floor(math.log2(n_heads))
    base = 2 ** (-(2 ** (-(math.log2(cp2) - 3))))
    slopes = base ** np.arange(1, cp2 + 1, dtype=np.float32)
    if cp2 != n_heads:
        extra_base = 2 ** (-(2 ** (-(math.log2(2 * cp2) - 3))))
        n_rem = min(cp2, n_heads - cp2)
        extra = extra_base ** np.arange(1, 1 + 2 * n_rem, 2, dtype=np.float32)
        slopes = np.concatenate([slopes, extra])
    return slopes.astype(np.float32)


class Ctx:

    def __init__(self, nc, sems):
        self.nc = nc
        self.sems = sems
        self.cnt = {k: 0 for k in sems}
        self.eng = None
        self.E = None

    def set_engine(self, name, eng):
        self.E = name
        self.eng = eng
        self.cnt = {k: 0 for k in self.sems}

    def dma(self, q, out, in_, sem, slow=False):
        if self.E == q:
            kw = {}
            if slow:
                kw["allow_slow_non_contiguous"] = True
            self.eng.dma_start(out=out, in_=in_, **kw).then_inc(self.sems[sem], 16)
        self.cnt[sem] += 16

    def cc(self, kind, op, ins, outs, sem):
        if self.E == "gpsimd":
            self.eng.collective_compute(
                kind, op, replica_groups=[list(range(NC))], ins=ins, outs=outs
            ).then_inc(self.sems[sem])
        self.cnt[sem] += 1

    def mm(self, out, lhsT, rhs, start, stop, sem=None):
        if self.E == "tensor":
            i = self.nc.tensor.matmul(out, lhsT, rhs, start=start, stop=stop)
            if sem is not None:
                i.then_inc(self.sems[sem], 1)
        if sem is not None:
            self.cnt[sem] += 1

    def op(self, eng_name, fn, sem=None):
        if self.E == eng_name:
            i = fn(self.eng)
            if sem is not None and i is not None:
                i.then_inc(self.sems[sem], 1)
        if sem is not None:
            self.cnt[sem] += 1

    def wait(self, who, sem, val):
        assert 0 <= val <= self.cnt[sem], (who, sem, val, self.cnt[sem])
        if self.E == who:
            self.eng.wait_ge(self.sems[sem], val)


def build_nc():
    nc = bass.Bass("TRN2", target_bir_lowering=False)

    t_x_rows = nc.dram_tensor("x_rows", [TOKB, D], F32, kind="ExternalInput")
    t_x_cols = nc.dram_tensor("x_cols", [NT, OC], F32, kind="ExternalInput")
    t_wqkv = nc.dram_tensor("w_qkv_slice", [D, 3 * OC], F32, kind="ExternalInput")
    t_bqkv = nc.dram_tensor("b_qkv_pp", [128, 12], F32, kind="ExternalInput")
    t_wproj = nc.dram_tensor("w_proj_cols", [D, OC], F32, kind="ExternalInput")
    t_bproj = nc.dram_tensor("b_proj_row", [1, OC], F32, kind="ExternalInput")
    t_wbT = nc.dram_tensor("w_baseT", [D, OC], F32, kind="ExternalInput")
    t_bbase = nc.dram_tensor("b_base_row", [1, OC], F32, kind="ExternalInput")
    t_wrA = nc.dram_tensor("wrA_host", [128, 32 * 136], F32, kind="ExternalInput")
    t_bfl = nc.dram_tensor("b_flat_my", [128, OC], F32, kind="ExternalInput")
    t_ident = nc.dram_tensor("ident", [128, 128], F32, kind="ExternalInput")
    t_ones = nc.dram_tensor("ones_sq", [128, 128], F32, kind="ExternalInput")
    t_onesr = nc.dram_tensor("ones_row", [1, 128], F32, kind="ExternalInput")
    t_iota = nc.dram_tensor("iota_k", [128, 32], F32, kind="ExternalInput")
    t_masks = nc.dram_tensor("diag_masks", [128, 4 * 512], BF16,
                             kind="ExternalInput")
    t_slope = nc.dram_tensor("slopes_row", [1, HL], F32, kind="ExternalInput")
    t_ksq = nc.dram_tensor("ksq3", [3, HL * 4480], BF16, kind="ExternalInput")

    t_out = nc.dram_tensor("out_cols", [NT, OC], F32, kind="ExternalOutput")

    d_ln1t_in = nc.dram_tensor("ln1t_in", [D, TOKB], BF16)
    d_ln1t_all = nc.dram_tensor("ln1t_all", [NC * D, TOKB], BF16, addr_space="Shared")
    d_qkvt = nc.dram_tensor("qkvt", [3 * OC, NT], BF16)
    d_attnt_in = nc.dram_tensor("attnt_in", [OC, NT], BF16)
    d_attnt_all = nc.dram_tensor("attnt_all", [NC * OC, NT], BF16, addr_space="Shared")
    d_ln2t_inA = nc.dram_tensor("ln2t_inA", [OC, S], BF16)
    d_ln2t_inB = nc.dram_tensor("ln2t_inB", [OC, S], BF16)
    d_ln2t_allA = nc.dram_tensor("ln2t_allA", [NC * OC, S], BF16,
                                 addr_space="Shared")
    d_ln2t_allB = nc.dram_tensor("ln2t_allB", [NC * OC, S], BF16,
                                 addr_space="Shared")
    d_stA_in = nc.dram_tensor("ln2statsA_in", [128, 64], F32)
    d_stA_out = nc.dram_tensor("ln2statsA_out", [128, 64], F32,
                               addr_space="Shared")
    d_stB_in = nc.dram_tensor("ln2statsB_in", [128, 64], F32)
    d_stB_out = nc.dram_tensor("ln2statsB_out", [128, 64], F32,
                               addr_space="Shared")
    d_trunk = nc.dram_tensor("trunk", [NT, OC], F32)

    dbg = {}
    if DEBUG:
        dbg["ln1t"] = nc.dram_tensor("dbg_ln1t", [D, TOKB], BF16, kind="ExternalOutput")
        dbg["qkvt"] = nc.dram_tensor("dbg_qkvt", [3 * OC, NT], BF16, kind="ExternalOutput")
        dbg["attnt"] = nc.dram_tensor("dbg_attnt", [OC, NT], BF16, kind="ExternalOutput")
        dbg["trunk"] = nc.dram_tensor("dbg_trunk", [NT, OC], F32, kind="ExternalOutput")
        dbg["ln2tA"] = nc.dram_tensor("dbg_ln2tA", [OC, S], BF16, kind="ExternalOutput")
        dbg["ln2tB"] = nc.dram_tensor("dbg_ln2tB", [OC, S], BF16, kind="ExternalOutput")
        dbg["stats"] = nc.dram_tensor("dbg_stats", [128, 128], F32, kind="ExternalOutput")
    if DEBUG >= 2:
        dbg["probe"] = nc.dram_tensor("dbg_probe", [128, 2048], F32, kind="ExternalOutput")
        dbg["wrA"] = nc.dram_tensor("dbg_wrA", [128, 32 * 136], BF16, kind="ExternalOutput")
        dbg["wbt"] = nc.dram_tensor("dbg_wbt", [128, 16384], BF16, kind="ExternalOutput")
        dbg["bflat"] = nc.dram_tensor("dbg_bflat", [128, 512], BF16, kind="ExternalOutput")

    es = ExitStack()
    final_counts = []
    with es:
        E = es.enter_context

        ident_bf = E(nc.sbuf_tensor("ident_bf", [128, 128], BF16))
        ones_bf = E(nc.sbuf_tensor("ones_bf", [128, 128], BF16))
        onesr_bf = E(nc.sbuf_tensor("onesr_bf", [1, 128], BF16))
        onesr_f = E(nc.sbuf_tensor("onesr_f", [1, 128], F32))
        iota_k = E(nc.sbuf_tensor("iota_k_sb", [128, 32], F32))
        masks_bf = E(nc.sbuf_tensor("masks_bf", [128, 4 * 512], BF16))
        slope_row = E(nc.sbuf_tensor("slope_row", [1, HL], F32))
        slope_pp = E(nc.sbuf_tensor("slope_pp", [128, HL], F32))
        bias_row = E(nc.sbuf_tensor("bias_row", [128, 1], F32))
        bqkv_pp = E(nc.sbuf_tensor("bqkv_pp", [128, 12], F32))
        bproj_bf = E(nc.sbuf_tensor("bproj_bf", [1, OC], BF16))
        bbase_bf = E(nc.sbuf_tensor("bbase_bf", [1, OC], BF16))
        wrA = E(nc.sbuf_tensor("wrA", [128, 32 * 136], BF16))
        bflat = E(nc.sbuf_tensor("bflat", [128, 4 * 128], BF16))
        stats1 = E(nc.sbuf_tensor("stats1", [128, 128], F32))
        stats2 = E(nc.sbuf_tensor("stats2", [128, 128], F32))
        s1 = E(nc.sbuf_tensor("s1", [128, 1], F32))
        s2 = E(nc.sbuf_tensor("s2", [128, 1], F32))
        meanb = E(nc.sbuf_tensor("meanb", [128, 1], F32))
        varb = E(nc.sbuf_tensor("varb", [128, 1], F32))
        rstdb = E(nc.sbuf_tensor("rstdb", [128, 1], F32))
        tm1 = E(nc.sbuf_tensor("tm1", [128, 8], F32))
        tm2 = E(nc.sbuf_tensor("tm2", [128, 8], F32))
        qmax2 = E(nc.sbuf_tensor("qmax2", [128, 1], F32))
        kmax2 = E(nc.sbuf_tensor("kmax2", [128, 1], F32))
        qkmax = E(nc.sbuf_tensor("qkmax", [128, 1], F32))
        p8 = E(nc.sbuf_tensor("p8", [128, 8], F32))
        selw = E(nc.sbuf_tensor("selw", [128, 8], F32))
        cmb = E(nc.sbuf_tensor("cmb", [128, 8], F32))
        hw_bf = E(nc.sbuf_tensor("hw_bf", [128, 128], BF16))
        hT_bf = E(nc.sbuf_tensor("hT_bf", [128, 128], BF16))
        hT_bf2 = E(nc.sbuf_tensor("hT_bf2", [128, 128], BF16))
        dnm = E(nc.sbuf_tensor("dnm", [128, 1], F32))

        AR_BYTES = 185 * KB
        arena = E(nc.sbuf_tensor("arena", [128, AR_BYTES // 2], BF16))

        def abf(off_kb, nbytes):
            o = int(off_kb * KB)
            assert o + nbytes <= AR_BYTES
            return arena[:, o // 2:(o + nbytes) // 2]

        def af32(off_kb, nbytes):
            return abf(off_kb, nbytes).bitcast(F32)

        wq_sb = abf(0, 96 * KB)
        ksq3v = abf(0, 35 * KB)[0:3, :]
        wbaseT = abf(0, 32 * KB)
        wproj_sb = abf(64, 32 * KB)

        xbuf = [af32(96, 16 * KB), af32(112, 16 * KB)]
        sqbuf = af32(128, 16 * KB)
        ln1_tok = abf(144, 8 * KB)
        ln1t_st = abf(152, 8 * KB)
        lnh = [abf(96, 16 * KB), abf(112, 16 * KB),
               abf(128, 16 * KB), abf(144, 16 * KB)]
        ktb = [abf(96, 8 * KB), abf(120, 8 * KB)]
        vtb = [abf(104, 8 * KB), abf(128, 8 * KB)]
        qtb = [abf(112, 8 * KB), abf(136, 8 * KB)]
        vsb2b = [abf(144, 8 * KB), abf(164, 8 * KB)]
        sq_bf = abf(152, 8 * KB)
        attbuf = [abf(96, 32 * KB), abf(128, 32 * KB)]
        l2buf = [abf(96, 32 * KB), abf(128, 32 * KB)]
        expb2 = [abf(160, 2 * KB), abf(162, 2 * KB)]
        rec = af32(178, 2 * KB)
        att_st = [abf(180, 1 * KB), abf(181, 1 * KB)]
        qkv_st = [abf(183, 1 * KB), abf(184, 1 * KB)]
        xcbuf = [af32(160, 2 * KB), af32(162, 2 * KB)]
        trk_st = [af32(164, 2 * KB), af32(166, 2 * KB)]
        sqc = af32(168, 2 * KB)
        trkb2 = [af32(170, 2 * KB), af32(172, 2 * KB)]
        ln2c2 = [abf(174, 1 * KB), abf(160, 1 * KB)]
        ln2t_st2 = [abf(175, 1 * KB), abf(161, 1 * KB)]
        trkb3 = [af32(176, 2 * KB), af32(178, 2 * KB)]
        out_st = [af32(180, 2 * KB), af32(182, 2 * KB)]
        ru2 = [af32(32, 1 * KB), af32(33, 1 * KB)]

        PA = E(nc.psum_tensor("PA", [128, 1024], F32))
        PB = E(nc.psum_tensor("PB", [128, 1024], F32))
        PS_ = E(nc.psum_tensor("PS", [128, 512], F32))
        PV = E(nc.psum_tensor("PV", [128, 512], F32))
        PT = E(nc.psum_tensor("PT", [128, 512], F32))
        PU = E(nc.psum_tensor("PU", [128, 512], F32))
        SC4 = [PA[:, 0:512], PA[:, 512:1024], PB[:, 0:512], PB[:, 512:1024]]

        sem_names = [
            "ld_sy", "ld_g", "st_ln1t", "st_qkv", "st_att", "st_trk", "st_stats",
            "st_ln2t", "st_out", "ld_ln", "ld_pair", "ld_att", "ld_xc", "ld_trk2",
            "ld_l2", "ld_trk3", "t_mm", "s_ev", "v_ms",
            "cc_ln1", "cc_att", "cc_st", "cc_stB", "cc_ln2", "cc_ln2B",
        ]
        sems = {n: E(nc.semaphore(n)) for n in sem_names}
        block = E(nc.Block())
        ctx = Ctx(nc, sems)

        def script(X: Ctx):
            W, G, V, Sc, T = "sync", "gpsimd", "vector", "scalar", "tensor"
            STQ = G

            X.dma(W, iota_k[:], t_iota[:, :], "ld_sy")
            X.dma(W, slope_row[:], t_slope[:, :], "ld_sy")
            X.dma(W, bqkv_pp[:], t_bqkv[:, :], "ld_sy")
            X.dma(W, onesr_f[:], t_onesr[:, :], "ld_sy")
            n_sy1 = X.cnt["ld_sy"]
            X.dma(G, ident_bf[:], t_ident[:, :], "ld_g")
            X.dma(G, ones_bf[:], t_ones[:, :], "ld_g")
            X.dma(G, onesr_bf[:], t_onesr[:, :], "ld_g")
            X.dma(G, masks_bf[:], t_masks[:, :], "ld_g")
            X.dma(G, bproj_bf[:], t_bproj[:, :], "ld_g")
            X.dma(G, bbase_bf[:], t_bbase[:, :], "ld_g")
            X.dma(G, wrA[:, :], t_wrA[:, :], "ld_g")
            X.dma(G, bflat[:, :], t_bfl[:, :], "ld_g")
            n_g0 = X.cnt["ld_g"]
            src_wq = t_wqkv.rearrange("(dt p) j -> p dt j", p=128)
            dst_wq = wq_sb.rearrange("p (dt j) -> p dt j", j=1536)
            for i in range(4):
                X.dma(G, dst_wq[:, i * 8:(i + 1) * 8, :],
                      src_wq[:, i * 8:(i + 1) * 8, :], "ld_g")
            n_g_wq = X.cnt["ld_g"]

            X.wait(T, "ld_sy", n_sy1)
            X.mm(PU[:, 0:HL], onesr_f[0:1, :], slope_row[0:1, :], True, True, "t_mm")
            X.wait(V, "t_mm", X.cnt["t_mm"])
            X.op(V, lambda e: e.tensor_copy(out=slope_pp[:], in_=PU[:, 0:HL]), "v_ms")
            v_slope = X.cnt["v_ms"]

            ev_ln0 = X.cnt["s_ev"]
            for i in range(8):
                if i >= 2:
                    X.wait(W, "v_ms", v_slope + 4 * i - 4)
                X.dma(W, xbuf[i % 2], t_x_rows[i * 128:(i + 1) * 128, :], "ld_sy")
                ld_here = X.cnt["ld_sy"]
                X.wait(V, "ld_sy", ld_here)
                X.op(V, lambda e, b=xbuf[i % 2]: e.reduce_sum(s1[:], b, axis=AX.X))
                X.wait(Sc, "ld_sy", ld_here)
                if i > 0:
                    X.wait(Sc, "v_ms", v_slope + 4 * i - 3)
                X.op(Sc, lambda e, b=xbuf[i % 2]: e.square(sqbuf, b), "s_ev")
                sq_done = X.cnt["s_ev"]
                X.wait(V, "s_ev", sq_done)
                X.op(V, lambda e: e.reduce_sum(s2[:], sqbuf, axis=AX.X))
                X.op(V, lambda e: e.tensor_scalar(out=meanb[:], in0=s1[:],
                                                  scalar1=1.0 / D, scalar2=None,
                                                  op0=ALU.mult), "v_ms")
                X.wait(V, "v_ms", X.cnt["v_ms"])
                X.op(V, lambda e: e.tensor_scalar(out=tm1[:, 0:1], in0=meanb[:],
                                                  scalar1=meanb[:, 0:1], scalar2=EPS,
                                                  op0=ALU.mult, op1=ALU.subtract),
                     "v_ms")
                X.wait(V, "v_ms", X.cnt["v_ms"])
                X.op(V, lambda e: e.scalar_tensor_tensor(
                    out=varb[:], in0=s2[:], scalar=1.0 / D, in1=tm1[:, 0:1],
                    op0=ALU.mult, op1=ALU.subtract), "v_ms")
                var_done = X.cnt["v_ms"]
                X.wait(Sc, "v_ms", var_done)
                X.op(Sc, lambda e: e.activation(out=varb[:], in_=varb[:], func=AF.Ln),
                     "s_ev")
                X.wait(Sc, "s_ev", X.cnt["s_ev"])
                X.op(Sc, lambda e: e.activation(out=rstdb[:], in_=varb[:],
                                                func=AF.Exp, scale=-0.5), "s_ev")
                rstd_done = X.cnt["s_ev"]
                X.wait(V, "s_ev", rstd_done)
                X.op(V, lambda e, b=xbuf[i % 2]: e.tensor_scalar(
                    out=ln1_tok, in0=b, scalar1=meanb[:, 0:1], scalar2=rstdb[:, 0:1],
                    op0=ALU.subtract, op1=ALU.mult), "v_ms")
                nrm_done = X.cnt["v_ms"]
                assert nrm_done == v_slope + 4 * i + 4
                X.wait(T, "ld_g", n_g0)
                X.wait(T, "v_ms", nrm_done)
                for grp in range(8):
                    if i > 0 or grp > 0:
                        X.wait(T, "s_ev", rstd_done + grp)
                    for j in range(4):
                        dt = grp * 4 + j
                        X.mm(PT[:, j * 128:(j + 1) * 128],
                             ln1_tok[:, dt * 128:(dt + 1) * 128], ident_bf[:],
                             True, True, "t_mm" if j == 3 else None)
                    X.wait(Sc, "t_mm", X.cnt["t_mm"])
                    if grp == 0 and i > 0:
                        X.wait(Sc, "st_ln1t", i * 16)
                    X.op(Sc, lambda e, g=grp: e.activation(
                        out=ln1t_st[:, g * 512:(g + 1) * 512], in_=PT[:, 0:512],
                        func=AF.Copy), "s_ev")
                X.wait(STQ, "s_ev", X.cnt["s_ev"])
                X.dma(STQ, d_ln1t_in[:, i * 128:(i + 1) * 128]
                      .rearrange("(dt p) t -> p dt t", p=128),
                      ln1t_st.rearrange("p (dt t) -> p dt t", t=128), "st_ln1t")
            X.wait(G, "st_ln1t", 8 * 16)
            X.cc("AllGather", ALU.bypass, [d_ln1t_in.ap().opt()],
                 [d_ln1t_all.ap().opt()], "cc_ln1")
            if DEBUG:
                X.wait(W, "st_ln1t", 8 * 16)
                X.dma(W, dbg["ln1t"][:, :], d_ln1t_in[:, :], "st_out")

            X.wait(W, "cc_ln1", 1)
            ev_1a = X.cnt["s_ev"]
            mm_1a = X.cnt["t_mm"]

            def lnslot(tokg, h):
                return (2 * tokg + h) % 4

            n_ld_ln = 0
            colg_idx = 0
            for tokg in range(16):
                r = tokg // 2
                c0 = (tokg % 2) * 512
                for h in range(2):
                    if tokg >= 2:
                        X.wait(W, "t_mm", mm_1a + (tokg - 1) * 12)
                    src = d_ln1t_all[r * D:(r + 1) * D, c0:c0 + 512] \
                        .rearrange("(dt p) t -> p dt t", p=128)
                    X.dma(W, lnh[lnslot(tokg, h)]
                          .rearrange("p (dt t) -> p dt t", t=512),
                          src[:, h * 16:(h + 1) * 16, :], "ld_ln")
                    n_ld_ln += 16
                for colg in range(12):
                    ps = SC4[colg_idx % 4]
                    X.wait(T, "ld_ln", n_ld_ln)
                    X.wait(T, "ld_g", n_g_wq)
                    if colg_idx >= 4:
                        X.wait(T, "s_ev", ev_1a + colg_idx - 3)
                    for dt in range(32):
                        hh = dt // 16
                        lhs = wq_sb[:, dt * 1536 + colg * 128:
                                    dt * 1536 + (colg + 1) * 128]
                        rhs = lnh[lnslot(tokg, hh)][:, (dt % 16) * 512:
                                                    (dt % 16 + 1) * 512]
                        X.mm(ps, lhs, rhs, dt == 0, dt == 31,
                             "t_mm" if dt == 31 else None)
                    X.wait(Sc, "t_mm", X.cnt["t_mm"])
                    if colg_idx >= 2:
                        X.wait(Sc, "st_qkv", (colg_idx - 1) * 16)
                    X.op(Sc, lambda e, ps=ps, cg=colg, st=qkv_st[colg_idx % 2]:
                         e.activation(out=st, in_=ps, func=AF.Identity,
                                      bias=bqkv_pp[:, cg:cg + 1]), "s_ev")
                    X.wait(STQ, "s_ev", X.cnt["s_ev"])
                    g0 = tokg * 512
                    X.dma(STQ, d_qkvt[colg * 128:(colg + 1) * 128, g0:g0 + 512],
                          qkv_st[colg_idx % 2][:, :], "st_qkv")
                    colg_idx += 1
            assert colg_idx == 192
            if DEBUG:
                X.wait(W, "st_qkv", 192 * 16)
                X.dma(W, dbg["qkvt"][:, :], d_qkvt[:, :], "st_out")

            X.wait(G, "t_mm", mm_1a + 192)
            src_wp = t_wproj.rearrange("(g p) o -> p g o", p=128)
            dst_wp = wproj_sb.rearrange("p (g o) -> p g o", o=512)
            for i in range(2):
                X.dma(G, dst_wp[:, i * 16:(i + 1) * 16, :],
                      src_wp[:, i * 16:(i + 1) * 16, :], "ld_g")
            n_g_wproj = X.cnt["ld_g"]

            st_att_n = 0
            n_ld_pair = 0
            mm_1a_end = X.cnt["t_mm"]
            pair_mm_end = {}
            ksq_ld = {}
            exp_hist = []
            att_v_last = 0
            qt_ctr = 0
            gidx = 0

            def pair_load(p_idx):
                nonlocal n_ld_pair
                hl, b = p_idx // 2, p_idx % 2
                par = p_idx % 2
                X.wait(W, "st_qkv", 192 * 16)
                if p_idx >= 2:
                    X.wait(W, "t_mm", pair_mm_end[p_idx - 2])
                for (buf, row0) in ((ktb[par], 512 + hl * 128),
                                    (vtb[par], 1024 + hl * 128),
                                    (qtb[par], hl * 128)):
                    X.dma(W, buf, d_qkvt[row0:row0 + 128, b * S:(b + 1) * S],
                          "ld_pair")
                    n_ld_pair += 16

            X.wait(W, "t_mm", mm_1a_end)
            X.dma(W, ksq3v, t_ksq[:, :], "ld_sy")
            ksq_ld_n = X.cnt["ld_sy"]

            inj_evac = {}

            def vtrans_grp(pt, j, warm=False):
                pb = PT if j % 2 == 0 else PU
                X.wait(T, "ld_pair", (pt + 1) * 3 * 16)
                if warm and j < 2:
                    X.wait(T, "s_ev", X.cnt["s_ev"])
                    X.wait(T, "v_ms", X.cnt["v_ms"])
                if (pt, j - 2) in inj_evac:
                    X.wait(T, "v_ms", inj_evac[(pt, j - 2)])
                elif (pt - 1, j + 6) in inj_evac:
                    X.wait(T, "v_ms", inj_evac[(pt - 1, j + 6)])
                for jj in range(4):
                    kt = j * 4 + jj
                    X.mm(pb[:, jj * 128:(jj + 1) * 128],
                         vtb[pt % 2][:, kt * 128:(kt + 1) * 128], ident_bf[:],
                         True, True, "t_mm" if jj == 3 else None)
                X.wait(V, "t_mm", X.cnt["t_mm"])
                X.op(V, lambda e, g=j, pb=pb, pt=pt: e.tensor_copy(
                    out=vsb2b[pt % 2][:, g * 512:(g + 1) * 512],
                    in_=pb[:, 0:512]), "v_ms")
                inj_evac[(pt, j)] = X.cnt["v_ms"]

            pair_load(0)
            pair_load(1)
            for p_idx in range(8):
                hl, b = p_idx // 2, p_idx % 2
                par = p_idx % 2
                vsb = vsb2b[par]

                if p_idx == 0:
                    X.wait(T, "ld_sy", ksq_ld_n)
                    for j in range(8):
                        vtrans_grp(0, j, warm=True)
                X.wait(V, "s_ev", X.cnt["s_ev"])
                X.op(V, lambda e, h=hl: e.tensor_scalar(
                    out=bias_row[:], in0=iota_k[:, 0:1],
                    scalar1=slope_pp[:, h:h + 1], scalar2=None,
                    op0=ALU.mult), "v_ms")
                bias_done = X.cnt["v_ms"]

                groups = [(qt, g) for qt in range(8) for g in range(2 * qt + 2)]
                sc_end = {}
                ex_end = {}
                rs_end = {}

                def emit_rsav(k):
                    nonlocal st_att_n, att_v_last, qt_ctr
                    qtk, gk = groups[k]
                    slot = expb2[(gbase + k) % 2]
                    first = gk == 0
                    last = gk == 2 * qtk + 1
                    X.wait(T, "s_ev", ex_end[k])
                    if first:
                        X.wait(T, "v_ms", att_v_last)
                    if p_idx >= 1:
                        jm = (2 * gk + 1) // 4
                        X.wait(T, "v_ms", inj_evac[(p_idx, jm)])
                    for j in range(2):
                        X.mm(PS_[:, :], ones_bf[:], slot[:, j * 512:(j + 1) * 512],
                             first and j == 0, last and j == 1, None)
                    for j in range(2):
                        kt = 2 * gk + j
                        X.mm(PV[:, :], vsb[:, kt * 128:(kt + 1) * 128],
                             slot[:, j * 512:(j + 1) * 512],
                             first and j == 0, last and j == 1,
                             "t_mm" if j == 1 else None)
                    rs_end[k] = X.cnt["t_mm"]
                    if last:
                        X.wait(V, "t_mm", X.cnt["t_mm"])
                        X.op(V, lambda e: e.reciprocal(rec, PS_[:, :]), "v_ms")
                        X.wait(V, "v_ms", X.cnt["v_ms"])
                        sidx = qt_ctr % 2
                        if qt_ctr >= 2:
                            X.wait(V, "st_att", st_att_n - 16)
                        X.op(V, lambda e, st=att_st[sidx]: e.tensor_tensor(
                            out=st, in0=PV[:, :], in1=rec, op=ALU.mult), "v_ms")
                        att_v_last = X.cnt["v_ms"]
                        X.wait(W, "v_ms", att_v_last)
                        X.dma(W, d_attnt_in[hl * 128:(hl + 1) * 128,
                                            b * S + qtk * 512: b * S + (qtk + 1) * 512],
                              att_st[sidx][:, :], "st_att")
                        st_att_n += 16
                        qt_ctr += 1

                gbase = gidx
                inj_sched = ({9 * j + 4: j for j in range(8)}
                             if p_idx + 1 < 8 else {})
                for i, (qt, g) in enumerate(groups):
                    if i in inj_sched:
                        vtrans_grp(p_idx + 1, inj_sched[i])
                    ps = PA if (gbase + i) % 2 == 0 else PB
                    slot = expb2[(gbase + i) % 2]
                    diag = g >= 2 * qt
                    if gbase + i >= 2:
                        X.wait(T, "s_ev", exp_hist[gbase + i - 2])
                    for j in range(2):
                        kt = 2 * g + j
                        o = 4 * qt - kt
                        k0 = hl * 4480 + (o + 3) * 128
                        X.mm(ps[:, j * 512:(j + 1) * 512], ones_bf[0:3, :],
                             ksq3v[:, k0:k0 + 512], True, False)
                        X.mm(ps[:, j * 512:(j + 1) * 512],
                             ktb[par][:, kt * 128:(kt + 1) * 128],
                             qtb[par][:, qt * 512:(qt + 1) * 512], False, True,
                             "t_mm" if j == 1 else None)
                    sc_end[i] = X.cnt["t_mm"]
                    if i >= 2:
                        emit_rsav(i - 2)
                    if diag:
                        X.wait(V, "t_mm", sc_end[i])
                        moff = 0 if g == 2 * qt else 1024
                        X.op(V, lambda e, ps=ps, mo=moff: e.tensor_tensor(
                            out=ps[:, 0:1024], in0=ps[:, 0:1024],
                            in1=masks_bf[:, mo:mo + 1024], op=ALU.add), "v_ms")
                        X.wait(Sc, "v_ms", X.cnt["v_ms"])
                    X.wait(Sc, "t_mm", rs_end[i - 2] if i >= 2 else sc_end[i])
                    if i == 0:
                        X.wait(Sc, "v_ms", bias_done)
                    X.op(Sc, lambda e, ps=ps, st=slot: e.activation(
                        out=st, in_=ps[:, 0:1024], func=AF.Exp,
                        bias=bias_row[:, 0:1], scale=SCALE), "s_ev")
                    ex_end[i] = X.cnt["s_ev"]
                    exp_hist.append(ex_end[i])
                emit_rsav(len(groups) - 2)
                emit_rsav(len(groups) - 1)
                gidx = gbase + len(groups)
                pair_mm_end[p_idx] = X.cnt["t_mm"]
                if p_idx + 2 < 8:
                    pair_load(p_idx + 2)

            X.wait(G, "st_att", st_att_n)
            X.cc("AllGather", ALU.bypass, [d_attnt_in.ap().opt()],
                 [d_attnt_all.ap().opt()], "cc_att")
            if DEBUG:
                X.wait(W, "st_att", st_att_n)
                X.dma(W, dbg["attnt"][:, :], d_attnt_in[:, :], "st_out")

            X.wait(G, "cc_att", 1)
            X.wait(G, "t_mm", pair_mm_end[7])
            src_wb = t_wbT.rearrange("(dt p) o -> p dt o", p=128)
            dst_wb = wbaseT.rearrange("p (dt o) -> p dt o", o=512)
            for i in range(2):
                X.dma(G, dst_wb[:, i * 16:(i + 1) * 16, :],
                      src_wb[:, i * 16:(i + 1) * 16, :], "ld_g")
            n_g_wbt = X.cnt["ld_g"]

            X.wait(W, "cc_att", 1)
            mm_2a = X.cnt["t_mm"]
            ev_2a = X.cnt["s_ev"]
            vm_2a = X.cnt["v_ms"]
            n_ld_att = 0
            n_ld_xc = 0
            n_st_trk = 0
            for tg2 in range(16):
                parg = tg2 % 2
                if tg2 >= 2:
                    X.wait(W, "t_mm", mm_2a + (tg2 - 1) * 4)
                src = d_attnt_all[:, tg2 * 512:(tg2 + 1) * 512] \
                    .rearrange("(g p) t -> p g t", p=128)
                dst = attbuf[parg].rearrange("p (g t) -> p g t", t=512)
                for q in range(4):
                    X.dma(W, dst[:, q * 8:(q + 1) * 8, :],
                          src[:, q * 8:(q + 1) * 8, :], "ld_att")
                    n_ld_att += 16
                for tt in range(4):
                    tokt = tg2 * 4 + tt
                    t0 = tokt * 128
                    spar = tokt % 2
                    if tokt >= 2:
                        X.wait(W, "v_ms", vm_2a + (tokt - 2) * 2 + 1)
                    X.dma(W, xcbuf[spar], t_x_cols[t0:t0 + 128, :], "ld_xc")
                    n_ld_xc += 16
                    ps = SC4[tokt % 4]
                    X.wait(T, "ld_att", n_ld_att)
                    X.wait(T, "ld_g", n_g_wproj)
                    if tokt >= 4:
                        X.wait(T, "v_ms", vm_2a + (tokt - 4) * 2 + 1)
                    X.mm(ps, onesr_bf[0:1, :], bproj_bf[0:1, :], True, False)
                    for g in range(32):
                        X.mm(ps, attbuf[parg][:, g * 512 + tt * 128:
                                              g * 512 + tt * 128 + 128],
                             wproj_sb[:, g * 512:(g + 1) * 512],
                             False, g == 31, "t_mm" if g == 31 else None)
                    mm_here = X.cnt["t_mm"]
                    X.wait(V, "t_mm", mm_here)
                    X.wait(V, "ld_xc", n_ld_xc)
                    if tokt >= 2:
                        X.wait(V, "st_trk", n_st_trk - 16)
                        X.wait(V, "s_ev", ev_2a + tokt - 1)
                    X.op(V, lambda e, ps=ps, sp=spar: e.tensor_tensor(
                        out=trk_st[sp], in0=ps, in1=xcbuf[sp], op=ALU.add), "v_ms")
                    X.wait(V, "v_ms", X.cnt["v_ms"])
                    X.op(V, lambda e, sp=spar, tk=tokt: e.reduce_sum(
                        stats1[:, tk:tk + 1], trk_st[sp], axis=AX.X))
                    X.wait(Sc, "v_ms", X.cnt["v_ms"])
                    X.op(Sc, lambda e, sp=spar: e.square(sqc, trk_st[sp]), "s_ev")
                    X.wait(V, "s_ev", X.cnt["s_ev"])
                    X.op(V, lambda e, tk=tokt: e.reduce_sum(
                        stats1[:, 64 + tk:64 + tk + 1], sqc, axis=AX.X), "v_ms")
                    X.wait(STQ, "v_ms", X.cnt["v_ms"] - 1)
                    X.dma(STQ, d_trunk[t0:t0 + 128, :], trk_st[spar][:, :], "st_trk")
                    n_st_trk += 16
                if tg2 == 7:
                    X.wait(STQ, "v_ms", X.cnt["v_ms"])
                    X.dma(STQ, d_stA_in[:, 0:32], stats1[:, 0:32], "st_stats")
                    X.dma(STQ, d_stA_in[:, 32:64], stats1[:, 64:96], "st_stats")
                    X.wait(G, "st_stats", 32)
                    X.cc("AllReduce", ALU.add, [d_stA_in.ap().opt()],
                         [d_stA_out.ap().opt()], "cc_st")
            ev_2a_end = X.cnt["s_ev"]
            X.wait(STQ, "v_ms", X.cnt["v_ms"])
            X.dma(STQ, d_stB_in[:, 0:32], stats1[:, 32:64], "st_stats")
            X.dma(STQ, d_stB_in[:, 32:64], stats1[:, 96:128], "st_stats")
            X.wait(G, "st_stats", 64)
            X.cc("AllReduce", ALU.add, [d_stB_in.ap().opt()],
                 [d_stB_out.ap().opt()], "cc_stB")
            if DEBUG:
                X.wait(W, "st_trk", n_st_trk)
                X.dma(W, dbg["trunk"][:, :], d_trunk[:, :], "st_out")

            half_rstd_ev = {}

            def stat_math(h):
                X.wait(W, "cc_st" if h == 0 else "cc_stB", 1)
                src = d_stA_out if h == 0 else d_stB_out
                X.dma(W, stats2[:, h * 64:(h + 1) * 64], src[:, :], "ld_sy")
                n_syh = X.cnt["ld_sy"]
                mean_h = stats1[:, h * 32:(h + 1) * 32]
                rstd_h = stats1[:, 64 + h * 32:64 + (h + 1) * 32]
                sqm_h = sqc[:, h * 96:h * 96 + 32]
                var_h = sqc[:, h * 96 + 32:h * 96 + 64]
                lnv_h = sqc[:, h * 96 + 64:h * 96 + 96]
                X.wait(V, "ld_sy", n_syh)
                if h == 0:
                    X.wait(V, "s_ev", ev_2a_end)
                X.op(V, lambda e: e.tensor_scalar(
                    out=mean_h, in0=stats2[:, h * 64:h * 64 + 32],
                    scalar1=1.0 / D, scalar2=None, op0=ALU.mult), "v_ms")
                X.wait(V, "v_ms", X.cnt["v_ms"])
                X.op(V, lambda e: e.scalar_tensor_tensor(
                    out=sqm_h, in0=mean_h, scalar=1.0, in1=mean_h,
                    op0=ALU.mult, op1=ALU.mult), "v_ms")
                X.wait(V, "v_ms", X.cnt["v_ms"])
                X.op(V, lambda e: e.scalar_tensor_tensor(
                    out=var_h, in0=stats2[:, h * 64 + 32:h * 64 + 64],
                    scalar=1.0 / D, in1=sqm_h,
                    op0=ALU.mult, op1=ALU.subtract), "v_ms")
                X.wait(V, "v_ms", X.cnt["v_ms"])
                X.op(V, lambda e: e.tensor_scalar(
                    out=var_h, in0=var_h, scalar1=EPS, scalar2=None,
                    op0=ALU.add), "v_ms")
                X.wait(Sc, "v_ms", X.cnt["v_ms"])
                X.op(Sc, lambda e: e.activation(out=lnv_h, in_=var_h,
                                                func=AF.Ln), "s_ev")
                X.wait(Sc, "s_ev", X.cnt["s_ev"])
                X.op(Sc, lambda e: e.activation(out=rstd_h, in_=lnv_h,
                                                func=AF.Exp, scale=-0.5), "s_ev")
                half_rstd_ev[h] = X.cnt["s_ev"]

            stat_math(0)
            n_ld_trk2 = 0
            norm_v = {}
            tr_mm = {}
            evac_ev = {}

            def st_ln2(k):
                X.wait(W, "s_ev", evac_ev[k])
                tgt = d_ln2t_inA if k < 32 else d_ln2t_inB
                t0h = (k % 32) * 128
                X.dma(W, tgt[:, t0h:t0h + 128]
                      .rearrange("(j p) t -> p j t", p=128),
                      ln2t_st2[k % 2].rearrange("p (j t) -> p j t", t=128),
                      "st_ln2t")

            for tokt in range(64):
                if tokt == 32:
                    stat_math(1)
                h, loc, q = tokt // 32, tokt % 32, tokt % 2
                t0 = tokt * 128
                if tokt >= 2:
                    X.wait(W, "v_ms", norm_v[tokt - 2])
                X.wait(W, "st_trk", (tokt + 1) * 16)
                X.dma(W, trkb2[q], d_trunk[t0:t0 + 128, :], "ld_trk2")
                n_ld_trk2 += 16
                if tokt >= 1:
                    st_ln2(tokt - 1)
                X.wait(V, "ld_trk2", n_ld_trk2)
                X.wait(V, "s_ev", half_rstd_ev[h])
                if tokt >= 2:
                    X.wait(V, "t_mm", tr_mm[tokt - 2])
                X.op(V, lambda e, q=q, h=h, lc=loc, sp=q: e.tensor_scalar(
                    out=ln2c2[q], in0=trkb2[sp],
                    scalar1=stats1[:, h * 32 + lc:h * 32 + lc + 1],
                    scalar2=stats1[:, 64 + h * 32 + lc:64 + h * 32 + lc + 1],
                    op0=ALU.subtract, op1=ALU.mult), "v_ms")
                norm_v[tokt] = X.cnt["v_ms"]
                X.wait(T, "v_ms", norm_v[tokt])
                if tokt >= 2:
                    X.wait(T, "s_ev", evac_ev[tokt - 2])
                pb = PT if q == 0 else PU
                for j in range(4):
                    X.mm(pb[:, j * 128:(j + 1) * 128],
                         ln2c2[q][:, j * 128:(j + 1) * 128],
                         ident_bf[:], True, True, "t_mm" if j == 3 else None)
                tr_mm[tokt] = X.cnt["t_mm"]
                X.wait(Sc, "t_mm", tr_mm[tokt])
                if tokt >= 2:
                    X.wait(Sc, "st_ln2t", (tokt - 1) * 16)
                X.op(Sc, lambda e, q=q, pb=pb: e.activation(
                    out=ln2t_st2[q], in_=pb[:, 0:512], func=AF.Copy), "s_ev")
                evac_ev[tokt] = X.cnt["s_ev"]
            st_ln2(63)
            X.wait(G, "st_ln2t", 64 * 16)
            X.cc("AllGather", ALU.bypass, [d_ln2t_inA.ap().opt()],
                 [d_ln2t_allA.ap().opt()], "cc_ln2")
            X.cc("AllGather", ALU.bypass, [d_ln2t_inB.ap().opt()],
                 [d_ln2t_allB.ap().opt()], "cc_ln2B")
            if DEBUG:
                X.wait(W, "st_ln2t", 64 * 16)
                X.dma(W, dbg["ln2tA"][:, :], d_ln2t_inA[:, :], "st_out")
                X.dma(W, dbg["ln2tB"][:, :], d_ln2t_inB[:, :], "st_out")
                X.wait(W, "s_ev", X.cnt["s_ev"])
                X.dma(W, dbg["stats"][:, :], stats1[:, :], "st_out")

            X.wait(W, "cc_ln2", 1)
            mm_2c = X.cnt["t_mm"]
            n_ld_l2 = 0
            n_ld_trk3 = 0
            st_out0 = X.cnt["st_out"]
            psb_l = [SC4[0], SC4[1], SC4[2]]
            psu_l = [SC4[3], PS_, PV]
            mm_main = {}
            mm_lora = {}
            mm_tr = {}
            v_hw = {}
            v_out = {}
            ev_ru = {}
            ev_hT = {}

            def close_tokt(k):
                qk = k % 2
                pbk = PT if qk == 0 else PU
                hT = hT_bf if qk == 0 else hT_bf2
                psb = psb_l[k % 3]
                X.wait(T, "v_ms", v_hw[k])
                if k >= 2:
                    X.wait(T, "s_ev", ev_hT[k - 2])
                X.mm(pbk[:, 0:128], hw_bf[:], ident_bf[:], True, True, "t_mm")
                mm_tr[k] = X.cnt["t_mm"]
                X.wait(Sc, "t_mm", X.cnt["t_mm"])
                X.op(Sc, lambda e, hT=hT, pbk=pbk: e.activation(
                    out=hT[:], in_=pbk[:, 0:128], func=AF.Copy), "s_ev")
                ev_hT[k] = X.cnt["s_ev"]
                X.wait(T, "s_ev", ev_hT[k])
                X.mm(psb, hT[:], bflat[:, 0:512], False, True, "t_mm")
                mm_lora[k] = X.cnt["t_mm"]
                X.wait(V, "t_mm", mm_lora[k])
                X.wait(V, "ld_trk3", (k + 1) * 16)
                if k >= 2:
                    X.wait(V, "st_out", st_out0 + (k - 1) * 16)
                X.op(V, lambda e, ps=psb, sp=k % 2: e.tensor_tensor(
                    out=out_st[sp], in0=ps, in1=trkb3[sp], op=ALU.add), "v_ms")
                v_out[k] = X.cnt["v_ms"]
                X.wait(STQ, "v_ms", v_out[k])
                X.dma(STQ, t_out[k * 128:k * 128 + 128, :],
                      out_st[k % 2][:, :], "st_out")

            for tg2 in range(16):
                parg = tg2 % 2
                if tg2 == 0:
                    X.wait(W, "cc_ln2B", 1)
                    X.wait(W, "t_mm", mm_2c)
                if tg2 >= 2:
                    X.wait(W, "t_mm", mm_main[tg2 * 4 - 5])
                l2src = d_ln2t_allA if tg2 < 8 else d_ln2t_allB
                c0 = (tg2 % 8) * 512
                src = l2src[:, c0:c0 + 512].rearrange("(g p) t -> p g t", p=128)
                dst = l2buf[parg].rearrange("p (g t) -> p g t", t=512)
                for q in range(4):
                    X.dma(W, dst[:, q * 8:(q + 1) * 8, :],
                          src[:, q * 8:(q + 1) * 8, :], "ld_l2")
                    n_ld_l2 += 16
                for tt in range(4):
                    tokt = tg2 * 4 + tt
                    t0 = tokt * 128
                    if tokt >= 2:
                        X.wait(W, "v_ms", v_out.get(tokt - 2, 0))
                    X.dma(W, trkb3[tokt % 2], d_trunk[t0:t0 + 128, :], "ld_trk3")
                    n_ld_trk3 += 16
                    psb = psb_l[tokt % 3]
                    psu = psu_l[tokt % 3]
                    X.wait(T, "ld_l2", n_ld_l2)
                    X.wait(T, "ld_g", n_g_wbt)
                    if tokt >= 3:
                        X.wait(T, "v_ms", v_out[tokt - 3])
                        X.wait(T, "s_ev", ev_ru[tokt - 3])
                    X.mm(psb, onesr_bf[0:1, :], bbase_bf[0:1, :], True, False)
                    for dt in range(32):
                        lhs = l2buf[parg][:, dt * 512 + tt * 128:
                                          dt * 512 + tt * 128 + 128]
                        X.mm(psb, lhs, wbaseT[:, dt * 512:(dt + 1) * 512],
                             False, False)
                        X.mm(psu[:, 0:136], lhs, wrA[:, dt * 136:(dt + 1) * 136],
                             dt == 0, dt == 31, "t_mm" if dt == 31 else None)
                    mm_main[tokt] = X.cnt["t_mm"]
                    X.wait(Sc, "t_mm", mm_main[tokt])
                    if tokt >= 2:
                        X.wait(Sc, "v_ms", v_hw[tokt - 2])
                    X.op(Sc, lambda e, u=psu, r=ru2[tokt % 2]: e.activation(
                        out=r[:, 0:136], in_=u[:, 0:136], func=AF.Copy), "s_ev")
                    ev_ru[tokt] = X.cnt["s_ev"]
                    if tokt >= 1:
                        close_tokt(tokt - 1)
                    ru = ru2[tokt % 2]
                    X.wait(V, "s_ev", ev_ru[tokt])
                    X.op(V, lambda e, u=ru: e.reduce_max(s1[:], u[:, 0:8],
                                                         axis=AX.X), "v_ms")
                    X.wait(V, "v_ms", X.cnt["v_ms"])
                    X.op(V, lambda e, u=ru: e.tensor_scalar(
                        out=tm1[:], in0=u[:, 0:8], scalar1=s1[:, 0:1],
                        scalar2=NEG_BIG, op0=ALU.is_ge, op1=ALU.mult), "v_ms")
                    X.wait(V, "v_ms", X.cnt["v_ms"])
                    X.op(V, lambda e, u=ru: e.tensor_add(tm2[:], u[:, 0:8], tm1[:]),
                         "v_ms")
                    X.wait(V, "v_ms", X.cnt["v_ms"])
                    X.op(V, lambda e: e.reduce_max(s2[:], tm2[:], axis=AX.X), "v_ms")
                    X.wait(V, "v_ms", X.cnt["v_ms"])
                    X.op(V, lambda e: e.tensor_scalar(
                        out=meanb[:], in0=s1[:], scalar1=-1.0, scalar2=None,
                        op0=ALU.mult))
                    X.op(V, lambda e: e.tensor_scalar(
                        out=varb[:], in0=s2[:], scalar1=s1[:, 0:1], scalar2=None,
                        op0=ALU.subtract))
                    X.op(V, lambda e, u=ru: e.tensor_scalar(
                        out=selw[:], in0=u[:, 0:8], scalar1=s2[:, 0:1],
                        scalar2=2.0, op0=ALU.is_ge, op1=ALU.mult), "v_ms")
                    X.wait(Sc, "v_ms", X.cnt["v_ms"])
                    X.op(Sc, lambda e, u=ru: e.activation(
                        out=p8[:], in_=u[:, 0:8], func=AF.Exp, bias=meanb[:, 0:1]),
                        "s_ev")
                    X.wait(Sc, "s_ev", X.cnt["s_ev"])
                    X.op(Sc, lambda e: e.activation(out=dnm[:], in_=varb[:],
                                                    func=AF.Exp), "s_ev")
                    X.wait(V, "s_ev", X.cnt["s_ev"])
                    X.op(V, lambda e: e.tensor_scalar(
                        out=dnm[:], in0=dnm[:], scalar1=1.0, scalar2=None,
                        op0=ALU.add), "v_ms")
                    X.wait(V, "v_ms", X.cnt["v_ms"])
                    X.op(V, lambda e: e.reciprocal(dnm[:], dnm[:]), "v_ms")
                    X.wait(V, "v_ms", X.cnt["v_ms"])
                    X.op(V, lambda e: e.scalar_tensor_tensor(
                        out=cmb[:], in0=p8[:], scalar=dnm[:, 0:1], in1=selw[:],
                        op0=ALU.mult, op1=ALU.mult), "v_ms")
                    X.wait(V, "v_ms", X.cnt["v_ms"])
                    if tokt >= 1:
                        X.wait(V, "t_mm", mm_tr[tokt - 1])
                    X.op(V, lambda e, u=ru: e.tensor_tensor(
                        out=hw_bf.rearrange("p (e r) -> p e r", r=16),
                        in0=u[:, 8:136].rearrange("p (e r) -> p e r", r=16),
                        in1=cmb[:, :, None].broadcast_to([128, 8, 16]),
                        op=ALU.mult), "v_ms")
                    v_hw[tokt] = X.cnt["v_ms"]
            close_tokt(63)

            if DEBUG >= 2:
                X.wait(W, "s_ev", X.cnt["s_ev"])
                X.wait(W, "t_mm", X.cnt["t_mm"])
                X.dma(W, dbg["wrA"][:, :], wrA[:, :], "st_out")
                X.dma(W, dbg["wbt"][:, :], wbaseT, "st_out")
                X.dma(W, dbg["bflat"][:, :], bflat[:, :], "st_out")
            X.wait(W, "st_out", X.cnt["st_out"])

        @block.sync
        def _(eng):
            ctx.set_engine("sync", eng)
            script(ctx)
            final_counts.append(dict(ctx.cnt))

        @block.gpsimd
        def _(eng):
            ctx.set_engine("gpsimd", eng)
            script(ctx)
            final_counts.append(dict(ctx.cnt))

        @block.vector
        def _(eng):
            ctx.set_engine("vector", eng)
            script(ctx)
            final_counts.append(dict(ctx.cnt))

        @block.scalar
        def _(eng):
            ctx.set_engine("scalar", eng)
            script(ctx)
            final_counts.append(dict(ctx.cnt))

        @block.tensor
        def _(eng):
            ctx.set_engine("tensor", eng)
            script(ctx)
            final_counts.append(dict(ctx.cnt))

    assert all(fc == final_counts[0] for fc in final_counts[1:]), "pass divergence"
    return nc


_NC_CACHE = {}


def _get_nc():
    key = ("v1", DEBUG)
    if key not in _NC_CACHE:
        _NC_CACHE[key] = build_nc()
    return _NC_CACHE[key]


def kernel(L0_attn_hidden_states, L0_attn_w_attn, L0_attn_b_attn, L0_attn_w_proj,
           L0_attn_b_proj, L0_ffn_W_base, L0_ffn_b_base, L0_ffn_W_router,
           L0_ffn_A_stack, L0_ffn_B_stack):
    x = np.ascontiguousarray(np.asarray(L0_attn_hidden_states, dtype=np.float32)
                             .reshape(NT, D))
    w_attn = np.asarray(L0_attn_w_attn, dtype=np.float32)
    b_attn = np.asarray(L0_attn_b_attn, dtype=np.float32)
    w_proj = np.asarray(L0_attn_w_proj, dtype=np.float32)
    b_proj = np.asarray(L0_attn_b_proj, dtype=np.float32)
    W_base = np.asarray(L0_ffn_W_base, dtype=np.float32)
    b_base = np.asarray(L0_ffn_b_base, dtype=np.float32)
    W_rout = np.asarray(L0_ffn_W_router, dtype=np.float32)
    A_stk = np.asarray(L0_ffn_A_stack, dtype=np.float32).reshape(128, D)
    B_stk = np.asarray(L0_ffn_B_stack, dtype=np.float32)

    ident = np.eye(128, dtype=np.float32)
    ones_sq = np.ones((128, 128), dtype=np.float32)
    ones_row = np.ones((1, 128), dtype=np.float32)
    iota = (np.arange(32, dtype=np.float32)[None, :] * 128
            + np.arange(128, dtype=np.float32)[:, None])
    masks = np.zeros((128, 4 * 512), dtype=np.float32)
    for j in range(4):
        dk = np.arange(128)[:, None]
        dq = np.arange(512)[None, :]
        masks[:, j * 512:(j + 1) * 512] = np.where(j * 128 + dk <= dq, 0.0, -300000.0)
    masks = masks.astype(ml_dtypes.bfloat16)
    slopes = _alibi_slopes(H)

    wrA_host = np.empty((128, 32, 136), dtype=np.float32)
    for dt in range(32):
        wrA_host[:, dt, 0:8] = W_rout[:, dt * 128:(dt + 1) * 128].T
        wrA_host[:, dt, 8:136] = A_stk[:, dt * 128:(dt + 1) * 128].T
    wrA_host = np.ascontiguousarray(wrA_host.reshape(128, 32 * 136))

    def _ksq3(core):
        out = np.empty((3, HL, 4480), dtype=ml_dtypes.bfloat16)
        mm_ = np.arange(4480, dtype=np.float64) - 384.0
        for hl in range(HL):
            sl = float(slopes[core * HL + hl])
            Xv = -128.0 * sl * mm_
            a = Xv.astype(ml_dtypes.bfloat16)
            r = Xv - a.astype(np.float64)
            b2 = r.astype(ml_dtypes.bfloat16)
            c3 = (r - b2.astype(np.float64)).astype(ml_dtypes.bfloat16)
            out[0, hl], out[1, hl], out[2, hl] = a, b2, c3
        return np.ascontiguousarray(out.reshape(3, HL * 4480))

    in_maps = []
    for c in range(8):
        qc = slice(c * OC, (c + 1) * OC)
        w_qkv_slice = np.concatenate(
            [w_attn[:, qc], w_attn[:, D + c * OC: D + (c + 1) * OC],
             w_attn[:, 2 * D + c * OC: 2 * D + (c + 1) * OC]], axis=1)
        b_qkv_slice = np.concatenate(
            [b_attn[qc], b_attn[D + c * OC: D + (c + 1) * OC],
             b_attn[2 * D + c * OC: 2 * D + (c + 1) * OC]])
        b_qkv_pp = np.ascontiguousarray(b_qkv_slice.reshape(12, 128).T)
        in_maps.append({
            "x_rows": np.ascontiguousarray(x[c * TOKB:(c + 1) * TOKB, :]),
            "x_cols": np.ascontiguousarray(x[:, qc]),
            "w_qkv_slice": np.ascontiguousarray(w_qkv_slice),
            "b_qkv_pp": b_qkv_pp,
            "w_proj_cols": np.ascontiguousarray(w_proj[:, qc]),
            "b_proj_row": np.ascontiguousarray(b_proj[qc][None, :]),
            "w_baseT": np.ascontiguousarray(W_base[qc, :].T),
            "b_base_row": np.ascontiguousarray(b_base[qc][None, :]),
            "wrA_host": wrA_host,
            "b_flat_my": np.ascontiguousarray(
                B_stk[:, qc, :].transpose(0, 2, 1).reshape(128, OC)),
            "ident": ident,
            "ones_sq": ones_sq,
            "ones_row": ones_row,
            "iota_k": iota,
            "diag_masks": masks,
            "slopes_row": np.ascontiguousarray(slopes[c * HL:(c + 1) * HL][None, :]),
            "ksq3": _ksq3(c),
        })

    nc = _get_nc()
    res = run_bass_kernel_spmd(nc, in_maps, core_ids=list(range(8)))
    out = np.empty((NT, D), dtype=np.float32)
    for c in range(8):
        out[:, c * OC:(c + 1) * OC] = res.results[c]["out_cols"]
    if DEBUG:
        kernel._dbg = [res.results[c] for c in range(8)]
    return out.reshape(BATCH, S, D)


# version 12
# speedup vs baseline: 1.0021x; 1.0021x over previous
import numpy as np
import ml_dtypes
from contextlib import ExitStack

import concourse.bass as bass
import concourse.mybir as mybir
from concourse.bass_utils import run_bass_kernel_spmd

F32 = mybir.dt.float32
BF16 = mybir.dt.bfloat16
AF = mybir.ActivationFunctionType
ALU = mybir.AluOpType
AX = mybir.AxisListType

D = 4096
S = 4096
BATCH = 2
NT = BATCH * S
H = 32
DH = 128
HL = 4
NC = 8
OC = 512
TOKB = NT // NC
SCALE = 1.0 / DH
EPS = 1e-5
NEG_BIG = -30000.0

DEBUG = 0

KB = 1024


def _alibi_slopes(n_heads):
    import math
    cp2 = 2 ** math.floor(math.log2(n_heads))
    base = 2 ** (-(2 ** (-(math.log2(cp2) - 3))))
    slopes = base ** np.arange(1, cp2 + 1, dtype=np.float32)
    if cp2 != n_heads:
        extra_base = 2 ** (-(2 ** (-(math.log2(2 * cp2) - 3))))
        n_rem = min(cp2, n_heads - cp2)
        extra = extra_base ** np.arange(1, 1 + 2 * n_rem, 2, dtype=np.float32)
        slopes = np.concatenate([slopes, extra])
    return slopes.astype(np.float32)


class Ctx:

    def __init__(self, nc, sems):
        self.nc = nc
        self.sems = sems
        self.cnt = {k: 0 for k in sems}
        self.eng = None
        self.E = None

    def set_engine(self, name, eng):
        self.E = name
        self.eng = eng
        self.cnt = {k: 0 for k in self.sems}

    def dma(self, q, out, in_, sem, slow=False):
        if self.E == q:
            kw = {}
            if slow:
                kw["allow_slow_non_contiguous"] = True
            self.eng.dma_start(out=out, in_=in_, **kw).then_inc(self.sems[sem], 16)
        self.cnt[sem] += 16

    def dmat(self, q, out, in_, sem):
        if self.E == q:
            self.eng.dma_start_transpose(out=out, in_=in_) \
                .then_inc(self.sems[sem], 16)
        self.cnt[sem] += 16

    def cc(self, kind, op, ins, outs, sem):
        if self.E == "gpsimd":
            self.eng.collective_compute(
                kind, op, replica_groups=[list(range(NC))], ins=ins, outs=outs
            ).then_inc(self.sems[sem])
        self.cnt[sem] += 1

    def mm(self, out, lhsT, rhs, start, stop, sem=None):
        if self.E == "tensor":
            i = self.nc.tensor.matmul(out, lhsT, rhs, start=start, stop=stop)
            if sem is not None:
                i.then_inc(self.sems[sem], 1)
        if sem is not None:
            self.cnt[sem] += 1

    def op(self, eng_name, fn, sem=None):
        if self.E == eng_name:
            i = fn(self.eng)
            if sem is not None and i is not None:
                i.then_inc(self.sems[sem], 1)
        if sem is not None:
            self.cnt[sem] += 1

    def wait(self, who, sem, val):
        assert 0 <= val <= self.cnt[sem], (who, sem, val, self.cnt[sem])
        if self.E == who:
            self.eng.wait_ge(self.sems[sem], val)


def build_nc():
    nc = bass.Bass("TRN2", target_bir_lowering=False)

    t_x_rows = nc.dram_tensor("x_rows", [TOKB, D], F32, kind="ExternalInput")
    t_x_cols = nc.dram_tensor("x_cols", [NT, OC], F32, kind="ExternalInput")
    t_wqkv = nc.dram_tensor("w_qkv_slice", [D, 3 * OC], F32, kind="ExternalInput")
    t_bqkv = nc.dram_tensor("b_qkv_pp", [128, 12], F32, kind="ExternalInput")
    t_wproj = nc.dram_tensor("w_proj_cols", [D, OC], F32, kind="ExternalInput")
    t_bproj = nc.dram_tensor("b_proj_row", [1, OC], F32, kind="ExternalInput")
    t_wbT = nc.dram_tensor("w_baseT", [D, OC], F32, kind="ExternalInput")
    t_bbase = nc.dram_tensor("b_base_row", [1, OC], F32, kind="ExternalInput")
    t_wrA = nc.dram_tensor("wrA_host", [128, 32 * 136], F32, kind="ExternalInput")
    t_bfl = nc.dram_tensor("b_flat_my", [128, OC], F32, kind="ExternalInput")
    t_ident = nc.dram_tensor("ident", [128, 128], F32, kind="ExternalInput")
    t_ones = nc.dram_tensor("ones_sq", [128, 128], F32, kind="ExternalInput")
    t_onesr = nc.dram_tensor("ones_row", [1, 128], F32, kind="ExternalInput")
    t_iota = nc.dram_tensor("iota_k", [128, 32], F32, kind="ExternalInput")
    t_masks = nc.dram_tensor("diag_masks", [128, 4 * 512], BF16,
                             kind="ExternalInput")
    t_slope = nc.dram_tensor("slopes_row", [1, HL], F32, kind="ExternalInput")
    t_ksq = nc.dram_tensor("ksq3", [3, HL * 4480], BF16, kind="ExternalInput")

    t_out = nc.dram_tensor("out_cols", [NT, OC], F32, kind="ExternalOutput")

    d_ln1t_in = nc.dram_tensor("ln1t_in", [D, TOKB], BF16)
    d_ln1t_all = nc.dram_tensor("ln1t_all", [NC * D, TOKB], BF16, addr_space="Shared")
    d_qkvt = nc.dram_tensor("qkvt", [3 * OC, NT], BF16)
    d_attnt_in = nc.dram_tensor("attnt_in", [OC, NT], BF16)
    d_attnt_all = nc.dram_tensor("attnt_all", [NC * OC, NT], BF16, addr_space="Shared")
    d_ln2t_inA = nc.dram_tensor("ln2t_inA", [OC, S], BF16)
    d_ln2t_inB = nc.dram_tensor("ln2t_inB", [OC, S], BF16)
    d_ln2t_allA = nc.dram_tensor("ln2t_allA", [NC * OC, S], BF16,
                                 addr_space="Shared")
    d_ln2t_allB = nc.dram_tensor("ln2t_allB", [NC * OC, S], BF16,
                                 addr_space="Shared")
    d_stA_in = nc.dram_tensor("ln2statsA_in", [128, 64], F32)
    d_stA_out = nc.dram_tensor("ln2statsA_out", [128, 64], F32,
                               addr_space="Shared")
    d_stB_in = nc.dram_tensor("ln2statsB_in", [128, 64], F32)
    d_stB_out = nc.dram_tensor("ln2statsB_out", [128, 64], F32,
                               addr_space="Shared")
    d_trunk = nc.dram_tensor("trunk", [NT, OC], F32)

    dbg = {}
    if DEBUG:
        dbg["ln1t"] = nc.dram_tensor("dbg_ln1t", [D, TOKB], BF16, kind="ExternalOutput")
        dbg["qkvt"] = nc.dram_tensor("dbg_qkvt", [3 * OC, NT], BF16, kind="ExternalOutput")
        dbg["attnt"] = nc.dram_tensor("dbg_attnt", [OC, NT], BF16, kind="ExternalOutput")
        dbg["trunk"] = nc.dram_tensor("dbg_trunk", [NT, OC], F32, kind="ExternalOutput")
        dbg["ln2tA"] = nc.dram_tensor("dbg_ln2tA", [OC, S], BF16, kind="ExternalOutput")
        dbg["ln2tB"] = nc.dram_tensor("dbg_ln2tB", [OC, S], BF16, kind="ExternalOutput")
        dbg["stats"] = nc.dram_tensor("dbg_stats", [128, 128], F32, kind="ExternalOutput")
    if DEBUG >= 2:
        dbg["probe"] = nc.dram_tensor("dbg_probe", [128, 2048], F32, kind="ExternalOutput")
        dbg["wrA"] = nc.dram_tensor("dbg_wrA", [128, 32 * 136], BF16, kind="ExternalOutput")
        dbg["wbt"] = nc.dram_tensor("dbg_wbt", [128, 16384], BF16, kind="ExternalOutput")
        dbg["bflat"] = nc.dram_tensor("dbg_bflat", [128, 512], BF16, kind="ExternalOutput")

    es = ExitStack()
    final_counts = []
    with es:
        E = es.enter_context

        ident_bf = E(nc.sbuf_tensor("ident_bf", [128, 128], BF16))
        ones_bf = E(nc.sbuf_tensor("ones_bf", [128, 128], BF16))
        onesr_bf = E(nc.sbuf_tensor("onesr_bf", [1, 128], BF16))
        onesr_f = E(nc.sbuf_tensor("onesr_f", [1, 128], F32))
        iota_k = E(nc.sbuf_tensor("iota_k_sb", [128, 32], F32))
        masks_bf = E(nc.sbuf_tensor("masks_bf", [128, 4 * 512], BF16))
        slope_row = E(nc.sbuf_tensor("slope_row", [1, HL], F32))
        slope_pp = E(nc.sbuf_tensor("slope_pp", [128, HL], F32))
        bias_row = E(nc.sbuf_tensor("bias_row", [128, 1], F32))
        bqkv_pp = E(nc.sbuf_tensor("bqkv_pp", [128, 12], F32))
        bproj_bf = E(nc.sbuf_tensor("bproj_bf", [1, OC], BF16))
        bbase_bf = E(nc.sbuf_tensor("bbase_bf", [1, OC], BF16))
        wrA = E(nc.sbuf_tensor("wrA", [128, 32 * 136], BF16))
        bflat = E(nc.sbuf_tensor("bflat", [128, 4 * 128], BF16))
        stats1 = E(nc.sbuf_tensor("stats1", [128, 128], F32))
        stats2 = E(nc.sbuf_tensor("stats2", [128, 128], F32))
        s1 = E(nc.sbuf_tensor("s1", [128, 1], F32))
        s2 = E(nc.sbuf_tensor("s2", [128, 1], F32))
        meanb = E(nc.sbuf_tensor("meanb", [128, 1], F32))
        varb = E(nc.sbuf_tensor("varb", [128, 1], F32))
        rstdb = E(nc.sbuf_tensor("rstdb", [128, 1], F32))
        tm1 = E(nc.sbuf_tensor("tm1", [128, 8], F32))
        tm2 = E(nc.sbuf_tensor("tm2", [128, 8], F32))
        qmax2 = E(nc.sbuf_tensor("qmax2", [128, 1], F32))
        kmax2 = E(nc.sbuf_tensor("kmax2", [128, 1], F32))
        qkmax = E(nc.sbuf_tensor("qkmax", [128, 1], F32))
        p8 = E(nc.sbuf_tensor("p8", [128, 8], F32))
        selw = E(nc.sbuf_tensor("selw", [128, 8], F32))
        cmb = E(nc.sbuf_tensor("cmb", [128, 8], F32))
        hw_bf = E(nc.sbuf_tensor("hw_bf", [128, 128], BF16))
        hT_bf = E(nc.sbuf_tensor("hT_bf", [128, 128], BF16))
        hT_bf2 = E(nc.sbuf_tensor("hT_bf2", [128, 128], BF16))
        dnm = E(nc.sbuf_tensor("dnm", [128, 1], F32))

        AR_BYTES = 185 * KB
        arena = E(nc.sbuf_tensor("arena", [128, AR_BYTES // 2], BF16))

        def abf(off_kb, nbytes):
            o = int(off_kb * KB)
            assert o + nbytes <= AR_BYTES
            return arena[:, o // 2:(o + nbytes) // 2]

        def af32(off_kb, nbytes):
            return abf(off_kb, nbytes).bitcast(F32)

        wq_sb = abf(0, 96 * KB)
        ksq3v = abf(0, 35 * KB)[0:3, :]
        wbaseT = abf(0, 32 * KB)
        wproj_sb = abf(64, 32 * KB)

        xbuf = [af32(96, 16 * KB), af32(112, 16 * KB)]
        sqbuf = af32(128, 16 * KB)
        ln1_tok = abf(144, 8 * KB)
        ln1t_st = abf(152, 8 * KB)
        lnh = [abf(96, 16 * KB), abf(112, 16 * KB),
               abf(128, 16 * KB), abf(144, 16 * KB)]
        ktb = [abf(96, 8 * KB), abf(120, 8 * KB)]
        vtb = [abf(104, 8 * KB), abf(128, 8 * KB)]
        qtb = [abf(112, 8 * KB), abf(136, 8 * KB)]
        vsb2b = [abf(144, 8 * KB), abf(164, 8 * KB)]
        sq_bf = abf(152, 8 * KB)
        attbuf = [abf(96, 32 * KB), abf(128, 32 * KB)]
        l2buf = [abf(96, 32 * KB), abf(128, 32 * KB)]
        expb2 = [abf(160, 2 * KB), abf(162, 2 * KB)]
        rec = af32(178, 2 * KB)
        att_st = [abf(180, 1 * KB), abf(181, 1 * KB)]
        qkv_st = [abf(183, 1 * KB), abf(184, 1 * KB)]
        xcbuf = [af32(160, 2 * KB), af32(162, 2 * KB)]
        trk_st = [af32(164, 2 * KB), af32(166, 2 * KB)]
        sqc = af32(168, 2 * KB)
        trkb2 = [af32(170, 2 * KB), af32(172, 2 * KB)]
        ln2c2 = [abf(174, 1 * KB), abf(160, 1 * KB)]
        ln2t_st2 = [abf(175, 1 * KB), abf(161, 1 * KB)]
        trkb3 = [af32(176, 2 * KB), af32(178, 2 * KB)]
        out_st = [af32(180, 2 * KB), af32(182, 2 * KB)]
        ru2 = [af32(32, 1 * KB), af32(33, 1 * KB)]

        PA = E(nc.psum_tensor("PA", [128, 1024], F32))
        PB = E(nc.psum_tensor("PB", [128, 1024], F32))
        PS_ = E(nc.psum_tensor("PS", [128, 512], F32))
        PV = E(nc.psum_tensor("PV", [128, 512], F32))
        PT = E(nc.psum_tensor("PT", [128, 512], F32))
        PU = E(nc.psum_tensor("PU", [128, 512], F32))
        SC4 = [PA[:, 0:512], PA[:, 512:1024], PB[:, 0:512], PB[:, 512:1024]]

        sem_names = [
            "ld_sy", "ld_g", "st_ln1t", "st_qkv", "st_att", "st_trk", "st_stats",
            "st_ln2t", "st_out", "ld_ln", "ld_pair", "ld_att", "ld_xc", "ld_trk2",
            "ld_l2", "ld_trk3", "t_mm", "s_ev", "v_ms",
            "cc_ln1", "cc_att", "cc_st", "cc_stB", "cc_ln2", "cc_ln2B",
        ]
        sems = {n: E(nc.semaphore(n)) for n in sem_names}
        block = E(nc.Block())
        ctx = Ctx(nc, sems)

        def script(X: Ctx):
            W, G, V, Sc, T = "sync", "gpsimd", "vector", "scalar", "tensor"
            STQ = G

            X.dma(W, iota_k[:], t_iota[:, :], "ld_sy")
            X.dma(W, slope_row[:], t_slope[:, :], "ld_sy")
            X.dma(W, bqkv_pp[:], t_bqkv[:, :], "ld_sy")
            X.dma(W, onesr_f[:], t_onesr[:, :], "ld_sy")
            n_sy1 = X.cnt["ld_sy"]
            X.dma(G, ident_bf[:], t_ident[:, :], "ld_g")
            X.dma(G, ones_bf[:], t_ones[:, :], "ld_g")
            X.dma(G, onesr_bf[:], t_onesr[:, :], "ld_g")
            X.dma(G, masks_bf[:], t_masks[:, :], "ld_g")
            X.dma(G, bproj_bf[:], t_bproj[:, :], "ld_g")
            X.dma(G, bbase_bf[:], t_bbase[:, :], "ld_g")
            X.dma(G, wrA[:, :], t_wrA[:, :], "ld_g")
            X.dma(G, bflat[:, :], t_bfl[:, :], "ld_g")
            n_g0 = X.cnt["ld_g"]
            src_wq = t_wqkv.rearrange("(dt p) j -> p dt j", p=128)
            dst_wq = wq_sb.rearrange("p (dt j) -> p dt j", j=1536)
            for i in range(4):
                X.dma(G, dst_wq[:, i * 8:(i + 1) * 8, :],
                      src_wq[:, i * 8:(i + 1) * 8, :], "ld_g")
            n_g_wq = X.cnt["ld_g"]

            X.wait(T, "ld_sy", n_sy1)
            X.mm(PU[:, 0:HL], onesr_f[0:1, :], slope_row[0:1, :], True, True, "t_mm")
            X.wait(V, "t_mm", X.cnt["t_mm"])
            X.op(V, lambda e: e.tensor_copy(out=slope_pp[:], in_=PU[:, 0:HL]), "v_ms")
            v_slope = X.cnt["v_ms"]

            ev_ln0 = X.cnt["s_ev"]
            for i in range(8):
                if i >= 2:
                    X.wait(W, "v_ms", v_slope + 4 * i - 4)
                X.dma(W, xbuf[i % 2], t_x_rows[i * 128:(i + 1) * 128, :], "ld_sy")
                ld_here = X.cnt["ld_sy"]
                X.wait(V, "ld_sy", ld_here)
                X.op(V, lambda e, b=xbuf[i % 2]: e.reduce_sum(s1[:], b, axis=AX.X))
                X.wait(Sc, "ld_sy", ld_here)
                if i > 0:
                    X.wait(Sc, "v_ms", v_slope + 4 * i - 3)
                X.op(Sc, lambda e, b=xbuf[i % 2]: e.square(sqbuf, b), "s_ev")
                sq_done = X.cnt["s_ev"]
                X.wait(V, "s_ev", sq_done)
                X.op(V, lambda e: e.reduce_sum(s2[:], sqbuf, axis=AX.X))
                X.op(V, lambda e: e.tensor_scalar(out=meanb[:], in0=s1[:],
                                                  scalar1=1.0 / D, scalar2=None,
                                                  op0=ALU.mult), "v_ms")
                X.wait(V, "v_ms", X.cnt["v_ms"])
                X.op(V, lambda e: e.tensor_scalar(out=tm1[:, 0:1], in0=meanb[:],
                                                  scalar1=meanb[:, 0:1], scalar2=EPS,
                                                  op0=ALU.mult, op1=ALU.subtract),
                     "v_ms")
                X.wait(V, "v_ms", X.cnt["v_ms"])
                X.op(V, lambda e: e.scalar_tensor_tensor(
                    out=varb[:], in0=s2[:], scalar=1.0 / D, in1=tm1[:, 0:1],
                    op0=ALU.mult, op1=ALU.subtract), "v_ms")
                var_done = X.cnt["v_ms"]
                X.wait(Sc, "v_ms", var_done)
                X.op(Sc, lambda e: e.activation(out=varb[:], in_=varb[:], func=AF.Ln),
                     "s_ev")
                X.wait(Sc, "s_ev", X.cnt["s_ev"])
                X.op(Sc, lambda e: e.activation(out=rstdb[:], in_=varb[:],
                                                func=AF.Exp, scale=-0.5), "s_ev")
                rstd_done = X.cnt["s_ev"]
                X.wait(V, "s_ev", rstd_done)
                X.op(V, lambda e, b=xbuf[i % 2]: e.tensor_scalar(
                    out=ln1_tok, in0=b, scalar1=meanb[:, 0:1], scalar2=rstdb[:, 0:1],
                    op0=ALU.subtract, op1=ALU.mult), "v_ms")
                nrm_done = X.cnt["v_ms"]
                assert nrm_done == v_slope + 4 * i + 4
                X.wait(T, "ld_g", n_g0)
                X.wait(T, "v_ms", nrm_done)
                for grp in range(8):
                    if i > 0 or grp > 0:
                        X.wait(T, "s_ev", rstd_done + grp)
                    for j in range(4):
                        dt = grp * 4 + j
                        X.mm(PT[:, j * 128:(j + 1) * 128],
                             ln1_tok[:, dt * 128:(dt + 1) * 128], ident_bf[:],
                             True, True, "t_mm" if j == 3 else None)
                    X.wait(Sc, "t_mm", X.cnt["t_mm"])
                    if grp == 0 and i > 0:
                        X.wait(Sc, "st_ln1t", i * 16)
                    X.op(Sc, lambda e, g=grp: e.activation(
                        out=ln1t_st[:, g * 512:(g + 1) * 512], in_=PT[:, 0:512],
                        func=AF.Copy), "s_ev")
                X.wait(STQ, "s_ev", X.cnt["s_ev"])
                X.dma(STQ, d_ln1t_in[:, i * 128:(i + 1) * 128]
                      .rearrange("(dt p) t -> p dt t", p=128),
                      ln1t_st.rearrange("p (dt t) -> p dt t", t=128), "st_ln1t")
            X.wait(G, "st_ln1t", 8 * 16)
            X.cc("AllGather", ALU.bypass, [d_ln1t_in.ap().opt()],
                 [d_ln1t_all.ap().opt()], "cc_ln1")
            if DEBUG:
                X.wait(W, "st_ln1t", 8 * 16)
                X.dma(W, dbg["ln1t"][:, :], d_ln1t_in[:, :], "st_out")

            X.wait(W, "cc_ln1", 1)
            ev_1a = X.cnt["s_ev"]
            mm_1a = X.cnt["t_mm"]

            def lnslot(tokg, h):
                return (2 * tokg + h) % 4

            n_ld_ln = 0
            colg_idx = 0
            for tokg in range(16):
                r = tokg // 2
                c0 = (tokg % 2) * 512
                for h in range(2):
                    if tokg >= 2:
                        X.wait(W, "t_mm", mm_1a + (tokg - 1) * 12)
                    src = d_ln1t_all[r * D:(r + 1) * D, c0:c0 + 512] \
                        .rearrange("(dt p) t -> p dt t", p=128)
                    X.dma(W, lnh[lnslot(tokg, h)]
                          .rearrange("p (dt t) -> p dt t", t=512),
                          src[:, h * 16:(h + 1) * 16, :], "ld_ln")
                    n_ld_ln += 16
                for colg in range(12):
                    ps = SC4[colg_idx % 4]
                    X.wait(T, "ld_ln", n_ld_ln)
                    X.wait(T, "ld_g", n_g_wq)
                    if colg_idx >= 4:
                        X.wait(T, "s_ev", ev_1a + colg_idx - 3)
                    for dt in range(32):
                        hh = dt // 16
                        lhs = wq_sb[:, dt * 1536 + colg * 128:
                                    dt * 1536 + (colg + 1) * 128]
                        rhs = lnh[lnslot(tokg, hh)][:, (dt % 16) * 512:
                                                    (dt % 16 + 1) * 512]
                        X.mm(ps, lhs, rhs, dt == 0, dt == 31,
                             "t_mm" if dt == 31 else None)
                    X.wait(Sc, "t_mm", X.cnt["t_mm"])
                    if colg_idx >= 2:
                        X.wait(Sc, "st_qkv", (colg_idx - 1) * 16)
                    X.op(Sc, lambda e, ps=ps, cg=colg, st=qkv_st[colg_idx % 2]:
                         e.activation(out=st, in_=ps, func=AF.Identity,
                                      bias=bqkv_pp[:, cg:cg + 1]), "s_ev")
                    X.wait(STQ, "s_ev", X.cnt["s_ev"])
                    g0 = tokg * 512
                    X.dma(STQ, d_qkvt[colg * 128:(colg + 1) * 128, g0:g0 + 512],
                          qkv_st[colg_idx % 2][:, :], "st_qkv")
                    colg_idx += 1
            assert colg_idx == 192
            if DEBUG:
                X.wait(W, "st_qkv", 192 * 16)
                X.dma(W, dbg["qkvt"][:, :], d_qkvt[:, :], "st_out")

            X.wait(G, "t_mm", mm_1a + 192)
            src_wp = t_wproj.rearrange("(g p) o -> p g o", p=128)
            dst_wp = wproj_sb.rearrange("p (g o) -> p g o", o=512)
            for i in range(2):
                X.dma(G, dst_wp[:, i * 16:(i + 1) * 16, :],
                      src_wp[:, i * 16:(i + 1) * 16, :], "ld_g")
            n_g_wproj = X.cnt["ld_g"]

            st_att_n = 0
            n_ld_pair = 0
            mm_1a_end = X.cnt["t_mm"]
            pair_mm_end = {}
            ksq_ld = {}
            exp_hist = []
            att_v_last = 0
            qt_ctr = 0
            gidx = 0

            def pair_load(p_idx):
                nonlocal n_ld_pair
                hl, b = p_idx // 2, p_idx % 2
                par = p_idx % 2
                X.wait(W, "st_qkv", 192 * 16)
                if p_idx >= 2:
                    X.wait(W, "t_mm", pair_mm_end[p_idx - 2])
                for (buf, row0) in ((ktb[par], 512 + hl * 128),
                                    (qtb[par], hl * 128)):
                    X.dma(W, buf, d_qkvt[row0:row0 + 128, b * S:(b + 1) * S],
                          "ld_pair")
                    n_ld_pair += 16
                vr = 1024 + hl * 128
                for kt in range(32):
                    X.dmat(W, vsb2b[par][:, kt * 128:(kt + 1) * 128],
                           d_qkvt[vr:vr + 128,
                                  b * S + kt * 128:b * S + (kt + 1) * 128],
                           "ld_pair")
                    n_ld_pair += 16

            X.wait(W, "t_mm", mm_1a_end)
            X.dma(W, ksq3v, t_ksq[:, :], "ld_sy")
            ksq_ld_n = X.cnt["ld_sy"]

            pair_load(0)
            pair_load(1)
            for p_idx in range(8):
                hl, b = p_idx // 2, p_idx % 2
                par = p_idx % 2
                vsb = vsb2b[par]

                X.wait(T, "ld_pair", (p_idx + 1) * 34 * 16)
                if p_idx == 0:
                    X.wait(T, "ld_sy", ksq_ld_n)
                X.wait(V, "s_ev", X.cnt["s_ev"])
                X.op(V, lambda e, h=hl: e.tensor_scalar(
                    out=bias_row[:], in0=iota_k[:, 0:1],
                    scalar1=slope_pp[:, h:h + 1], scalar2=None,
                    op0=ALU.mult), "v_ms")
                bias_done = X.cnt["v_ms"]

                groups = [(qt, g) for qt in range(8) for g in range(2 * qt + 2)]
                sc_end = {}
                ex_end = {}
                rs_end = {}

                def emit_rsav(k):
                    nonlocal st_att_n, att_v_last, qt_ctr
                    qtk, gk = groups[k]
                    slot = expb2[(gbase + k) % 2]
                    first = gk == 0
                    last = gk == 2 * qtk + 1
                    X.wait(T, "s_ev", ex_end[k])
                    if first:
                        X.wait(T, "v_ms", att_v_last)
                    for j in range(2):
                        X.mm(PS_[:, :], ones_bf[:], slot[:, j * 512:(j + 1) * 512],
                             first and j == 0, last and j == 1, None)
                    for j in range(2):
                        kt = 2 * gk + j
                        X.mm(PV[:, :], vsb[:, kt * 128:(kt + 1) * 128],
                             slot[:, j * 512:(j + 1) * 512],
                             first and j == 0, last and j == 1,
                             "t_mm" if j == 1 else None)
                    rs_end[k] = X.cnt["t_mm"]
                    if last:
                        X.wait(V, "t_mm", X.cnt["t_mm"])
                        X.op(V, lambda e: e.reciprocal(rec, PS_[:, :]), "v_ms")
                        X.wait(V, "v_ms", X.cnt["v_ms"])
                        sidx = qt_ctr % 2
                        if qt_ctr >= 2:
                            X.wait(V, "st_att", st_att_n - 16)
                        X.op(V, lambda e, st=att_st[sidx]: e.tensor_tensor(
                            out=st, in0=PV[:, :], in1=rec, op=ALU.mult), "v_ms")
                        att_v_last = X.cnt["v_ms"]
                        X.wait(W, "v_ms", att_v_last)
                        X.dma(W, d_attnt_in[hl * 128:(hl + 1) * 128,
                                            b * S + qtk * 512: b * S + (qtk + 1) * 512],
                              att_st[sidx][:, :], "st_att")
                        st_att_n += 16
                        qt_ctr += 1

                gbase = gidx
                for i, (qt, g) in enumerate(groups):
                    ps = PA if (gbase + i) % 2 == 0 else PB
                    slot = expb2[(gbase + i) % 2]
                    diag = g >= 2 * qt
                    if gbase + i >= 2:
                        X.wait(T, "s_ev", exp_hist[gbase + i - 2])
                    for j in range(2):
                        kt = 2 * g + j
                        o = 4 * qt - kt
                        k0 = hl * 4480 + (o + 3) * 128
                        X.mm(ps[:, j * 512:(j + 1) * 512], ones_bf[0:3, :],
                             ksq3v[:, k0:k0 + 512], True, False)
                        X.mm(ps[:, j * 512:(j + 1) * 512],
                             ktb[par][:, kt * 128:(kt + 1) * 128],
                             qtb[par][:, qt * 512:(qt + 1) * 512], False, True,
                             "t_mm" if j == 1 else None)
                    sc_end[i] = X.cnt["t_mm"]
                    if i >= 2:
                        emit_rsav(i - 2)
                    if diag:
                        X.wait(V, "t_mm", sc_end[i])
                        moff = 0 if g == 2 * qt else 1024
                        X.op(V, lambda e, ps=ps, mo=moff: e.tensor_tensor(
                            out=ps[:, 0:1024], in0=ps[:, 0:1024],
                            in1=masks_bf[:, mo:mo + 1024], op=ALU.add), "v_ms")
                        X.wait(Sc, "v_ms", X.cnt["v_ms"])
                    X.wait(Sc, "t_mm", rs_end[i - 2] if i >= 2 else sc_end[i])
                    if i == 0:
                        X.wait(Sc, "v_ms", bias_done)
                    X.op(Sc, lambda e, ps=ps, st=slot: e.activation(
                        out=st, in_=ps[:, 0:1024], func=AF.Exp,
                        bias=bias_row[:, 0:1], scale=SCALE), "s_ev")
                    ex_end[i] = X.cnt["s_ev"]
                    exp_hist.append(ex_end[i])
                emit_rsav(len(groups) - 2)
                emit_rsav(len(groups) - 1)
                gidx = gbase + len(groups)
                pair_mm_end[p_idx] = X.cnt["t_mm"]
                if p_idx + 2 < 8:
                    pair_load(p_idx + 2)

            X.wait(G, "st_att", st_att_n)
            X.cc("AllGather", ALU.bypass, [d_attnt_in.ap().opt()],
                 [d_attnt_all.ap().opt()], "cc_att")
            if DEBUG:
                X.wait(W, "st_att", st_att_n)
                X.dma(W, dbg["attnt"][:, :], d_attnt_in[:, :], "st_out")

            X.wait(G, "cc_att", 1)
            X.wait(G, "t_mm", pair_mm_end[7])
            src_wb = t_wbT.rearrange("(dt p) o -> p dt o", p=128)
            dst_wb = wbaseT.rearrange("p (dt o) -> p dt o", o=512)
            for i in range(2):
                X.dma(G, dst_wb[:, i * 16:(i + 1) * 16, :],
                      src_wb[:, i * 16:(i + 1) * 16, :], "ld_g")
            n_g_wbt = X.cnt["ld_g"]

            X.wait(W, "cc_att", 1)
            mm_2a = X.cnt["t_mm"]
            ev_2a = X.cnt["s_ev"]
            vm_2a = X.cnt["v_ms"]
            n_ld_att = 0
            n_ld_xc = 0
            n_st_trk = 0
            for tg2 in range(16):
                parg = tg2 % 2
                if tg2 >= 2:
                    X.wait(W, "t_mm", mm_2a + (tg2 - 1) * 4)
                src = d_attnt_all[:, tg2 * 512:(tg2 + 1) * 512] \
                    .rearrange("(g p) t -> p g t", p=128)
                dst = attbuf[parg].rearrange("p (g t) -> p g t", t=512)
                for q in range(4):
                    X.dma(W, dst[:, q * 8:(q + 1) * 8, :],
                          src[:, q * 8:(q + 1) * 8, :], "ld_att")
                    n_ld_att += 16
                for tt in range(4):
                    tokt = tg2 * 4 + tt
                    t0 = tokt * 128
                    spar = tokt % 2
                    if tokt >= 2:
                        X.wait(W, "v_ms", vm_2a + (tokt - 2) * 2 + 1)
                    X.dma(W, xcbuf[spar], t_x_cols[t0:t0 + 128, :], "ld_xc")
                    n_ld_xc += 16
                    ps = SC4[tokt % 4]
                    X.wait(T, "ld_att", n_ld_att)
                    X.wait(T, "ld_g", n_g_wproj)
                    if tokt >= 4:
                        X.wait(T, "v_ms", vm_2a + (tokt - 4) * 2 + 1)
                    X.mm(ps, onesr_bf[0:1, :], bproj_bf[0:1, :], True, False)
                    for g in range(32):
                        X.mm(ps, attbuf[parg][:, g * 512 + tt * 128:
                                              g * 512 + tt * 128 + 128],
                             wproj_sb[:, g * 512:(g + 1) * 512],
                             False, g == 31, "t_mm" if g == 31 else None)
                    mm_here = X.cnt["t_mm"]
                    X.wait(V, "t_mm", mm_here)
                    X.wait(V, "ld_xc", n_ld_xc)
                    if tokt >= 2:
                        X.wait(V, "st_trk", n_st_trk - 16)
                        X.wait(V, "s_ev", ev_2a + tokt - 1)
                    X.op(V, lambda e, ps=ps, sp=spar: e.tensor_tensor(
                        out=trk_st[sp], in0=ps, in1=xcbuf[sp], op=ALU.add), "v_ms")
                    X.wait(V, "v_ms", X.cnt["v_ms"])
                    X.op(V, lambda e, sp=spar, tk=tokt: e.reduce_sum(
                        stats1[:, tk:tk + 1], trk_st[sp], axis=AX.X))
                    X.wait(Sc, "v_ms", X.cnt["v_ms"])
                    X.op(Sc, lambda e, sp=spar: e.square(sqc, trk_st[sp]), "s_ev")
                    X.wait(V, "s_ev", X.cnt["s_ev"])
                    X.op(V, lambda e, tk=tokt: e.reduce_sum(
                        stats1[:, 64 + tk:64 + tk + 1], sqc, axis=AX.X), "v_ms")
                    X.wait(STQ, "v_ms", X.cnt["v_ms"] - 1)
                    X.dma(STQ, d_trunk[t0:t0 + 128, :], trk_st[spar][:, :], "st_trk")
                    n_st_trk += 16
                if tg2 == 7:
                    X.wait(STQ, "v_ms", X.cnt["v_ms"])
                    X.dma(STQ, d_stA_in[:, 0:32], stats1[:, 0:32], "st_stats")
                    X.dma(STQ, d_stA_in[:, 32:64], stats1[:, 64:96], "st_stats")
                    X.wait(G, "st_stats", 32)
                    X.cc("AllReduce", ALU.add, [d_stA_in.ap().opt()],
                         [d_stA_out.ap().opt()], "cc_st")
            ev_2a_end = X.cnt["s_ev"]
            X.wait(STQ, "v_ms", X.cnt["v_ms"])
            X.dma(STQ, d_stB_in[:, 0:32], stats1[:, 32:64], "st_stats")
            X.dma(STQ, d_stB_in[:, 32:64], stats1[:, 96:128], "st_stats")
            X.wait(G, "st_stats", 64)
            X.cc("AllReduce", ALU.add, [d_stB_in.ap().opt()],
                 [d_stB_out.ap().opt()], "cc_stB")
            if DEBUG:
                X.wait(W, "st_trk", n_st_trk)
                X.dma(W, dbg["trunk"][:, :], d_trunk[:, :], "st_out")

            half_rstd_ev = {}

            def stat_math(h):
                X.wait(W, "cc_st" if h == 0 else "cc_stB", 1)
                src = d_stA_out if h == 0 else d_stB_out
                X.dma(W, stats2[:, h * 64:(h + 1) * 64], src[:, :], "ld_sy")
                n_syh = X.cnt["ld_sy"]
                mean_h = stats1[:, h * 32:(h + 1) * 32]
                rstd_h = stats1[:, 64 + h * 32:64 + (h + 1) * 32]
                sqm_h = sqc[:, h * 96:h * 96 + 32]
                var_h = sqc[:, h * 96 + 32:h * 96 + 64]
                lnv_h = sqc[:, h * 96 + 64:h * 96 + 96]
                X.wait(V, "ld_sy", n_syh)
                if h == 0:
                    X.wait(V, "s_ev", ev_2a_end)
                X.op(V, lambda e: e.tensor_scalar(
                    out=mean_h, in0=stats2[:, h * 64:h * 64 + 32],
                    scalar1=1.0 / D, scalar2=None, op0=ALU.mult), "v_ms")
                X.wait(V, "v_ms", X.cnt["v_ms"])
                X.op(V, lambda e: e.scalar_tensor_tensor(
                    out=sqm_h, in0=mean_h, scalar=1.0, in1=mean_h,
                    op0=ALU.mult, op1=ALU.mult), "v_ms")
                X.wait(V, "v_ms", X.cnt["v_ms"])
                X.op(V, lambda e: e.scalar_tensor_tensor(
                    out=var_h, in0=stats2[:, h * 64 + 32:h * 64 + 64],
                    scalar=1.0 / D, in1=sqm_h,
                    op0=ALU.mult, op1=ALU.subtract), "v_ms")
                X.wait(V, "v_ms", X.cnt["v_ms"])
                X.op(V, lambda e: e.tensor_scalar(
                    out=var_h, in0=var_h, scalar1=EPS, scalar2=None,
                    op0=ALU.add), "v_ms")
                X.wait(Sc, "v_ms", X.cnt["v_ms"])
                X.op(Sc, lambda e: e.activation(out=lnv_h, in_=var_h,
                                                func=AF.Ln), "s_ev")
                X.wait(Sc, "s_ev", X.cnt["s_ev"])
                X.op(Sc, lambda e: e.activation(out=rstd_h, in_=lnv_h,
                                                func=AF.Exp, scale=-0.5), "s_ev")
                half_rstd_ev[h] = X.cnt["s_ev"]

            stat_math(0)
            n_ld_trk2 = 0
            norm_v = {}
            tr_mm = {}
            evac_ev = {}

            def st_ln2(k):
                X.wait(W, "s_ev", evac_ev[k])
                tgt = d_ln2t_inA if k < 32 else d_ln2t_inB
                t0h = (k % 32) * 128
                X.dma(W, tgt[:, t0h:t0h + 128]
                      .rearrange("(j p) t -> p j t", p=128),
                      ln2t_st2[k % 2].rearrange("p (j t) -> p j t", t=128),
                      "st_ln2t")

            for tokt in range(64):
                if tokt == 32:
                    stat_math(1)
                h, loc, q = tokt // 32, tokt % 32, tokt % 2
                t0 = tokt * 128
                if tokt >= 2:
                    X.wait(W, "v_ms", norm_v[tokt - 2])
                X.wait(W, "st_trk", (tokt + 1) * 16)
                X.dma(W, trkb2[q], d_trunk[t0:t0 + 128, :], "ld_trk2")
                n_ld_trk2 += 16
                if tokt >= 1:
                    st_ln2(tokt - 1)
                X.wait(V, "ld_trk2", n_ld_trk2)
                X.wait(V, "s_ev", half_rstd_ev[h])
                if tokt >= 2:
                    X.wait(V, "t_mm", tr_mm[tokt - 2])
                X.op(V, lambda e, q=q, h=h, lc=loc, sp=q: e.tensor_scalar(
                    out=ln2c2[q], in0=trkb2[sp],
                    scalar1=stats1[:, h * 32 + lc:h * 32 + lc + 1],
                    scalar2=stats1[:, 64 + h * 32 + lc:64 + h * 32 + lc + 1],
                    op0=ALU.subtract, op1=ALU.mult), "v_ms")
                norm_v[tokt] = X.cnt["v_ms"]
                X.wait(T, "v_ms", norm_v[tokt])
                if tokt >= 2:
                    X.wait(T, "s_ev", evac_ev[tokt - 2])
                pb = PT if q == 0 else PU
                for j in range(4):
                    X.mm(pb[:, j * 128:(j + 1) * 128],
                         ln2c2[q][:, j * 128:(j + 1) * 128],
                         ident_bf[:], True, True, "t_mm" if j == 3 else None)
                tr_mm[tokt] = X.cnt["t_mm"]
                X.wait(Sc, "t_mm", tr_mm[tokt])
                if tokt >= 2:
                    X.wait(Sc, "st_ln2t", (tokt - 1) * 16)
                X.op(Sc, lambda e, q=q, pb=pb: e.activation(
                    out=ln2t_st2[q], in_=pb[:, 0:512], func=AF.Copy), "s_ev")
                evac_ev[tokt] = X.cnt["s_ev"]
            st_ln2(63)
            X.wait(G, "st_ln2t", 64 * 16)
            X.cc("AllGather", ALU.bypass, [d_ln2t_inA.ap().opt()],
                 [d_ln2t_allA.ap().opt()], "cc_ln2")
            X.cc("AllGather", ALU.bypass, [d_ln2t_inB.ap().opt()],
                 [d_ln2t_allB.ap().opt()], "cc_ln2B")
            if DEBUG:
                X.wait(W, "st_ln2t", 64 * 16)
                X.dma(W, dbg["ln2tA"][:, :], d_ln2t_inA[:, :], "st_out")
                X.dma(W, dbg["ln2tB"][:, :], d_ln2t_inB[:, :], "st_out")
                X.wait(W, "s_ev", X.cnt["s_ev"])
                X.dma(W, dbg["stats"][:, :], stats1[:, :], "st_out")

            X.wait(W, "cc_ln2", 1)
            mm_2c = X.cnt["t_mm"]
            n_ld_l2 = 0
            n_ld_trk3 = 0
            st_out0 = X.cnt["st_out"]
            psb_l = [SC4[0], SC4[1], SC4[2]]
            psu_l = [SC4[3], PS_, PV]
            mm_main = {}
            mm_lora = {}
            mm_tr = {}
            v_hw = {}
            v_out = {}
            ev_ru = {}
            ev_hT = {}

            def close_tokt(k):
                qk = k % 2
                pbk = PT if qk == 0 else PU
                hT = hT_bf if qk == 0 else hT_bf2
                psb = psb_l[k % 3]
                X.wait(T, "v_ms", v_hw[k])
                if k >= 2:
                    X.wait(T, "s_ev", ev_hT[k - 2])
                X.mm(pbk[:, 0:128], hw_bf[:], ident_bf[:], True, True, "t_mm")
                mm_tr[k] = X.cnt["t_mm"]
                X.wait(Sc, "t_mm", X.cnt["t_mm"])
                X.op(Sc, lambda e, hT=hT, pbk=pbk: e.activation(
                    out=hT[:], in_=pbk[:, 0:128], func=AF.Copy), "s_ev")
                ev_hT[k] = X.cnt["s_ev"]
                X.wait(T, "s_ev", ev_hT[k])
                X.mm(psb, hT[:], bflat[:, 0:512], False, True, "t_mm")
                mm_lora[k] = X.cnt["t_mm"]
                X.wait(V, "t_mm", mm_lora[k])
                X.wait(V, "ld_trk3", (k + 1) * 16)
                if k >= 2:
                    X.wait(V, "st_out", st_out0 + (k - 1) * 16)
                X.op(V, lambda e, ps=psb, sp=k % 2: e.tensor_tensor(
                    out=out_st[sp], in0=ps, in1=trkb3[sp], op=ALU.add), "v_ms")
                v_out[k] = X.cnt["v_ms"]
                X.wait(STQ, "v_ms", v_out[k])
                X.dma(STQ, t_out[k * 128:k * 128 + 128, :],
                      out_st[k % 2][:, :], "st_out")

            for tg2 in range(16):
                parg = tg2 % 2
                if tg2 == 0:
                    X.wait(W, "cc_ln2B", 1)
                    X.wait(W, "t_mm", mm_2c)
                if tg2 >= 2:
                    X.wait(W, "t_mm", mm_main[tg2 * 4 - 5])
                l2src = d_ln2t_allA if tg2 < 8 else d_ln2t_allB
                c0 = (tg2 % 8) * 512
                src = l2src[:, c0:c0 + 512].rearrange("(g p) t -> p g t", p=128)
                dst = l2buf[parg].rearrange("p (g t) -> p g t", t=512)
                for q in range(4):
                    X.dma(W, dst[:, q * 8:(q + 1) * 8, :],
                          src[:, q * 8:(q + 1) * 8, :], "ld_l2")
                    n_ld_l2 += 16
                for tt in range(4):
                    tokt = tg2 * 4 + tt
                    t0 = tokt * 128
                    if tokt >= 2:
                        X.wait(W, "v_ms", v_out.get(tokt - 2, 0))
                    X.dma(W, trkb3[tokt % 2], d_trunk[t0:t0 + 128, :], "ld_trk3")
                    n_ld_trk3 += 16
                    psb = psb_l[tokt % 3]
                    psu = psu_l[tokt % 3]
                    X.wait(T, "ld_l2", n_ld_l2)
                    X.wait(T, "ld_g", n_g_wbt)
                    if tokt >= 3:
                        X.wait(T, "v_ms", v_out[tokt - 3])
                        X.wait(T, "s_ev", ev_ru[tokt - 3])
                    X.mm(psb, onesr_bf[0:1, :], bbase_bf[0:1, :], True, False)
                    for dt in range(32):
                        lhs = l2buf[parg][:, dt * 512 + tt * 128:
                                          dt * 512 + tt * 128 + 128]
                        X.mm(psb, lhs, wbaseT[:, dt * 512:(dt + 1) * 512],
                             False, False)
                        X.mm(psu[:, 0:136], lhs, wrA[:, dt * 136:(dt + 1) * 136],
                             dt == 0, dt == 31, "t_mm" if dt == 31 else None)
                    mm_main[tokt] = X.cnt["t_mm"]
                    X.wait(Sc, "t_mm", mm_main[tokt])
                    if tokt >= 2:
                        X.wait(Sc, "v_ms", v_hw[tokt - 2])
                    X.op(Sc, lambda e, u=psu, r=ru2[tokt % 2]: e.activation(
                        out=r[:, 0:136], in_=u[:, 0:136], func=AF.Copy), "s_ev")
                    ev_ru[tokt] = X.cnt["s_ev"]
                    if tokt >= 1:
                        close_tokt(tokt - 1)
                    ru = ru2[tokt % 2]
                    X.wait(V, "s_ev", ev_ru[tokt])
                    X.op(V, lambda e, u=ru: e.reduce_max(s1[:], u[:, 0:8],
                                                         axis=AX.X), "v_ms")
                    X.wait(V, "v_ms", X.cnt["v_ms"])
                    X.op(V, lambda e, u=ru: e.tensor_scalar(
                        out=tm1[:], in0=u[:, 0:8], scalar1=s1[:, 0:1],
                        scalar2=NEG_BIG, op0=ALU.is_ge, op1=ALU.mult), "v_ms")
                    X.wait(V, "v_ms", X.cnt["v_ms"])
                    X.op(V, lambda e, u=ru: e.tensor_add(tm2[:], u[:, 0:8], tm1[:]),
                         "v_ms")
                    X.wait(V, "v_ms", X.cnt["v_ms"])
                    X.op(V, lambda e: e.reduce_max(s2[:], tm2[:], axis=AX.X), "v_ms")
                    X.wait(V, "v_ms", X.cnt["v_ms"])
                    X.op(V, lambda e: e.tensor_scalar(
                        out=meanb[:], in0=s1[:], scalar1=-1.0, scalar2=None,
                        op0=ALU.mult))
                    X.op(V, lambda e: e.tensor_scalar(
                        out=varb[:], in0=s2[:], scalar1=s1[:, 0:1], scalar2=None,
                        op0=ALU.subtract))
                    X.op(V, lambda e, u=ru: e.tensor_scalar(
                        out=selw[:], in0=u[:, 0:8], scalar1=s2[:, 0:1],
                        scalar2=2.0, op0=ALU.is_ge, op1=ALU.mult), "v_ms")
                    X.wait(Sc, "v_ms", X.cnt["v_ms"])
                    X.op(Sc, lambda e, u=ru: e.activation(
                        out=p8[:], in_=u[:, 0:8], func=AF.Exp, bias=meanb[:, 0:1]),
                        "s_ev")
                    X.wait(Sc, "s_ev", X.cnt["s_ev"])
                    X.op(Sc, lambda e: e.activation(out=dnm[:], in_=varb[:],
                                                    func=AF.Exp), "s_ev")
                    X.wait(V, "s_ev", X.cnt["s_ev"])
                    X.op(V, lambda e: e.tensor_scalar(
                        out=dnm[:], in0=dnm[:], scalar1=1.0, scalar2=None,
                        op0=ALU.add), "v_ms")
                    X.wait(V, "v_ms", X.cnt["v_ms"])
                    X.op(V, lambda e: e.reciprocal(dnm[:], dnm[:]), "v_ms")
                    X.wait(V, "v_ms", X.cnt["v_ms"])
                    X.op(V, lambda e: e.scalar_tensor_tensor(
                        out=cmb[:], in0=p8[:], scalar=dnm[:, 0:1], in1=selw[:],
                        op0=ALU.mult, op1=ALU.mult), "v_ms")
                    X.wait(V, "v_ms", X.cnt["v_ms"])
                    if tokt >= 1:
                        X.wait(V, "t_mm", mm_tr[tokt - 1])
                    X.op(V, lambda e, u=ru: e.tensor_tensor(
                        out=hw_bf.rearrange("p (e r) -> p e r", r=16),
                        in0=u[:, 8:136].rearrange("p (e r) -> p e r", r=16),
                        in1=cmb[:, :, None].broadcast_to([128, 8, 16]),
                        op=ALU.mult), "v_ms")
                    v_hw[tokt] = X.cnt["v_ms"]
            close_tokt(63)

            if DEBUG >= 2:
                X.wait(W, "s_ev", X.cnt["s_ev"])
                X.wait(W, "t_mm", X.cnt["t_mm"])
                X.dma(W, dbg["wrA"][:, :], wrA[:, :], "st_out")
                X.dma(W, dbg["wbt"][:, :], wbaseT, "st_out")
                X.dma(W, dbg["bflat"][:, :], bflat[:, :], "st_out")
            X.wait(W, "st_out", X.cnt["st_out"])

        @block.sync
        def _(eng):
            ctx.set_engine("sync", eng)
            script(ctx)
            final_counts.append(dict(ctx.cnt))

        @block.gpsimd
        def _(eng):
            ctx.set_engine("gpsimd", eng)
            script(ctx)
            final_counts.append(dict(ctx.cnt))

        @block.vector
        def _(eng):
            ctx.set_engine("vector", eng)
            script(ctx)
            final_counts.append(dict(ctx.cnt))

        @block.scalar
        def _(eng):
            ctx.set_engine("scalar", eng)
            script(ctx)
            final_counts.append(dict(ctx.cnt))

        @block.tensor
        def _(eng):
            ctx.set_engine("tensor", eng)
            script(ctx)
            final_counts.append(dict(ctx.cnt))

    assert all(fc == final_counts[0] for fc in final_counts[1:]), "pass divergence"
    return nc


_NC_CACHE = {}


def _get_nc():
    key = ("v1", DEBUG)
    if key not in _NC_CACHE:
        _NC_CACHE[key] = build_nc()
    return _NC_CACHE[key]


def kernel(L0_attn_hidden_states, L0_attn_w_attn, L0_attn_b_attn, L0_attn_w_proj,
           L0_attn_b_proj, L0_ffn_W_base, L0_ffn_b_base, L0_ffn_W_router,
           L0_ffn_A_stack, L0_ffn_B_stack):
    x = np.ascontiguousarray(np.asarray(L0_attn_hidden_states, dtype=np.float32)
                             .reshape(NT, D))
    w_attn = np.asarray(L0_attn_w_attn, dtype=np.float32)
    b_attn = np.asarray(L0_attn_b_attn, dtype=np.float32)
    w_proj = np.asarray(L0_attn_w_proj, dtype=np.float32)
    b_proj = np.asarray(L0_attn_b_proj, dtype=np.float32)
    W_base = np.asarray(L0_ffn_W_base, dtype=np.float32)
    b_base = np.asarray(L0_ffn_b_base, dtype=np.float32)
    W_rout = np.asarray(L0_ffn_W_router, dtype=np.float32)
    A_stk = np.asarray(L0_ffn_A_stack, dtype=np.float32).reshape(128, D)
    B_stk = np.asarray(L0_ffn_B_stack, dtype=np.float32)

    ident = np.eye(128, dtype=np.float32)
    ones_sq = np.ones((128, 128), dtype=np.float32)
    ones_row = np.ones((1, 128), dtype=np.float32)
    iota = (np.arange(32, dtype=np.float32)[None, :] * 128
            + np.arange(128, dtype=np.float32)[:, None])
    masks = np.zeros((128, 4 * 512), dtype=np.float32)
    for j in range(4):
        dk = np.arange(128)[:, None]
        dq = np.arange(512)[None, :]
        masks[:, j * 512:(j + 1) * 512] = np.where(j * 128 + dk <= dq, 0.0, -300000.0)
    masks = masks.astype(ml_dtypes.bfloat16)
    slopes = _alibi_slopes(H)

    wrA_host = np.empty((128, 32, 136), dtype=np.float32)
    for dt in range(32):
        wrA_host[:, dt, 0:8] = W_rout[:, dt * 128:(dt + 1) * 128].T
        wrA_host[:, dt, 8:136] = A_stk[:, dt * 128:(dt + 1) * 128].T
    wrA_host = np.ascontiguousarray(wrA_host.reshape(128, 32 * 136))

    def _ksq3(core):
        out = np.empty((3, HL, 4480), dtype=ml_dtypes.bfloat16)
        mm_ = np.arange(4480, dtype=np.float64) - 384.0
        for hl in range(HL):
            sl = float(slopes[core * HL + hl])
            Xv = -128.0 * sl * mm_
            a = Xv.astype(ml_dtypes.bfloat16)
            r = Xv - a.astype(np.float64)
            b2 = r.astype(ml_dtypes.bfloat16)
            c3 = (r - b2.astype(np.float64)).astype(ml_dtypes.bfloat16)
            out[0, hl], out[1, hl], out[2, hl] = a, b2, c3
        return np.ascontiguousarray(out.reshape(3, HL * 4480))

    in_maps = []
    for c in range(8):
        qc = slice(c * OC, (c + 1) * OC)
        w_qkv_slice = np.concatenate(
            [w_attn[:, qc], w_attn[:, D + c * OC: D + (c + 1) * OC],
             w_attn[:, 2 * D + c * OC: 2 * D + (c + 1) * OC]], axis=1)
        b_qkv_slice = np.concatenate(
            [b_attn[qc], b_attn[D + c * OC: D + (c + 1) * OC],
             b_attn[2 * D + c * OC: 2 * D + (c + 1) * OC]])
        b_qkv_pp = np.ascontiguousarray(b_qkv_slice.reshape(12, 128).T)
        in_maps.append({
            "x_rows": np.ascontiguousarray(x[c * TOKB:(c + 1) * TOKB, :]),
            "x_cols": np.ascontiguousarray(x[:, qc]),
            "w_qkv_slice": np.ascontiguousarray(w_qkv_slice),
            "b_qkv_pp": b_qkv_pp,
            "w_proj_cols": np.ascontiguousarray(w_proj[:, qc]),
            "b_proj_row": np.ascontiguousarray(b_proj[qc][None, :]),
            "w_baseT": np.ascontiguousarray(W_base[qc, :].T),
            "b_base_row": np.ascontiguousarray(b_base[qc][None, :]),
            "wrA_host": wrA_host,
            "b_flat_my": np.ascontiguousarray(
                B_stk[:, qc, :].transpose(0, 2, 1).reshape(128, OC)),
            "ident": ident,
            "ones_sq": ones_sq,
            "ones_row": ones_row,
            "iota_k": iota,
            "diag_masks": masks,
            "slopes_row": np.ascontiguousarray(slopes[c * HL:(c + 1) * HL][None, :]),
            "ksq3": _ksq3(c),
        })

    nc = _get_nc()
    res = run_bass_kernel_spmd(nc, in_maps, core_ids=list(range(8)))
    out = np.empty((NT, D), dtype=np.float32)
    for c in range(8):
        out[:, c * OC:(c + 1) * OC] = res.results[c]["out_cols"]
    if DEBUG:
        kernel._dbg = [res.results[c] for c in range(8)]
    return out.reshape(BATCH, S, D)


# version 13
# speedup vs baseline: 1.0881x; 1.0858x over previous
import numpy as np
import ml_dtypes
from contextlib import ExitStack

import concourse.bass as bass
import concourse.mybir as mybir
from concourse.bass_utils import run_bass_kernel_spmd

F32 = mybir.dt.float32
BF16 = mybir.dt.bfloat16
AF = mybir.ActivationFunctionType
ALU = mybir.AluOpType
AX = mybir.AxisListType

D = 4096
S = 4096
BATCH = 2
NT = BATCH * S
H = 32
DH = 128
HL = 4
NC = 8
OC = 512
TOKB = NT // NC
SCALE = 1.0 / DH
EPS = 1e-5
NEG_BIG = -30000.0

DEBUG = 0

KB = 1024


def _alibi_slopes(n_heads):
    import math
    cp2 = 2 ** math.floor(math.log2(n_heads))
    base = 2 ** (-(2 ** (-(math.log2(cp2) - 3))))
    slopes = base ** np.arange(1, cp2 + 1, dtype=np.float32)
    if cp2 != n_heads:
        extra_base = 2 ** (-(2 ** (-(math.log2(2 * cp2) - 3))))
        n_rem = min(cp2, n_heads - cp2)
        extra = extra_base ** np.arange(1, 1 + 2 * n_rem, 2, dtype=np.float32)
        slopes = np.concatenate([slopes, extra])
    return slopes.astype(np.float32)


class Ctx:

    def __init__(self, nc, sems):
        self.nc = nc
        self.sems = sems
        self.cnt = {k: 0 for k in sems}
        self.eng = None
        self.E = None

    def set_engine(self, name, eng):
        self.E = name
        self.eng = eng
        self.cnt = {k: 0 for k in self.sems}

    def dma(self, q, out, in_, sem, slow=False):
        if self.E == q:
            kw = {}
            if slow:
                kw["allow_slow_non_contiguous"] = True
            self.eng.dma_start(out=out, in_=in_, **kw).then_inc(self.sems[sem], 16)
        self.cnt[sem] += 16

    def dmat(self, q, out, in_, sem):
        if self.E == q:
            self.eng.dma_start_transpose(out=out, in_=in_) \
                .then_inc(self.sems[sem], 16)
        self.cnt[sem] += 16

    def cc(self, kind, op, ins, outs, sem):
        if self.E == "gpsimd":
            self.eng.collective_compute(
                kind, op, replica_groups=[list(range(NC))], ins=ins, outs=outs
            ).then_inc(self.sems[sem])
        self.cnt[sem] += 1

    def mm(self, out, lhsT, rhs, start, stop, sem=None):
        if self.E == "tensor":
            i = self.nc.tensor.matmul(out, lhsT, rhs, start=start, stop=stop)
            if sem is not None:
                i.then_inc(self.sems[sem], 1)
        if sem is not None:
            self.cnt[sem] += 1

    def op(self, eng_name, fn, sem=None):
        if self.E == eng_name:
            i = fn(self.eng)
            if sem is not None and i is not None:
                i.then_inc(self.sems[sem], 1)
        if sem is not None:
            self.cnt[sem] += 1

    def wait(self, who, sem, val):
        assert 0 <= val <= self.cnt[sem], (who, sem, val, self.cnt[sem])
        if self.E == who:
            self.eng.wait_ge(self.sems[sem], val)


def build_nc():
    nc = bass.Bass("TRN2", target_bir_lowering=False)

    t_x_rows = nc.dram_tensor("x_rows", [TOKB, D], F32, kind="ExternalInput")
    t_x_cols = nc.dram_tensor("x_cols", [NT, OC], F32, kind="ExternalInput")
    t_wqkv = nc.dram_tensor("w_qkv_slice", [D, 3 * OC], F32, kind="ExternalInput")
    t_bqkv = nc.dram_tensor("b_qkv_pp", [128, 12], F32, kind="ExternalInput")
    t_wproj = nc.dram_tensor("w_proj_cols", [D, OC], F32, kind="ExternalInput")
    t_bproj = nc.dram_tensor("b_proj_row", [1, OC], F32, kind="ExternalInput")
    t_wbT = nc.dram_tensor("w_baseT", [D, OC], F32, kind="ExternalInput")
    t_bbase = nc.dram_tensor("b_base_row", [1, OC], F32, kind="ExternalInput")
    t_wrA = nc.dram_tensor("wrA_host", [128, 32 * 136], F32, kind="ExternalInput")
    t_bfl = nc.dram_tensor("b_flat_my", [128, OC], F32, kind="ExternalInput")
    t_ident = nc.dram_tensor("ident", [128, 128], F32, kind="ExternalInput")
    t_ones = nc.dram_tensor("ones_sq", [128, 128], F32, kind="ExternalInput")
    t_onesr = nc.dram_tensor("ones_row", [1, 128], F32, kind="ExternalInput")
    t_iota = nc.dram_tensor("iota_k", [128, 32], F32, kind="ExternalInput")
    t_masks = nc.dram_tensor("diag_masks", [128, 4 * 512], BF16,
                             kind="ExternalInput")
    t_slope = nc.dram_tensor("slopes_row", [1, HL], F32, kind="ExternalInput")
    t_ksq = nc.dram_tensor("ksq3", [3, HL * 4480], BF16, kind="ExternalInput")

    t_out = nc.dram_tensor("out_cols", [NT, OC], F32, kind="ExternalOutput")

    d_ln1t_in = nc.dram_tensor("ln1t_in", [D, TOKB], BF16)
    d_ln1t_all = nc.dram_tensor("ln1t_all", [NC * D, TOKB], BF16, addr_space="Shared")
    d_qkvt = nc.dram_tensor("qkvt", [3 * OC, NT], BF16)
    d_attnt_in = nc.dram_tensor("attnt_in", [OC, NT], BF16)
    d_attnt_all = nc.dram_tensor("attnt_all", [NC * OC, NT], BF16, addr_space="Shared")
    d_ln2t_inA = nc.dram_tensor("ln2t_inA", [OC, S], BF16)
    d_ln2t_inB = nc.dram_tensor("ln2t_inB", [OC, S], BF16)
    d_ln2t_allA = nc.dram_tensor("ln2t_allA", [NC * OC, S], BF16,
                                 addr_space="Shared")
    d_ln2t_allB = nc.dram_tensor("ln2t_allB", [NC * OC, S], BF16,
                                 addr_space="Shared")
    d_stA_in = nc.dram_tensor("ln2statsA_in", [128, 64], F32)
    d_stA_out = nc.dram_tensor("ln2statsA_out", [128, 64], F32,
                               addr_space="Shared")
    d_stB_in = nc.dram_tensor("ln2statsB_in", [128, 64], F32)
    d_stB_out = nc.dram_tensor("ln2statsB_out", [128, 64], F32,
                               addr_space="Shared")
    d_trunk = nc.dram_tensor("trunk", [NT, OC], F32)

    dbg = {}
    if DEBUG:
        dbg["ln1t"] = nc.dram_tensor("dbg_ln1t", [D, TOKB], BF16, kind="ExternalOutput")
        dbg["qkvt"] = nc.dram_tensor("dbg_qkvt", [3 * OC, NT], BF16, kind="ExternalOutput")
        dbg["attnt"] = nc.dram_tensor("dbg_attnt", [OC, NT], BF16, kind="ExternalOutput")
        dbg["trunk"] = nc.dram_tensor("dbg_trunk", [NT, OC], F32, kind="ExternalOutput")
        dbg["ln2tA"] = nc.dram_tensor("dbg_ln2tA", [OC, S], BF16, kind="ExternalOutput")
        dbg["ln2tB"] = nc.dram_tensor("dbg_ln2tB", [OC, S], BF16, kind="ExternalOutput")
        dbg["stats"] = nc.dram_tensor("dbg_stats", [128, 128], F32, kind="ExternalOutput")
    if DEBUG >= 2:
        dbg["probe"] = nc.dram_tensor("dbg_probe", [128, 2048], F32, kind="ExternalOutput")
        dbg["wrA"] = nc.dram_tensor("dbg_wrA", [128, 32 * 136], BF16, kind="ExternalOutput")
        dbg["wbt"] = nc.dram_tensor("dbg_wbt", [128, 16384], BF16, kind="ExternalOutput")
        dbg["bflat"] = nc.dram_tensor("dbg_bflat", [128, 512], BF16, kind="ExternalOutput")

    es = ExitStack()
    final_counts = []
    with es:
        E = es.enter_context

        ident_bf = E(nc.sbuf_tensor("ident_bf", [128, 128], BF16))
        ones_bf = E(nc.sbuf_tensor("ones_bf", [128, 128], BF16))
        onesr_bf = E(nc.sbuf_tensor("onesr_bf", [1, 128], BF16))
        onesr_f = E(nc.sbuf_tensor("onesr_f", [1, 128], F32))
        iota_k = E(nc.sbuf_tensor("iota_k_sb", [128, 32], F32))
        masks_bf = E(nc.sbuf_tensor("masks_bf", [128, 4 * 512], BF16))
        slope_row = E(nc.sbuf_tensor("slope_row", [1, HL], F32))
        slope_pp = E(nc.sbuf_tensor("slope_pp", [128, HL], F32))
        bias4 = E(nc.sbuf_tensor("bias4", [128, HL], F32))
        bqkv_pp = E(nc.sbuf_tensor("bqkv_pp", [128, 12], F32))
        bproj_bf = E(nc.sbuf_tensor("bproj_bf", [1, OC], BF16))
        bbase_bf = E(nc.sbuf_tensor("bbase_bf", [1, OC], BF16))
        wrA = E(nc.sbuf_tensor("wrA", [128, 32 * 136], BF16))
        bflat = E(nc.sbuf_tensor("bflat", [128, 4 * 128], BF16))
        stats1 = E(nc.sbuf_tensor("stats1", [128, 128], F32))
        stats2 = E(nc.sbuf_tensor("stats2", [128, 128], F32))
        s1 = E(nc.sbuf_tensor("s1", [128, 1], F32))
        s2 = E(nc.sbuf_tensor("s2", [128, 1], F32))
        meanb = E(nc.sbuf_tensor("meanb", [128, 1], F32))
        varb = E(nc.sbuf_tensor("varb", [128, 1], F32))
        rstdb = E(nc.sbuf_tensor("rstdb", [128, 1], F32))
        tm1 = E(nc.sbuf_tensor("tm1", [128, 8], F32))
        tm2 = E(nc.sbuf_tensor("tm2", [128, 8], F32))
        qmax2 = E(nc.sbuf_tensor("qmax2", [128, 1], F32))
        kmax2 = E(nc.sbuf_tensor("kmax2", [128, 1], F32))
        qkmax = E(nc.sbuf_tensor("qkmax", [128, 1], F32))
        p8 = E(nc.sbuf_tensor("p8", [128, 8], F32))
        selw = E(nc.sbuf_tensor("selw", [128, 8], F32))
        cmb = E(nc.sbuf_tensor("cmb", [128, 8], F32))
        hw_bf = E(nc.sbuf_tensor("hw_bf", [128, 128], BF16))
        hT_bf = E(nc.sbuf_tensor("hT_bf", [128, 128], BF16))
        hT_bf2 = E(nc.sbuf_tensor("hT_bf2", [128, 128], BF16))
        dnm = E(nc.sbuf_tensor("dnm", [128, 1], F32))

        AR_BYTES = 185 * KB
        arena = E(nc.sbuf_tensor("arena", [128, AR_BYTES // 2], BF16))

        def abf(off_kb, nbytes):
            o = int(off_kb * KB)
            assert o + nbytes <= AR_BYTES
            return arena[:, o // 2:(o + nbytes) // 2]

        def af32(off_kb, nbytes):
            return abf(off_kb, nbytes).bitcast(F32)

        wq_sb = abf(0, 96 * KB)
        ksq3v = abf(0, 35 * KB)[0:3, :]
        wbaseT = abf(0, 32 * KB)
        wproj_sb = abf(64, 32 * KB)

        xbuf = [af32(96, 16 * KB), af32(112, 16 * KB)]
        sqbuf = af32(128, 16 * KB)
        ln1_tok = abf(144, 8 * KB)
        ln1t_st = abf(152, 8 * KB)
        lnh = [abf(96, 16 * KB), abf(112, 16 * KB),
               abf(128, 16 * KB), abf(144, 16 * KB)]
        ktb = [abf(96, 8 * KB), abf(120, 8 * KB)]
        vtb = [abf(104, 8 * KB), abf(128, 8 * KB)]
        qtb = [abf(112, 8 * KB), abf(136, 8 * KB)]
        vsb2b = [abf(144, 8 * KB), abf(164, 8 * KB)]
        sq_bf = abf(152, 8 * KB)
        attbuf = [abf(96, 32 * KB), abf(128, 32 * KB)]
        l2buf = [abf(96, 32 * KB), abf(128, 32 * KB)]
        expb2 = [abf(160, 2 * KB), abf(162, 2 * KB)]
        rec = af32(178, 2 * KB)
        att_st = [abf(180, 1 * KB), abf(181, 1 * KB)]
        qkv_st = [abf(183, 1 * KB), abf(184, 1 * KB)]
        xcbuf = [af32(160, 2 * KB), af32(162, 2 * KB)]
        trk_st = [af32(164, 2 * KB), af32(166, 2 * KB)]
        sqc = af32(168, 2 * KB)
        trkb2 = [af32(170, 2 * KB), af32(172, 2 * KB)]
        ln2c2 = [abf(174, 1 * KB), abf(160, 1 * KB)]
        ln2t_st2 = [abf(175, 1 * KB), abf(161, 1 * KB)]
        trkb3 = [af32(176, 2 * KB), af32(178, 2 * KB)]
        out_st = [af32(180, 2 * KB), af32(182, 2 * KB)]
        ru2 = [af32(32, 1 * KB), af32(33, 1 * KB)]

        PA = E(nc.psum_tensor("PA", [128, 1024], F32))
        PB = E(nc.psum_tensor("PB", [128, 1024], F32))
        PS_ = E(nc.psum_tensor("PS", [128, 512], F32))
        PV = E(nc.psum_tensor("PV", [128, 512], F32))
        PT = E(nc.psum_tensor("PT", [128, 512], F32))
        PU = E(nc.psum_tensor("PU", [128, 512], F32))
        SC4 = [PA[:, 0:512], PA[:, 512:1024], PB[:, 0:512], PB[:, 512:1024]]

        sem_names = [
            "ld_sy", "ld_g", "st_ln1t", "st_qkv", "st_att", "st_trk", "st_stats",
            "st_ln2t", "st_out", "ld_ln", "ld_pair", "ld_att", "ld_xc", "ld_trk2",
            "ld_l2", "ld_trk3", "t_mm", "s_ev", "v_ms",
            "cc_ln1", "cc_att", "cc_st", "cc_stB", "cc_ln2", "cc_ln2B",
        ]
        sems = {n: E(nc.semaphore(n)) for n in sem_names}
        block = E(nc.Block())
        ctx = Ctx(nc, sems)

        def script(X: Ctx):
            W, G, V, Sc, T = "sync", "gpsimd", "vector", "scalar", "tensor"
            STQ = G

            X.dma(W, iota_k[:], t_iota[:, :], "ld_sy")
            X.dma(W, slope_row[:], t_slope[:, :], "ld_sy")
            X.dma(W, bqkv_pp[:], t_bqkv[:, :], "ld_sy")
            X.dma(W, onesr_f[:], t_onesr[:, :], "ld_sy")
            n_sy1 = X.cnt["ld_sy"]
            X.dma(G, ident_bf[:], t_ident[:, :], "ld_g")
            X.dma(G, ones_bf[:], t_ones[:, :], "ld_g")
            X.dma(G, onesr_bf[:], t_onesr[:, :], "ld_g")
            X.dma(G, masks_bf[:], t_masks[:, :], "ld_g")
            X.dma(G, bproj_bf[:], t_bproj[:, :], "ld_g")
            X.dma(G, bbase_bf[:], t_bbase[:, :], "ld_g")
            X.dma(G, wrA[:, :], t_wrA[:, :], "ld_g")
            X.dma(G, bflat[:, :], t_bfl[:, :], "ld_g")
            n_g0 = X.cnt["ld_g"]
            src_wq = t_wqkv.rearrange("(dt p) j -> p dt j", p=128)
            dst_wq = wq_sb.rearrange("p (dt j) -> p dt j", j=1536)
            for i in range(4):
                X.dma(G, dst_wq[:, i * 8:(i + 1) * 8, :],
                      src_wq[:, i * 8:(i + 1) * 8, :], "ld_g")
            n_g_wq = X.cnt["ld_g"]

            X.wait(T, "ld_sy", n_sy1)
            X.mm(PU[:, 0:HL], onesr_f[0:1, :], slope_row[0:1, :], True, True, "t_mm")
            X.wait(V, "t_mm", X.cnt["t_mm"])
            X.op(V, lambda e: e.tensor_copy(out=slope_pp[:], in_=PU[:, 0:HL]), "v_ms")
            v_slope = X.cnt["v_ms"]

            ev_ln0 = X.cnt["s_ev"]
            for i in range(8):
                if i >= 2:
                    X.wait(W, "v_ms", v_slope + 4 * i - 4)
                X.dma(W, xbuf[i % 2], t_x_rows[i * 128:(i + 1) * 128, :], "ld_sy")
                ld_here = X.cnt["ld_sy"]
                X.wait(V, "ld_sy", ld_here)
                X.op(V, lambda e, b=xbuf[i % 2]: e.reduce_sum(s1[:], b, axis=AX.X))
                X.wait(Sc, "ld_sy", ld_here)
                if i > 0:
                    X.wait(Sc, "v_ms", v_slope + 4 * i - 3)
                X.op(Sc, lambda e, b=xbuf[i % 2]: e.square(sqbuf, b), "s_ev")
                sq_done = X.cnt["s_ev"]
                X.wait(V, "s_ev", sq_done)
                X.op(V, lambda e: e.reduce_sum(s2[:], sqbuf, axis=AX.X))
                X.op(V, lambda e: e.tensor_scalar(out=meanb[:], in0=s1[:],
                                                  scalar1=1.0 / D, scalar2=None,
                                                  op0=ALU.mult), "v_ms")
                X.wait(V, "v_ms", X.cnt["v_ms"])
                X.op(V, lambda e: e.tensor_scalar(out=tm1[:, 0:1], in0=meanb[:],
                                                  scalar1=meanb[:, 0:1], scalar2=EPS,
                                                  op0=ALU.mult, op1=ALU.subtract),
                     "v_ms")
                X.wait(V, "v_ms", X.cnt["v_ms"])
                X.op(V, lambda e: e.scalar_tensor_tensor(
                    out=varb[:], in0=s2[:], scalar=1.0 / D, in1=tm1[:, 0:1],
                    op0=ALU.mult, op1=ALU.subtract), "v_ms")
                var_done = X.cnt["v_ms"]
                X.wait(Sc, "v_ms", var_done)
                X.op(Sc, lambda e: e.activation(out=varb[:], in_=varb[:], func=AF.Ln),
                     "s_ev")
                X.wait(Sc, "s_ev", X.cnt["s_ev"])
                X.op(Sc, lambda e: e.activation(out=rstdb[:], in_=varb[:],
                                                func=AF.Exp, scale=-0.5), "s_ev")
                rstd_done = X.cnt["s_ev"]
                X.wait(V, "s_ev", rstd_done)
                X.op(V, lambda e, b=xbuf[i % 2]: e.tensor_scalar(
                    out=ln1_tok, in0=b, scalar1=meanb[:, 0:1], scalar2=rstdb[:, 0:1],
                    op0=ALU.subtract, op1=ALU.mult), "v_ms")
                nrm_done = X.cnt["v_ms"]
                assert nrm_done == v_slope + 4 * i + 4
                X.wait(T, "ld_g", n_g0)
                X.wait(T, "v_ms", nrm_done)
                for grp in range(8):
                    if i > 0 or grp > 0:
                        X.wait(T, "s_ev", rstd_done + grp)
                    for j in range(4):
                        dt = grp * 4 + j
                        X.mm(PT[:, j * 128:(j + 1) * 128],
                             ln1_tok[:, dt * 128:(dt + 1) * 128], ident_bf[:],
                             True, True, "t_mm" if j == 3 else None)
                    X.wait(Sc, "t_mm", X.cnt["t_mm"])
                    if grp == 0 and i > 0:
                        X.wait(Sc, "st_ln1t", i * 16)
                    X.op(Sc, lambda e, g=grp: e.activation(
                        out=ln1t_st[:, g * 512:(g + 1) * 512], in_=PT[:, 0:512],
                        func=AF.Copy), "s_ev")
                X.wait(STQ, "s_ev", X.cnt["s_ev"])
                X.dma(STQ, d_ln1t_in[:, i * 128:(i + 1) * 128]
                      .rearrange("(dt p) t -> p dt t", p=128),
                      ln1t_st.rearrange("p (dt t) -> p dt t", t=128), "st_ln1t")
            X.wait(G, "st_ln1t", 8 * 16)
            X.cc("AllGather", ALU.bypass, [d_ln1t_in.ap().opt()],
                 [d_ln1t_all.ap().opt()], "cc_ln1")
            if DEBUG:
                X.wait(W, "st_ln1t", 8 * 16)
                X.dma(W, dbg["ln1t"][:, :], d_ln1t_in[:, :], "st_out")

            X.wait(W, "cc_ln1", 1)
            ev_1a = X.cnt["s_ev"]
            mm_1a = X.cnt["t_mm"]

            def lnslot(tokg, h):
                return (2 * tokg + h) % 4

            n_ld_ln = 0
            colg_idx = 0
            for tokg in range(16):
                r = tokg // 2
                c0 = (tokg % 2) * 512
                for h in range(2):
                    if tokg >= 2:
                        X.wait(W, "t_mm", mm_1a + (tokg - 1) * 12)
                    src = d_ln1t_all[r * D:(r + 1) * D, c0:c0 + 512] \
                        .rearrange("(dt p) t -> p dt t", p=128)
                    X.dma(W, lnh[lnslot(tokg, h)]
                          .rearrange("p (dt t) -> p dt t", t=512),
                          src[:, h * 16:(h + 1) * 16, :], "ld_ln")
                    n_ld_ln += 16
                for colg in range(12):
                    ps = SC4[colg_idx % 4]
                    X.wait(T, "ld_ln", n_ld_ln)
                    X.wait(T, "ld_g", n_g_wq)
                    if colg_idx >= 4:
                        X.wait(T, "s_ev", ev_1a + colg_idx - 3)
                    for dt in range(32):
                        hh = dt // 16
                        lhs = wq_sb[:, dt * 1536 + colg * 128:
                                    dt * 1536 + (colg + 1) * 128]
                        rhs = lnh[lnslot(tokg, hh)][:, (dt % 16) * 512:
                                                    (dt % 16 + 1) * 512]
                        X.mm(ps, lhs, rhs, dt == 0, dt == 31,
                             "t_mm" if dt == 31 else None)
                    X.wait(Sc, "t_mm", X.cnt["t_mm"])
                    if colg_idx >= 2:
                        X.wait(Sc, "st_qkv", (colg_idx - 1) * 16)
                    X.op(Sc, lambda e, ps=ps, cg=colg, st=qkv_st[colg_idx % 2]:
                         e.activation(out=st, in_=ps, func=AF.Identity,
                                      bias=bqkv_pp[:, cg:cg + 1]), "s_ev")
                    X.wait(STQ, "s_ev", X.cnt["s_ev"])
                    g0 = tokg * 512
                    X.dma(STQ, d_qkvt[colg * 128:(colg + 1) * 128, g0:g0 + 512],
                          qkv_st[colg_idx % 2][:, :], "st_qkv")
                    colg_idx += 1
            assert colg_idx == 192
            if DEBUG:
                X.wait(W, "st_qkv", 192 * 16)
                X.dma(W, dbg["qkvt"][:, :], d_qkvt[:, :], "st_out")

            X.wait(G, "t_mm", mm_1a + 192)
            src_wp = t_wproj.rearrange("(g p) o -> p g o", p=128)
            dst_wp = wproj_sb.rearrange("p (g o) -> p g o", o=512)
            for i in range(2):
                X.dma(G, dst_wp[:, i * 16:(i + 1) * 16, :],
                      src_wp[:, i * 16:(i + 1) * 16, :], "ld_g")
            n_g_wproj = X.cnt["ld_g"]

            st_att_n = 0
            n_ld_pair = 0
            mm_1a_end = X.cnt["t_mm"]
            pair_mm_end = {}
            ksq_ld = {}
            exp_hist = []
            att_v2 = {0: 0, 1: 0}
            qt_ctr = 0
            gidx = 0

            def pair_load(p_idx):
                nonlocal n_ld_pair
                hl, b = p_idx // 2, p_idx % 2
                par = p_idx % 2
                X.wait(W, "st_qkv", 192 * 16)
                if p_idx >= 2:
                    X.wait(W, "t_mm", pair_mm_end[p_idx - 2])
                for (buf, row0) in ((ktb[par], 512 + hl * 128),
                                    (qtb[par], hl * 128)):
                    X.dma(W, buf, d_qkvt[row0:row0 + 128, b * S:(b + 1) * S],
                          "ld_pair")
                    n_ld_pair += 16
                vr = 1024 + hl * 128
                for kt in range(32):
                    X.dmat(W, vsb2b[par][:, kt * 128:(kt + 1) * 128],
                           d_qkvt[vr:vr + 128,
                                  b * S + kt * 128:b * S + (kt + 1) * 128],
                           "ld_pair")
                    n_ld_pair += 16

            X.wait(W, "t_mm", mm_1a_end)
            X.dma(W, ksq3v, t_ksq[:, :], "ld_sy")
            ksq_ld_n = X.cnt["ld_sy"]

            X.op(V, lambda e: e.tensor_tensor(
                out=bias4[:, 0:HL],
                in0=iota_k[:, 0:1].broadcast_to([128, HL]),
                in1=slope_pp[:, 0:HL], op=ALU.mult), "v_ms")
            bias_done = X.cnt["v_ms"]

            pair_load(0)
            pair_load(1)
            for p_idx in range(8):
                hl, b = p_idx // 2, p_idx % 2
                par = p_idx % 2
                vsb = vsb2b[par]

                X.wait(T, "ld_pair", (p_idx + 1) * 34 * 16)
                if p_idx == 0:
                    X.wait(T, "ld_sy", ksq_ld_n)

                groups = [(qt, g) for qt in range(8) for g in range(2 * qt + 2)]
                sc_end = {}
                ex_end = {}
                rs_end = {}

                def emit_rsav(k):
                    nonlocal st_att_n, qt_ctr
                    qtk, gk = groups[k]
                    slot = expb2[(gbase + k) % 2]
                    first = gk == 0
                    last = gk == 2 * qtk + 1
                    PSq = PS_ if qtk % 2 == 0 else PT
                    PVq = PV if qtk % 2 == 0 else PU
                    X.wait(T, "s_ev", ex_end[k])
                    if first:
                        X.wait(T, "v_ms", att_v2[qtk % 2])
                    for j in range(2):
                        X.mm(PSq[:, 0:512], ones_bf[:],
                             slot[:, j * 512:(j + 1) * 512],
                             first and j == 0, last and j == 1, None)
                    for j in range(2):
                        kt = 2 * gk + j
                        X.mm(PVq[:, 0:512], vsb[:, kt * 128:(kt + 1) * 128],
                             slot[:, j * 512:(j + 1) * 512],
                             first and j == 0, last and j == 1,
                             "t_mm" if j == 1 else None)
                    rs_end[k] = X.cnt["t_mm"]
                    if last:
                        X.wait(V, "t_mm", X.cnt["t_mm"])
                        X.op(V, lambda e, PSq=PSq: e.reciprocal(
                            rec, PSq[:, 0:512]), "v_ms")
                        X.wait(V, "v_ms", X.cnt["v_ms"])
                        sidx = qt_ctr % 2
                        if qt_ctr >= 2:
                            X.wait(V, "st_att", st_att_n - 16)
                        X.op(V, lambda e, st=att_st[sidx], PVq=PVq:
                             e.tensor_tensor(out=st, in0=PVq[:, 0:512],
                                             in1=rec, op=ALU.mult), "v_ms")
                        att_v2[qtk % 2] = X.cnt["v_ms"]
                        X.wait(W, "v_ms", att_v2[qtk % 2])
                        X.dma(W, d_attnt_in[hl * 128:(hl + 1) * 128,
                                            b * S + qtk * 512: b * S + (qtk + 1) * 512],
                              att_st[sidx][:, :], "st_att")
                        st_att_n += 16
                        qt_ctr += 1

                gbase = gidx
                for i, (qt, g) in enumerate(groups):
                    ps = PA if (gbase + i) % 2 == 0 else PB
                    slot = expb2[(gbase + i) % 2]
                    diag = g >= 2 * qt
                    if gbase + i >= 2:
                        X.wait(T, "s_ev", exp_hist[gbase + i - 2])
                    for j in range(2):
                        kt = 2 * g + j
                        o = 4 * qt - kt
                        k0 = hl * 4480 + (o + 3) * 128
                        X.mm(ps[:, j * 512:(j + 1) * 512], ones_bf[0:3, :],
                             ksq3v[:, k0:k0 + 512], True, False)
                        X.mm(ps[:, j * 512:(j + 1) * 512],
                             ktb[par][:, kt * 128:(kt + 1) * 128],
                             qtb[par][:, qt * 512:(qt + 1) * 512], False, True,
                             "t_mm" if j == 1 else None)
                    sc_end[i] = X.cnt["t_mm"]
                    if i >= 2:
                        emit_rsav(i - 2)
                    if diag:
                        X.wait(V, "t_mm", sc_end[i])
                        moff = 0 if g == 2 * qt else 1024
                        X.op(V, lambda e, ps=ps, mo=moff: e.tensor_tensor(
                            out=ps[:, 0:1024], in0=ps[:, 0:1024],
                            in1=masks_bf[:, mo:mo + 1024], op=ALU.add), "v_ms")
                        X.wait(Sc, "v_ms", X.cnt["v_ms"])
                    X.wait(Sc, "t_mm", rs_end[i - 2] if i >= 2 else sc_end[i])
                    if i == 0:
                        X.wait(Sc, "v_ms", bias_done)
                    X.op(Sc, lambda e, ps=ps, st=slot: e.activation(
                        out=st, in_=ps[:, 0:1024], func=AF.Exp,
                        bias=bias4[:, hl:hl + 1], scale=SCALE), "s_ev")
                    ex_end[i] = X.cnt["s_ev"]
                    exp_hist.append(ex_end[i])
                emit_rsav(len(groups) - 2)
                emit_rsav(len(groups) - 1)
                gidx = gbase + len(groups)
                pair_mm_end[p_idx] = X.cnt["t_mm"]
                if p_idx + 2 < 8:
                    pair_load(p_idx + 2)

            X.wait(G, "st_att", st_att_n)
            X.cc("AllGather", ALU.bypass, [d_attnt_in.ap().opt()],
                 [d_attnt_all.ap().opt()], "cc_att")
            if DEBUG:
                X.wait(W, "st_att", st_att_n)
                X.dma(W, dbg["attnt"][:, :], d_attnt_in[:, :], "st_out")

            X.wait(G, "cc_att", 1)
            X.wait(G, "t_mm", pair_mm_end[7])
            src_wb = t_wbT.rearrange("(dt p) o -> p dt o", p=128)
            dst_wb = wbaseT.rearrange("p (dt o) -> p dt o", o=512)
            for i in range(2):
                X.dma(G, dst_wb[:, i * 16:(i + 1) * 16, :],
                      src_wb[:, i * 16:(i + 1) * 16, :], "ld_g")
            n_g_wbt = X.cnt["ld_g"]

            X.wait(W, "cc_att", 1)
            mm_2a = X.cnt["t_mm"]
            ev_2a = X.cnt["s_ev"]
            vm_2a = X.cnt["v_ms"]
            n_ld_att = 0
            n_ld_xc = 0
            n_st_trk = 0
            for tg2 in range(16):
                parg = tg2 % 2
                if tg2 >= 2:
                    X.wait(W, "t_mm", mm_2a + (tg2 - 1) * 4)
                src = d_attnt_all[:, tg2 * 512:(tg2 + 1) * 512] \
                    .rearrange("(g p) t -> p g t", p=128)
                dst = attbuf[parg].rearrange("p (g t) -> p g t", t=512)
                for q in range(4):
                    X.dma(W, dst[:, q * 8:(q + 1) * 8, :],
                          src[:, q * 8:(q + 1) * 8, :], "ld_att")
                    n_ld_att += 16
                for tt in range(4):
                    tokt = tg2 * 4 + tt
                    t0 = tokt * 128
                    spar = tokt % 2
                    if tokt >= 2:
                        X.wait(W, "v_ms", vm_2a + (tokt - 2) * 2 + 1)
                    X.dma(W, xcbuf[spar], t_x_cols[t0:t0 + 128, :], "ld_xc")
                    n_ld_xc += 16
                    ps = SC4[tokt % 4]
                    X.wait(T, "ld_att", n_ld_att)
                    X.wait(T, "ld_g", n_g_wproj)
                    if tokt >= 4:
                        X.wait(T, "v_ms", vm_2a + (tokt - 4) * 2 + 1)
                    X.mm(ps, onesr_bf[0:1, :], bproj_bf[0:1, :], True, False)
                    for g in range(32):
                        X.mm(ps, attbuf[parg][:, g * 512 + tt * 128:
                                              g * 512 + tt * 128 + 128],
                             wproj_sb[:, g * 512:(g + 1) * 512],
                             False, g == 31, "t_mm" if g == 31 else None)
                    mm_here = X.cnt["t_mm"]
                    X.wait(V, "t_mm", mm_here)
                    X.wait(V, "ld_xc", n_ld_xc)
                    if tokt >= 2:
                        X.wait(V, "st_trk", n_st_trk - 16)
                        X.wait(V, "s_ev", ev_2a + tokt - 1)
                    X.op(V, lambda e, ps=ps, sp=spar: e.tensor_tensor(
                        out=trk_st[sp], in0=ps, in1=xcbuf[sp], op=ALU.add), "v_ms")
                    X.wait(V, "v_ms", X.cnt["v_ms"])
                    X.op(V, lambda e, sp=spar, tk=tokt: e.reduce_sum(
                        stats1[:, tk:tk + 1], trk_st[sp], axis=AX.X))
                    X.wait(Sc, "v_ms", X.cnt["v_ms"])
                    X.op(Sc, lambda e, sp=spar: e.square(sqc, trk_st[sp]), "s_ev")
                    X.wait(V, "s_ev", X.cnt["s_ev"])
                    X.op(V, lambda e, tk=tokt: e.reduce_sum(
                        stats1[:, 64 + tk:64 + tk + 1], sqc, axis=AX.X), "v_ms")
                    X.wait(STQ, "v_ms", X.cnt["v_ms"] - 1)
                    X.dma(STQ, d_trunk[t0:t0 + 128, :], trk_st[spar][:, :], "st_trk")
                    n_st_trk += 16
                if tg2 == 7:
                    X.wait(STQ, "v_ms", X.cnt["v_ms"])
                    X.dma(STQ, d_stA_in[:, 0:32], stats1[:, 0:32], "st_stats")
                    X.dma(STQ, d_stA_in[:, 32:64], stats1[:, 64:96], "st_stats")
                    X.wait(G, "st_stats", 32)
                    X.cc("AllReduce", ALU.add, [d_stA_in.ap().opt()],
                         [d_stA_out.ap().opt()], "cc_st")
            ev_2a_end = X.cnt["s_ev"]
            X.wait(STQ, "v_ms", X.cnt["v_ms"])
            X.dma(STQ, d_stB_in[:, 0:32], stats1[:, 32:64], "st_stats")
            X.dma(STQ, d_stB_in[:, 32:64], stats1[:, 96:128], "st_stats")
            X.wait(G, "st_stats", 64)
            X.cc("AllReduce", ALU.add, [d_stB_in.ap().opt()],
                 [d_stB_out.ap().opt()], "cc_stB")
            if DEBUG:
                X.wait(W, "st_trk", n_st_trk)
                X.dma(W, dbg["trunk"][:, :], d_trunk[:, :], "st_out")

            half_rstd_ev = {}

            def stat_math(h):
                X.wait(W, "cc_st" if h == 0 else "cc_stB", 1)
                src = d_stA_out if h == 0 else d_stB_out
                X.dma(W, stats2[:, h * 64:(h + 1) * 64], src[:, :], "ld_sy")
                n_syh = X.cnt["ld_sy"]
                mean_h = stats1[:, h * 32:(h + 1) * 32]
                rstd_h = stats1[:, 64 + h * 32:64 + (h + 1) * 32]
                sqm_h = sqc[:, h * 96:h * 96 + 32]
                var_h = sqc[:, h * 96 + 32:h * 96 + 64]
                lnv_h = sqc[:, h * 96 + 64:h * 96 + 96]
                X.wait(V, "ld_sy", n_syh)
                if h == 0:
                    X.wait(V, "s_ev", ev_2a_end)
                X.op(V, lambda e: e.tensor_scalar(
                    out=mean_h, in0=stats2[:, h * 64:h * 64 + 32],
                    scalar1=1.0 / D, scalar2=None, op0=ALU.mult), "v_ms")
                X.wait(V, "v_ms", X.cnt["v_ms"])
                X.op(V, lambda e: e.scalar_tensor_tensor(
                    out=sqm_h, in0=mean_h, scalar=1.0, in1=mean_h,
                    op0=ALU.mult, op1=ALU.mult), "v_ms")
                X.wait(V, "v_ms", X.cnt["v_ms"])
                X.op(V, lambda e: e.scalar_tensor_tensor(
                    out=var_h, in0=stats2[:, h * 64 + 32:h * 64 + 64],
                    scalar=1.0 / D, in1=sqm_h,
                    op0=ALU.mult, op1=ALU.subtract), "v_ms")
                X.wait(V, "v_ms", X.cnt["v_ms"])
                X.op(V, lambda e: e.tensor_scalar(
                    out=var_h, in0=var_h, scalar1=EPS, scalar2=None,
                    op0=ALU.add), "v_ms")
                X.wait(Sc, "v_ms", X.cnt["v_ms"])
                X.op(Sc, lambda e: e.activation(out=lnv_h, in_=var_h,
                                                func=AF.Ln), "s_ev")
                X.wait(Sc, "s_ev", X.cnt["s_ev"])
                X.op(Sc, lambda e: e.activation(out=rstd_h, in_=lnv_h,
                                                func=AF.Exp, scale=-0.5), "s_ev")
                half_rstd_ev[h] = X.cnt["s_ev"]

            stat_math(0)
            n_ld_trk2 = 0
            norm_v = {}
            tr_mm = {}
            evac_ev = {}

            def st_ln2(k):
                X.wait(W, "s_ev", evac_ev[k])
                tgt = d_ln2t_inA if k < 32 else d_ln2t_inB
                t0h = (k % 32) * 128
                X.dma(W, tgt[:, t0h:t0h + 128]
                      .rearrange("(j p) t -> p j t", p=128),
                      ln2t_st2[k % 2].rearrange("p (j t) -> p j t", t=128),
                      "st_ln2t")

            for tokt in range(64):
                if tokt == 32:
                    stat_math(1)
                h, loc, q = tokt // 32, tokt % 32, tokt % 2
                t0 = tokt * 128
                if tokt >= 2:
                    X.wait(W, "v_ms", norm_v[tokt - 2])
                X.wait(W, "st_trk", (tokt + 1) * 16)
                X.dma(W, trkb2[q], d_trunk[t0:t0 + 128, :], "ld_trk2")
                n_ld_trk2 += 16
                if tokt >= 1:
                    st_ln2(tokt - 1)
                X.wait(V, "ld_trk2", n_ld_trk2)
                X.wait(V, "s_ev", half_rstd_ev[h])
                if tokt >= 2:
                    X.wait(V, "t_mm", tr_mm[tokt - 2])
                X.op(V, lambda e, q=q, h=h, lc=loc, sp=q: e.tensor_scalar(
                    out=ln2c2[q], in0=trkb2[sp],
                    scalar1=stats1[:, h * 32 + lc:h * 32 + lc + 1],
                    scalar2=stats1[:, 64 + h * 32 + lc:64 + h * 32 + lc + 1],
                    op0=ALU.subtract, op1=ALU.mult), "v_ms")
                norm_v[tokt] = X.cnt["v_ms"]
                X.wait(T, "v_ms", norm_v[tokt])
                if tokt >= 2:
                    X.wait(T, "s_ev", evac_ev[tokt - 2])
                pb = PT if q == 0 else PU
                for j in range(4):
                    X.mm(pb[:, j * 128:(j + 1) * 128],
                         ln2c2[q][:, j * 128:(j + 1) * 128],
                         ident_bf[:], True, True, "t_mm" if j == 3 else None)
                tr_mm[tokt] = X.cnt["t_mm"]
                X.wait(Sc, "t_mm", tr_mm[tokt])
                if tokt >= 2:
                    X.wait(Sc, "st_ln2t", (tokt - 1) * 16)
                X.op(Sc, lambda e, q=q, pb=pb: e.activation(
                    out=ln2t_st2[q], in_=pb[:, 0:512], func=AF.Copy), "s_ev")
                evac_ev[tokt] = X.cnt["s_ev"]
            st_ln2(63)
            X.wait(G, "st_ln2t", 64 * 16)
            X.cc("AllGather", ALU.bypass, [d_ln2t_inA.ap().opt()],
                 [d_ln2t_allA.ap().opt()], "cc_ln2")
            X.cc("AllGather", ALU.bypass, [d_ln2t_inB.ap().opt()],
                 [d_ln2t_allB.ap().opt()], "cc_ln2B")
            if DEBUG:
                X.wait(W, "st_ln2t", 64 * 16)
                X.dma(W, dbg["ln2tA"][:, :], d_ln2t_inA[:, :], "st_out")
                X.dma(W, dbg["ln2tB"][:, :], d_ln2t_inB[:, :], "st_out")
                X.wait(W, "s_ev", X.cnt["s_ev"])
                X.dma(W, dbg["stats"][:, :], stats1[:, :], "st_out")

            X.wait(W, "cc_ln2", 1)
            mm_2c = X.cnt["t_mm"]
            n_ld_l2 = 0
            n_ld_trk3 = 0
            st_out0 = X.cnt["st_out"]
            psb_l = [SC4[0], SC4[1], SC4[2]]
            psu_l = [SC4[3], PS_, PV]
            mm_main = {}
            mm_lora = {}
            mm_tr = {}
            v_hw = {}
            v_out = {}
            ev_ru = {}
            ev_hT = {}

            def close_tokt(k):
                qk = k % 2
                pbk = PT if qk == 0 else PU
                hT = hT_bf if qk == 0 else hT_bf2
                psb = psb_l[k % 3]
                X.wait(T, "v_ms", v_hw[k])
                if k >= 2:
                    X.wait(T, "s_ev", ev_hT[k - 2])
                X.mm(pbk[:, 0:128], hw_bf[:], ident_bf[:], True, True, "t_mm")
                mm_tr[k] = X.cnt["t_mm"]
                X.wait(Sc, "t_mm", X.cnt["t_mm"])
                X.op(Sc, lambda e, hT=hT, pbk=pbk: e.activation(
                    out=hT[:], in_=pbk[:, 0:128], func=AF.Copy), "s_ev")
                ev_hT[k] = X.cnt["s_ev"]
                X.wait(T, "s_ev", ev_hT[k])
                X.mm(psb, hT[:], bflat[:, 0:512], False, True, "t_mm")
                mm_lora[k] = X.cnt["t_mm"]
                X.wait(V, "t_mm", mm_lora[k])
                X.wait(V, "ld_trk3", (k + 1) * 16)
                if k >= 2:
                    X.wait(V, "st_out", st_out0 + (k - 1) * 16)
                X.op(V, lambda e, ps=psb, sp=k % 2: e.tensor_tensor(
                    out=out_st[sp], in0=ps, in1=trkb3[sp], op=ALU.add), "v_ms")
                v_out[k] = X.cnt["v_ms"]
                X.wait(STQ, "v_ms", v_out[k])
                X.dma(STQ, t_out[k * 128:k * 128 + 128, :],
                      out_st[k % 2][:, :], "st_out")

            for tg2 in range(16):
                parg = tg2 % 2
                if tg2 == 0:
                    X.wait(W, "cc_ln2B", 1)
                    X.wait(W, "t_mm", mm_2c)
                if tg2 >= 2:
                    X.wait(W, "t_mm", mm_main[tg2 * 4 - 5])
                l2src = d_ln2t_allA if tg2 < 8 else d_ln2t_allB
                c0 = (tg2 % 8) * 512
                src = l2src[:, c0:c0 + 512].rearrange("(g p) t -> p g t", p=128)
                dst = l2buf[parg].rearrange("p (g t) -> p g t", t=512)
                for q in range(4):
                    X.dma(W, dst[:, q * 8:(q + 1) * 8, :],
                          src[:, q * 8:(q + 1) * 8, :], "ld_l2")
                    n_ld_l2 += 16
                for tt in range(4):
                    tokt = tg2 * 4 + tt
                    t0 = tokt * 128
                    if tokt >= 2:
                        X.wait(W, "v_ms", v_out.get(tokt - 2, 0))
                    X.dma(W, trkb3[tokt % 2], d_trunk[t0:t0 + 128, :], "ld_trk3")
                    n_ld_trk3 += 16
                    psb = psb_l[tokt % 3]
                    psu = psu_l[tokt % 3]
                    X.wait(T, "ld_l2", n_ld_l2)
                    X.wait(T, "ld_g", n_g_wbt)
                    if tokt >= 3:
                        X.wait(T, "v_ms", v_out[tokt - 3])
                        X.wait(T, "s_ev", ev_ru[tokt - 3])
                    X.mm(psb, onesr_bf[0:1, :], bbase_bf[0:1, :], True, False)
                    for dt in range(32):
                        lhs = l2buf[parg][:, dt * 512 + tt * 128:
                                          dt * 512 + tt * 128 + 128]
                        X.mm(psb, lhs, wbaseT[:, dt * 512:(dt + 1) * 512],
                             False, False)
                        X.mm(psu[:, 0:136], lhs, wrA[:, dt * 136:(dt + 1) * 136],
                             dt == 0, dt == 31, "t_mm" if dt == 31 else None)
                    mm_main[tokt] = X.cnt["t_mm"]
                    X.wait(Sc, "t_mm", mm_main[tokt])
                    if tokt >= 2:
                        X.wait(Sc, "v_ms", v_hw[tokt - 2])
                    X.op(Sc, lambda e, u=psu, r=ru2[tokt % 2]: e.activation(
                        out=r[:, 0:136], in_=u[:, 0:136], func=AF.Copy), "s_ev")
                    ev_ru[tokt] = X.cnt["s_ev"]
                    if tokt >= 1:
                        close_tokt(tokt - 1)
                    ru = ru2[tokt % 2]
                    X.wait(V, "s_ev", ev_ru[tokt])
                    X.op(V, lambda e, u=ru: e.reduce_max(s1[:], u[:, 0:8],
                                                         axis=AX.X), "v_ms")
                    X.wait(V, "v_ms", X.cnt["v_ms"])
                    X.op(V, lambda e, u=ru: e.tensor_scalar(
                        out=tm1[:], in0=u[:, 0:8], scalar1=s1[:, 0:1],
                        scalar2=NEG_BIG, op0=ALU.is_ge, op1=ALU.mult), "v_ms")
                    X.wait(V, "v_ms", X.cnt["v_ms"])
                    X.op(V, lambda e, u=ru: e.tensor_add(tm2[:], u[:, 0:8], tm1[:]),
                         "v_ms")
                    X.wait(V, "v_ms", X.cnt["v_ms"])
                    X.op(V, lambda e: e.reduce_max(s2[:], tm2[:], axis=AX.X), "v_ms")
                    X.wait(V, "v_ms", X.cnt["v_ms"])
                    X.op(V, lambda e: e.tensor_scalar(
                        out=meanb[:], in0=s1[:], scalar1=-1.0, scalar2=None,
                        op0=ALU.mult))
                    X.op(V, lambda e: e.tensor_scalar(
                        out=varb[:], in0=s2[:], scalar1=s1[:, 0:1], scalar2=None,
                        op0=ALU.subtract))
                    X.op(V, lambda e, u=ru: e.tensor_scalar(
                        out=selw[:], in0=u[:, 0:8], scalar1=s2[:, 0:1],
                        scalar2=2.0, op0=ALU.is_ge, op1=ALU.mult), "v_ms")
                    X.wait(Sc, "v_ms", X.cnt["v_ms"])
                    X.op(Sc, lambda e, u=ru: e.activation(
                        out=p8[:], in_=u[:, 0:8], func=AF.Exp, bias=meanb[:, 0:1]),
                        "s_ev")
                    X.wait(Sc, "s_ev", X.cnt["s_ev"])
                    X.op(Sc, lambda e: e.activation(out=dnm[:], in_=varb[:],
                                                    func=AF.Exp), "s_ev")
                    X.wait(V, "s_ev", X.cnt["s_ev"])
                    X.op(V, lambda e: e.tensor_scalar(
                        out=dnm[:], in0=dnm[:], scalar1=1.0, scalar2=None,
                        op0=ALU.add), "v_ms")
                    X.wait(V, "v_ms", X.cnt["v_ms"])
                    X.op(V, lambda e: e.reciprocal(dnm[:], dnm[:]), "v_ms")
                    X.wait(V, "v_ms", X.cnt["v_ms"])
                    X.op(V, lambda e: e.scalar_tensor_tensor(
                        out=cmb[:], in0=p8[:], scalar=dnm[:, 0:1], in1=selw[:],
                        op0=ALU.mult, op1=ALU.mult), "v_ms")
                    X.wait(V, "v_ms", X.cnt["v_ms"])
                    if tokt >= 1:
                        X.wait(V, "t_mm", mm_tr[tokt - 1])
                    X.op(V, lambda e, u=ru: e.tensor_tensor(
                        out=hw_bf.rearrange("p (e r) -> p e r", r=16),
                        in0=u[:, 8:136].rearrange("p (e r) -> p e r", r=16),
                        in1=cmb[:, :, None].broadcast_to([128, 8, 16]),
                        op=ALU.mult), "v_ms")
                    v_hw[tokt] = X.cnt["v_ms"]
            close_tokt(63)

            if DEBUG >= 2:
                X.wait(W, "s_ev", X.cnt["s_ev"])
                X.wait(W, "t_mm", X.cnt["t_mm"])
                X.dma(W, dbg["wrA"][:, :], wrA[:, :], "st_out")
                X.dma(W, dbg["wbt"][:, :], wbaseT, "st_out")
                X.dma(W, dbg["bflat"][:, :], bflat[:, :], "st_out")
            X.wait(W, "st_out", X.cnt["st_out"])

        @block.sync
        def _(eng):
            ctx.set_engine("sync", eng)
            script(ctx)
            final_counts.append(dict(ctx.cnt))

        @block.gpsimd
        def _(eng):
            ctx.set_engine("gpsimd", eng)
            script(ctx)
            final_counts.append(dict(ctx.cnt))

        @block.vector
        def _(eng):
            ctx.set_engine("vector", eng)
            script(ctx)
            final_counts.append(dict(ctx.cnt))

        @block.scalar
        def _(eng):
            ctx.set_engine("scalar", eng)
            script(ctx)
            final_counts.append(dict(ctx.cnt))

        @block.tensor
        def _(eng):
            ctx.set_engine("tensor", eng)
            script(ctx)
            final_counts.append(dict(ctx.cnt))

    assert all(fc == final_counts[0] for fc in final_counts[1:]), "pass divergence"
    return nc


_NC_CACHE = {}


def _get_nc():
    key = ("v1", DEBUG)
    if key not in _NC_CACHE:
        _NC_CACHE[key] = build_nc()
    return _NC_CACHE[key]


def kernel(L0_attn_hidden_states, L0_attn_w_attn, L0_attn_b_attn, L0_attn_w_proj,
           L0_attn_b_proj, L0_ffn_W_base, L0_ffn_b_base, L0_ffn_W_router,
           L0_ffn_A_stack, L0_ffn_B_stack):
    x = np.ascontiguousarray(np.asarray(L0_attn_hidden_states, dtype=np.float32)
                             .reshape(NT, D))
    w_attn = np.asarray(L0_attn_w_attn, dtype=np.float32)
    b_attn = np.asarray(L0_attn_b_attn, dtype=np.float32)
    w_proj = np.asarray(L0_attn_w_proj, dtype=np.float32)
    b_proj = np.asarray(L0_attn_b_proj, dtype=np.float32)
    W_base = np.asarray(L0_ffn_W_base, dtype=np.float32)
    b_base = np.asarray(L0_ffn_b_base, dtype=np.float32)
    W_rout = np.asarray(L0_ffn_W_router, dtype=np.float32)
    A_stk = np.asarray(L0_ffn_A_stack, dtype=np.float32).reshape(128, D)
    B_stk = np.asarray(L0_ffn_B_stack, dtype=np.float32)

    ident = np.eye(128, dtype=np.float32)
    ones_sq = np.ones((128, 128), dtype=np.float32)
    ones_row = np.ones((1, 128), dtype=np.float32)
    iota = (np.arange(32, dtype=np.float32)[None, :] * 128
            + np.arange(128, dtype=np.float32)[:, None])
    masks = np.zeros((128, 4 * 512), dtype=np.float32)
    for j in range(4):
        dk = np.arange(128)[:, None]
        dq = np.arange(512)[None, :]
        masks[:, j * 512:(j + 1) * 512] = np.where(j * 128 + dk <= dq, 0.0, -300000.0)
    masks = masks.astype(ml_dtypes.bfloat16)
    slopes = _alibi_slopes(H)

    wrA_host = np.empty((128, 32, 136), dtype=np.float32)
    for dt in range(32):
        wrA_host[:, dt, 0:8] = W_rout[:, dt * 128:(dt + 1) * 128].T
        wrA_host[:, dt, 8:136] = A_stk[:, dt * 128:(dt + 1) * 128].T
    wrA_host = np.ascontiguousarray(wrA_host.reshape(128, 32 * 136))

    def _ksq3(core):
        out = np.empty((3, HL, 4480), dtype=ml_dtypes.bfloat16)
        mm_ = np.arange(4480, dtype=np.float64) - 384.0
        for hl in range(HL):
            sl = float(slopes[core * HL + hl])
            Xv = -128.0 * sl * mm_
            a = Xv.astype(ml_dtypes.bfloat16)
            r = Xv - a.astype(np.float64)
            b2 = r.astype(ml_dtypes.bfloat16)
            c3 = (r - b2.astype(np.float64)).astype(ml_dtypes.bfloat16)
            out[0, hl], out[1, hl], out[2, hl] = a, b2, c3
        return np.ascontiguousarray(out.reshape(3, HL * 4480))

    in_maps = []
    for c in range(8):
        qc = slice(c * OC, (c + 1) * OC)
        w_qkv_slice = np.concatenate(
            [w_attn[:, qc], w_attn[:, D + c * OC: D + (c + 1) * OC],
             w_attn[:, 2 * D + c * OC: 2 * D + (c + 1) * OC]], axis=1)
        b_qkv_slice = np.concatenate(
            [b_attn[qc], b_attn[D + c * OC: D + (c + 1) * OC],
             b_attn[2 * D + c * OC: 2 * D + (c + 1) * OC]])
        b_qkv_pp = np.ascontiguousarray(b_qkv_slice.reshape(12, 128).T)
        in_maps.append({
            "x_rows": np.ascontiguousarray(x[c * TOKB:(c + 1) * TOKB, :]),
            "x_cols": np.ascontiguousarray(x[:, qc]),
            "w_qkv_slice": np.ascontiguousarray(w_qkv_slice),
            "b_qkv_pp": b_qkv_pp,
            "w_proj_cols": np.ascontiguousarray(w_proj[:, qc]),
            "b_proj_row": np.ascontiguousarray(b_proj[qc][None, :]),
            "w_baseT": np.ascontiguousarray(W_base[qc, :].T),
            "b_base_row": np.ascontiguousarray(b_base[qc][None, :]),
            "wrA_host": wrA_host,
            "b_flat_my": np.ascontiguousarray(
                B_stk[:, qc, :].transpose(0, 2, 1).reshape(128, OC)),
            "ident": ident,
            "ones_sq": ones_sq,
            "ones_row": ones_row,
            "iota_k": iota,
            "diag_masks": masks,
            "slopes_row": np.ascontiguousarray(slopes[c * HL:(c + 1) * HL][None, :]),
            "ksq3": _ksq3(c),
        })

    nc = _get_nc()
    res = run_bass_kernel_spmd(nc, in_maps, core_ids=list(range(8)))
    out = np.empty((NT, D), dtype=np.float32)
    for c in range(8):
        out[:, c * OC:(c + 1) * OC] = res.results[c]["out_cols"]
    if DEBUG:
        kernel._dbg = [res.results[c] for c in range(8)]
    return out.reshape(BATCH, S, D)


# version 14
# speedup vs baseline: 1.1103x; 1.0205x over previous
import numpy as np
import ml_dtypes
from contextlib import ExitStack

import concourse.bass as bass
import concourse.mybir as mybir
from concourse.bass_utils import run_bass_kernel_spmd

F32 = mybir.dt.float32
BF16 = mybir.dt.bfloat16
AF = mybir.ActivationFunctionType
ALU = mybir.AluOpType
AX = mybir.AxisListType

D = 4096
S = 4096
BATCH = 2
NT = BATCH * S
H = 32
DH = 128
HL = 4
NC = 8
OC = 512
TOKB = NT // NC
SCALE = 1.0 / DH
EPS = 1e-5
NEG_BIG = -30000.0

DEBUG = 0

KB = 1024


def _alibi_slopes(n_heads):
    import math
    cp2 = 2 ** math.floor(math.log2(n_heads))
    base = 2 ** (-(2 ** (-(math.log2(cp2) - 3))))
    slopes = base ** np.arange(1, cp2 + 1, dtype=np.float32)
    if cp2 != n_heads:
        extra_base = 2 ** (-(2 ** (-(math.log2(2 * cp2) - 3))))
        n_rem = min(cp2, n_heads - cp2)
        extra = extra_base ** np.arange(1, 1 + 2 * n_rem, 2, dtype=np.float32)
        slopes = np.concatenate([slopes, extra])
    return slopes.astype(np.float32)


class Ctx:

    def __init__(self, nc, sems):
        self.nc = nc
        self.sems = sems
        self.cnt = {k: 0 for k in sems}
        self.eng = None
        self.E = None

    def set_engine(self, name, eng):
        self.E = name
        self.eng = eng
        self.cnt = {k: 0 for k in self.sems}

    def dma(self, q, out, in_, sem, slow=False):
        if self.E == q:
            kw = {}
            if slow:
                kw["allow_slow_non_contiguous"] = True
            self.eng.dma_start(out=out, in_=in_, **kw).then_inc(self.sems[sem], 16)
        self.cnt[sem] += 16

    def dmat(self, q, out, in_, sem):
        if self.E == q:
            self.eng.dma_start_transpose(out=out, in_=in_) \
                .then_inc(self.sems[sem], 16)
        self.cnt[sem] += 16

    def cc(self, kind, op, ins, outs, sem):
        if self.E == "gpsimd":
            self.eng.collective_compute(
                kind, op, replica_groups=[list(range(NC))], ins=ins, outs=outs
            ).then_inc(self.sems[sem])
        self.cnt[sem] += 1

    def mm(self, out, lhsT, rhs, start, stop, sem=None):
        if self.E == "tensor":
            i = self.nc.tensor.matmul(out, lhsT, rhs, start=start, stop=stop)
            if sem is not None:
                i.then_inc(self.sems[sem], 1)
        if sem is not None:
            self.cnt[sem] += 1

    def op(self, eng_name, fn, sem=None):
        if self.E == eng_name:
            i = fn(self.eng)
            if sem is not None and i is not None:
                i.then_inc(self.sems[sem], 1)
        if sem is not None:
            self.cnt[sem] += 1

    def wait(self, who, sem, val):
        assert 0 <= val <= self.cnt[sem], (who, sem, val, self.cnt[sem])
        if self.E == who:
            self.eng.wait_ge(self.sems[sem], val)


def build_nc():
    nc = bass.Bass("TRN2", target_bir_lowering=False)

    t_x_rows = nc.dram_tensor("x_rows", [TOKB, D], F32, kind="ExternalInput")
    t_x_cols = nc.dram_tensor("x_cols", [NT, OC], F32, kind="ExternalInput")
    t_wqkv = nc.dram_tensor("w_qkv_slice", [D, 3 * OC], F32, kind="ExternalInput")
    t_bqkv = nc.dram_tensor("b_qkv_pp", [128, 12], F32, kind="ExternalInput")
    t_wproj = nc.dram_tensor("w_proj_cols", [D, OC], F32, kind="ExternalInput")
    t_bproj = nc.dram_tensor("b_proj_row", [1, OC], F32, kind="ExternalInput")
    t_wbT = nc.dram_tensor("w_baseT", [D, OC], F32, kind="ExternalInput")
    t_bbase = nc.dram_tensor("b_base_row", [1, OC], F32, kind="ExternalInput")
    t_wrA = nc.dram_tensor("wrA_host", [128, 32 * 136], F32, kind="ExternalInput")
    t_bfl = nc.dram_tensor("b_flat_my", [128, OC], F32, kind="ExternalInput")
    t_ident = nc.dram_tensor("ident", [128, 128], F32, kind="ExternalInput")
    t_ones = nc.dram_tensor("ones_sq", [128, 128], F32, kind="ExternalInput")
    t_onesr = nc.dram_tensor("ones_row", [1, 128], F32, kind="ExternalInput")
    t_iota = nc.dram_tensor("iota_k", [128, 32], F32, kind="ExternalInput")
    t_masks = nc.dram_tensor("diag_masks", [128, 4 * 512], BF16,
                             kind="ExternalInput")
    t_slope = nc.dram_tensor("slopes_row", [1, HL], F32, kind="ExternalInput")
    t_ksq = nc.dram_tensor("ksq3", [3, HL * 4480], BF16, kind="ExternalInput")

    t_out = nc.dram_tensor("out_cols", [NT, OC], F32, kind="ExternalOutput")

    d_ln1t_in = nc.dram_tensor("ln1t_in", [D, TOKB], BF16)
    d_ln1t_all = nc.dram_tensor("ln1t_all", [NC * D, TOKB], BF16, addr_space="Shared")
    d_qkvt = nc.dram_tensor("qkvt", [3 * OC, NT], BF16)
    d_attnt_in = nc.dram_tensor("attnt_in", [OC, NT], BF16)
    d_attnt_all = nc.dram_tensor("attnt_all", [NC * OC, NT], BF16, addr_space="Shared")
    d_ln2t_inA = nc.dram_tensor("ln2t_inA", [OC, S], BF16)
    d_ln2t_inB = nc.dram_tensor("ln2t_inB", [OC, S], BF16)
    d_ln2t_allA = nc.dram_tensor("ln2t_allA", [NC * OC, S], BF16,
                                 addr_space="Shared")
    d_ln2t_allB = nc.dram_tensor("ln2t_allB", [NC * OC, S], BF16,
                                 addr_space="Shared")
    d_stA_in = nc.dram_tensor("ln2statsA_in", [128, 64], F32)
    d_stA_out = nc.dram_tensor("ln2statsA_out", [128, 64], F32,
                               addr_space="Shared")
    d_stB_in = nc.dram_tensor("ln2statsB_in", [128, 64], F32)
    d_stB_out = nc.dram_tensor("ln2statsB_out", [128, 64], F32,
                               addr_space="Shared")
    d_trunk = nc.dram_tensor("trunk", [NT, OC], F32)

    dbg = {}
    if DEBUG:
        dbg["ln1t"] = nc.dram_tensor("dbg_ln1t", [D, TOKB], BF16, kind="ExternalOutput")
        dbg["qkvt"] = nc.dram_tensor("dbg_qkvt", [3 * OC, NT], BF16, kind="ExternalOutput")
        dbg["attnt"] = nc.dram_tensor("dbg_attnt", [OC, NT], BF16, kind="ExternalOutput")
        dbg["trunk"] = nc.dram_tensor("dbg_trunk", [NT, OC], F32, kind="ExternalOutput")
        dbg["ln2tA"] = nc.dram_tensor("dbg_ln2tA", [OC, S], BF16, kind="ExternalOutput")
        dbg["ln2tB"] = nc.dram_tensor("dbg_ln2tB", [OC, S], BF16, kind="ExternalOutput")
        dbg["stats"] = nc.dram_tensor("dbg_stats", [128, 128], F32, kind="ExternalOutput")
    if DEBUG >= 2:
        dbg["probe"] = nc.dram_tensor("dbg_probe", [128, 2048], F32, kind="ExternalOutput")
        dbg["wrA"] = nc.dram_tensor("dbg_wrA", [128, 32 * 136], BF16, kind="ExternalOutput")
        dbg["wbt"] = nc.dram_tensor("dbg_wbt", [128, 16384], BF16, kind="ExternalOutput")
        dbg["bflat"] = nc.dram_tensor("dbg_bflat", [128, 512], BF16, kind="ExternalOutput")

    es = ExitStack()
    final_counts = []
    with es:
        E = es.enter_context

        ident_bf = E(nc.sbuf_tensor("ident_bf", [128, 128], BF16))
        ones_bf = E(nc.sbuf_tensor("ones_bf", [128, 128], BF16))
        onesr_bf = E(nc.sbuf_tensor("onesr_bf", [1, 128], BF16))
        onesr_f = E(nc.sbuf_tensor("onesr_f", [1, 128], F32))
        iota_k = E(nc.sbuf_tensor("iota_k_sb", [128, 32], F32))
        masks_bf = E(nc.sbuf_tensor("masks_bf", [128, 4 * 512], BF16))
        slope_row = E(nc.sbuf_tensor("slope_row", [1, HL], F32))
        slope_pp = E(nc.sbuf_tensor("slope_pp", [128, HL], F32))
        bias4 = E(nc.sbuf_tensor("bias4", [128, HL], F32))
        bqkv_pp = E(nc.sbuf_tensor("bqkv_pp", [128, 12], F32))
        bproj_bf = E(nc.sbuf_tensor("bproj_bf", [1, OC], BF16))
        bbase_bf = E(nc.sbuf_tensor("bbase_bf", [1, OC], BF16))
        wrA = E(nc.sbuf_tensor("wrA", [128, 32 * 136], BF16))
        bflat = E(nc.sbuf_tensor("bflat", [128, 4 * 128], BF16))
        stats1 = E(nc.sbuf_tensor("stats1", [128, 128], F32))
        stats2 = E(nc.sbuf_tensor("stats2", [128, 128], F32))
        s1 = E(nc.sbuf_tensor("s1", [128, 1], F32))
        s2 = E(nc.sbuf_tensor("s2", [128, 1], F32))
        meanb = E(nc.sbuf_tensor("meanb", [128, 1], F32))
        varb = E(nc.sbuf_tensor("varb", [128, 1], F32))
        rstdb = E(nc.sbuf_tensor("rstdb", [128, 1], F32))
        tm1 = E(nc.sbuf_tensor("tm1", [128, 8], F32))
        tm2 = E(nc.sbuf_tensor("tm2", [128, 8], F32))
        qmax2 = E(nc.sbuf_tensor("qmax2", [128, 1], F32))
        kmax2 = E(nc.sbuf_tensor("kmax2", [128, 1], F32))
        qkmax = E(nc.sbuf_tensor("qkmax", [128, 1], F32))
        p8 = E(nc.sbuf_tensor("p8", [128, 8], F32))
        selw = E(nc.sbuf_tensor("selw", [128, 8], F32))
        cmb = E(nc.sbuf_tensor("cmb", [128, 8], F32))
        hw_bf = E(nc.sbuf_tensor("hw_bf", [128, 128], BF16))
        hT_bf = E(nc.sbuf_tensor("hT_bf", [128, 128], BF16))
        hT_bf2 = E(nc.sbuf_tensor("hT_bf2", [128, 128], BF16))
        dnm = E(nc.sbuf_tensor("dnm", [128, 1], F32))

        AR_BYTES = 185 * KB
        arena = E(nc.sbuf_tensor("arena", [128, AR_BYTES // 2], BF16))

        def abf(off_kb, nbytes):
            o = int(off_kb * KB)
            assert o + nbytes <= AR_BYTES
            return arena[:, o // 2:(o + nbytes) // 2]

        def af32(off_kb, nbytes):
            return abf(off_kb, nbytes).bitcast(F32)

        wq_sb = abf(0, 96 * KB)
        ksq3v = abf(0, 35 * KB)[0:3, :]
        wbaseT = abf(0, 32 * KB)
        wproj_sb = abf(64, 32 * KB)

        xbuf = [af32(96, 16 * KB), af32(112, 16 * KB)]
        sqbuf = af32(128, 16 * KB)
        ln1_tok = abf(144, 8 * KB)
        ln1t_st = abf(152, 8 * KB)
        lnh = [abf(96, 16 * KB), abf(112, 16 * KB),
               abf(128, 16 * KB), abf(144, 16 * KB)]
        ktb = [abf(96, 8 * KB), abf(120, 8 * KB)]
        vtb = [abf(104, 8 * KB), abf(128, 8 * KB)]
        qtb = [abf(112, 8 * KB), abf(136, 8 * KB)]
        vsb2b = [abf(144, 8 * KB), abf(164, 8 * KB)]
        sq_bf = abf(152, 8 * KB)
        attbuf = [abf(96, 32 * KB), abf(128, 32 * KB)]
        l2buf = [abf(96, 32 * KB), abf(128, 32 * KB)]
        expb2 = [abf(160, 2 * KB), abf(162, 2 * KB)]
        rec = af32(178, 2 * KB)
        att_st = [abf(180, 1 * KB), abf(181, 1 * KB)]
        qkv_st = [abf(183, 1 * KB), abf(184, 1 * KB)]
        xcbuf = [af32(160, 2 * KB), af32(162, 2 * KB)]
        trk_st = [af32(164, 2 * KB), af32(166, 2 * KB)]
        sqc = af32(168, 2 * KB)
        trkb2 = [af32(170, 2 * KB), af32(172, 2 * KB)]
        ln2c2 = [abf(174, 1 * KB), abf(160, 1 * KB)]
        ln2t_st2 = [abf(175, 1 * KB), abf(161, 1 * KB)]
        trkb3 = [af32(176, 2 * KB), af32(178, 2 * KB)]
        out_st = [af32(180, 2 * KB), af32(182, 2 * KB)]
        ru2 = [af32(32, 1 * KB), af32(33, 1 * KB)]

        PA = E(nc.psum_tensor("PA", [128, 1024], F32))
        PB = E(nc.psum_tensor("PB", [128, 1024], F32))
        PS_ = E(nc.psum_tensor("PS", [128, 512], F32))
        PV = E(nc.psum_tensor("PV", [128, 512], F32))
        PT = E(nc.psum_tensor("PT", [128, 512], F32))
        PU = E(nc.psum_tensor("PU", [128, 512], F32))
        SC4 = [PA[:, 0:512], PA[:, 512:1024], PB[:, 0:512], PB[:, 512:1024]]

        sem_names = [
            "ld_sy", "ld_g", "st_ln1t", "st_qkv", "st_att", "st_trk", "st_stats",
            "st_ln2t", "st_out", "ld_ln", "ld_pair", "ld_att", "ld_xc", "ld_trk2",
            "ld_l2", "ld_trk3", "t_mm", "s_ev", "v_ms",
            "cc_ln1", "cc_att", "cc_st", "cc_stB", "cc_ln2", "cc_ln2B",
        ]
        sems = {n: E(nc.semaphore(n)) for n in sem_names}
        block = E(nc.Block())
        ctx = Ctx(nc, sems)

        def script(X: Ctx):
            W, G, V, Sc, T = "sync", "gpsimd", "vector", "scalar", "tensor"
            STQ = G

            X.dma(W, iota_k[:], t_iota[:, :], "ld_sy")
            X.dma(W, slope_row[:], t_slope[:, :], "ld_sy")
            X.dma(W, bqkv_pp[:], t_bqkv[:, :], "ld_sy")
            X.dma(W, onesr_f[:], t_onesr[:, :], "ld_sy")
            n_sy1 = X.cnt["ld_sy"]
            X.dma(G, ident_bf[:], t_ident[:, :], "ld_g")
            X.dma(G, ones_bf[:], t_ones[:, :], "ld_g")
            X.dma(G, onesr_bf[:], t_onesr[:, :], "ld_g")
            X.dma(G, masks_bf[:], t_masks[:, :], "ld_g")
            X.dma(G, bproj_bf[:], t_bproj[:, :], "ld_g")
            X.dma(G, bbase_bf[:], t_bbase[:, :], "ld_g")
            X.dma(G, wrA[:, :], t_wrA[:, :], "ld_g")
            X.dma(G, bflat[:, :], t_bfl[:, :], "ld_g")
            n_g0 = X.cnt["ld_g"]
            src_wq = t_wqkv.rearrange("(dt p) j -> p dt j", p=128)
            dst_wq = wq_sb.rearrange("p (dt j) -> p dt j", j=1536)
            for i in range(4):
                X.dma(G, dst_wq[:, i * 8:(i + 1) * 8, :],
                      src_wq[:, i * 8:(i + 1) * 8, :], "ld_g")
            n_g_wq = X.cnt["ld_g"]

            X.wait(T, "ld_sy", n_sy1)
            X.mm(PU[:, 0:HL], onesr_f[0:1, :], slope_row[0:1, :], True, True, "t_mm")
            X.wait(V, "t_mm", X.cnt["t_mm"])
            X.op(V, lambda e: e.tensor_copy(out=slope_pp[:], in_=PU[:, 0:HL]), "v_ms")
            v_slope = X.cnt["v_ms"]

            ev_ln0 = X.cnt["s_ev"]
            for i in range(8):
                if i >= 2:
                    X.wait(W, "v_ms", v_slope + 4 * i - 4)
                X.dma(W, xbuf[i % 2], t_x_rows[i * 128:(i + 1) * 128, :], "ld_sy")
                ld_here = X.cnt["ld_sy"]
                X.wait(V, "ld_sy", ld_here)
                X.op(V, lambda e, b=xbuf[i % 2]: e.reduce_sum(s1[:], b, axis=AX.X))
                X.wait(Sc, "ld_sy", ld_here)
                if i > 0:
                    X.wait(Sc, "v_ms", v_slope + 4 * i - 3)
                X.op(Sc, lambda e, b=xbuf[i % 2]: e.square(sqbuf, b), "s_ev")
                sq_done = X.cnt["s_ev"]
                X.wait(V, "s_ev", sq_done)
                X.op(V, lambda e: e.reduce_sum(s2[:], sqbuf, axis=AX.X))
                X.op(V, lambda e: e.tensor_scalar(out=meanb[:], in0=s1[:],
                                                  scalar1=1.0 / D, scalar2=None,
                                                  op0=ALU.mult), "v_ms")
                X.wait(V, "v_ms", X.cnt["v_ms"])
                X.op(V, lambda e: e.tensor_scalar(out=tm1[:, 0:1], in0=meanb[:],
                                                  scalar1=meanb[:, 0:1], scalar2=EPS,
                                                  op0=ALU.mult, op1=ALU.subtract),
                     "v_ms")
                X.wait(V, "v_ms", X.cnt["v_ms"])
                X.op(V, lambda e: e.scalar_tensor_tensor(
                    out=varb[:], in0=s2[:], scalar=1.0 / D, in1=tm1[:, 0:1],
                    op0=ALU.mult, op1=ALU.subtract), "v_ms")
                var_done = X.cnt["v_ms"]
                X.wait(Sc, "v_ms", var_done)
                X.op(Sc, lambda e: e.activation(out=varb[:], in_=varb[:], func=AF.Ln),
                     "s_ev")
                X.wait(Sc, "s_ev", X.cnt["s_ev"])
                X.op(Sc, lambda e: e.activation(out=rstdb[:], in_=varb[:],
                                                func=AF.Exp, scale=-0.5), "s_ev")
                rstd_done = X.cnt["s_ev"]
                X.wait(V, "s_ev", rstd_done)
                X.op(V, lambda e, b=xbuf[i % 2]: e.tensor_scalar(
                    out=ln1_tok, in0=b, scalar1=meanb[:, 0:1], scalar2=rstdb[:, 0:1],
                    op0=ALU.subtract, op1=ALU.mult), "v_ms")
                nrm_done = X.cnt["v_ms"]
                assert nrm_done == v_slope + 4 * i + 4
                X.wait(T, "ld_g", n_g0)
                X.wait(T, "v_ms", nrm_done)
                for grp in range(8):
                    if i > 0 or grp > 0:
                        X.wait(T, "s_ev", rstd_done + grp)
                    for j in range(4):
                        dt = grp * 4 + j
                        X.mm(PT[:, j * 128:(j + 1) * 128],
                             ln1_tok[:, dt * 128:(dt + 1) * 128], ident_bf[:],
                             True, True, "t_mm" if j == 3 else None)
                    X.wait(Sc, "t_mm", X.cnt["t_mm"])
                    if grp == 0 and i > 0:
                        X.wait(Sc, "st_ln1t", i * 16)
                    X.op(Sc, lambda e, g=grp: e.activation(
                        out=ln1t_st[:, g * 512:(g + 1) * 512], in_=PT[:, 0:512],
                        func=AF.Copy), "s_ev")
                X.wait(STQ, "s_ev", X.cnt["s_ev"])
                X.dma(STQ, d_ln1t_in[:, i * 128:(i + 1) * 128]
                      .rearrange("(dt p) t -> p dt t", p=128),
                      ln1t_st.rearrange("p (dt t) -> p dt t", t=128), "st_ln1t")
            X.wait(G, "st_ln1t", 8 * 16)
            X.cc("AllGather", ALU.bypass, [d_ln1t_in.ap().opt()],
                 [d_ln1t_all.ap().opt()], "cc_ln1")
            if DEBUG:
                X.wait(W, "st_ln1t", 8 * 16)
                X.dma(W, dbg["ln1t"][:, :], d_ln1t_in[:, :], "st_out")

            X.wait(W, "cc_ln1", 1)
            ev_1a = X.cnt["s_ev"]
            mm_1a = X.cnt["t_mm"]

            def lnslot(tokg, h):
                return (2 * tokg + h) % 4

            n_ld_ln = 0
            colg_idx = 0
            for tokg in range(16):
                r = tokg // 2
                c0 = (tokg % 2) * 512
                for h in range(2):
                    if tokg >= 2:
                        X.wait(W, "t_mm", mm_1a + (tokg - 1) * 12)
                    src = d_ln1t_all[r * D:(r + 1) * D, c0:c0 + 512] \
                        .rearrange("(dt p) t -> p dt t", p=128)
                    X.dma(W, lnh[lnslot(tokg, h)]
                          .rearrange("p (dt t) -> p dt t", t=512),
                          src[:, h * 16:(h + 1) * 16, :], "ld_ln")
                    n_ld_ln += 16
                for colg in range(12):
                    ps = SC4[colg_idx % 4]
                    X.wait(T, "ld_ln", n_ld_ln)
                    X.wait(T, "ld_g", n_g_wq)
                    if colg_idx >= 4:
                        X.wait(T, "s_ev", ev_1a + colg_idx - 3)
                    for dt in range(32):
                        hh = dt // 16
                        lhs = wq_sb[:, dt * 1536 + colg * 128:
                                    dt * 1536 + (colg + 1) * 128]
                        rhs = lnh[lnslot(tokg, hh)][:, (dt % 16) * 512:
                                                    (dt % 16 + 1) * 512]
                        X.mm(ps, lhs, rhs, dt == 0, dt == 31,
                             "t_mm" if dt == 31 else None)
                    X.wait(Sc, "t_mm", X.cnt["t_mm"])
                    if colg_idx >= 2:
                        X.wait(Sc, "st_qkv", (colg_idx - 1) * 16)
                    X.op(Sc, lambda e, ps=ps, cg=colg, st=qkv_st[colg_idx % 2]:
                         e.activation(out=st, in_=ps, func=AF.Identity,
                                      bias=bqkv_pp[:, cg:cg + 1]), "s_ev")
                    X.wait(STQ, "s_ev", X.cnt["s_ev"])
                    g0 = tokg * 512
                    X.dma(STQ, d_qkvt[colg * 128:(colg + 1) * 128, g0:g0 + 512],
                          qkv_st[colg_idx % 2][:, :], "st_qkv")
                    colg_idx += 1
            assert colg_idx == 192
            if DEBUG:
                X.wait(W, "st_qkv", 192 * 16)
                X.dma(W, dbg["qkvt"][:, :], d_qkvt[:, :], "st_out")

            X.wait(G, "t_mm", mm_1a + 192)
            src_wp = t_wproj.rearrange("(g p) o -> p g o", p=128)
            dst_wp = wproj_sb.rearrange("p (g o) -> p g o", o=512)
            for i in range(2):
                X.dma(G, dst_wp[:, i * 16:(i + 1) * 16, :],
                      src_wp[:, i * 16:(i + 1) * 16, :], "ld_g")
            n_g_wproj = X.cnt["ld_g"]

            st_att_n = 0
            n_ld_pair = 0
            mm_1a_end = X.cnt["t_mm"]
            pair_mm_end = {}
            ksq_ld = {}
            exp_hist = []
            att_v2 = {0: 0, 1: 0}
            qt_ctr = 0
            gidx = 0

            def pair_load(p_idx):
                nonlocal n_ld_pair
                hl, b = p_idx // 2, p_idx % 2
                par = p_idx % 2
                X.wait(W, "st_qkv", 192 * 16)
                if p_idx >= 2:
                    X.wait(W, "t_mm", pair_mm_end[p_idx - 2])
                for (buf, row0) in ((ktb[par], 512 + hl * 128),
                                    (qtb[par], hl * 128)):
                    X.dma(W, buf, d_qkvt[row0:row0 + 128, b * S:(b + 1) * S],
                          "ld_pair")
                    n_ld_pair += 16
                vr = 1024 + hl * 128
                for kt in range(32):
                    X.dmat(W, vsb2b[par][:, kt * 128:(kt + 1) * 128],
                           d_qkvt[vr:vr + 128,
                                  b * S + kt * 128:b * S + (kt + 1) * 128],
                           "ld_pair")
                    n_ld_pair += 16

            X.wait(W, "t_mm", mm_1a_end)
            X.dma(W, ksq3v, t_ksq[:, :], "ld_sy")
            ksq_ld_n = X.cnt["ld_sy"]

            X.op(V, lambda e: e.tensor_tensor(
                out=bias4[:, 0:HL],
                in0=iota_k[:, 0:1].broadcast_to([128, HL]),
                in1=slope_pp[:, 0:HL], op=ALU.mult), "v_ms")
            bias_done = X.cnt["v_ms"]

            pair_load(0)
            pair_load(1)
            for p_idx in range(8):
                hl, b = p_idx // 2, p_idx % 2
                par = p_idx % 2
                vsb = vsb2b[par]

                X.wait(T, "ld_pair", (p_idx + 1) * 34 * 16)
                if p_idx == 0:
                    X.wait(T, "ld_sy", ksq_ld_n)

                groups = [(qt, g) for qt in range(8) for g in range(2 * qt + 2)]
                sc_end = {}
                ex_end = {}
                rs_end = {}

                def emit_rsav(k):
                    nonlocal st_att_n, qt_ctr
                    qtk, gk = groups[k]
                    slot = expb2[(gbase + k) % 2]
                    first = gk == 0
                    last = gk == 2 * qtk + 1
                    PSq = PS_ if qtk % 2 == 0 else PT
                    PVq = PV if qtk % 2 == 0 else PU
                    X.wait(T, "s_ev", ex_end[k])
                    if first:
                        X.wait(T, "v_ms", att_v2[qtk % 2])
                    for j in range(2):
                        X.mm(PSq[:, 0:512], ones_bf[:],
                             slot[:, j * 512:(j + 1) * 512],
                             first and j == 0, last and j == 1, None)
                    for j in range(2):
                        kt = 2 * gk + j
                        X.mm(PVq[:, 0:512], vsb[:, kt * 128:(kt + 1) * 128],
                             slot[:, j * 512:(j + 1) * 512],
                             first and j == 0, last and j == 1,
                             "t_mm" if j == 1 else None)
                    rs_end[k] = X.cnt["t_mm"]
                    if last:
                        X.wait(V, "t_mm", X.cnt["t_mm"])
                        X.op(V, lambda e, PSq=PSq: e.reciprocal(
                            rec, PSq[:, 0:512]), "v_ms")
                        X.wait(V, "v_ms", X.cnt["v_ms"])
                        sidx = qt_ctr % 2
                        if qt_ctr >= 2:
                            X.wait(V, "st_att", st_att_n - 16)
                        X.op(V, lambda e, st=att_st[sidx], PVq=PVq:
                             e.tensor_tensor(out=st, in0=PVq[:, 0:512],
                                             in1=rec, op=ALU.mult), "v_ms")
                        att_v2[qtk % 2] = X.cnt["v_ms"]
                        X.wait(W, "v_ms", att_v2[qtk % 2])
                        X.dma(W, d_attnt_in[hl * 128:(hl + 1) * 128,
                                            b * S + qtk * 512: b * S + (qtk + 1) * 512],
                              att_st[sidx][:, :], "st_att")
                        st_att_n += 16
                        qt_ctr += 1

                gbase = gidx
                for i, (qt, g) in enumerate(groups):
                    ps = PA if (gbase + i) % 2 == 0 else PB
                    slot = expb2[(gbase + i) % 2]
                    diag = g >= 2 * qt
                    if gbase + i >= 2:
                        X.wait(T, "s_ev", exp_hist[gbase + i - 2])
                    for j in range(2):
                        kt = 2 * g + j
                        o = 4 * qt - kt
                        k0 = hl * 4480 + (o + 3) * 128
                        X.mm(ps[:, j * 512:(j + 1) * 512], ones_bf[0:3, :],
                             ksq3v[:, k0:k0 + 512], True, False)
                        X.mm(ps[:, j * 512:(j + 1) * 512],
                             ktb[par][:, kt * 128:(kt + 1) * 128],
                             qtb[par][:, qt * 512:(qt + 1) * 512], False, True,
                             "t_mm" if j == 1 else None)
                    sc_end[i] = X.cnt["t_mm"]
                    if i >= 2:
                        emit_rsav(i - 2)
                    if diag:
                        X.wait(V, "t_mm", sc_end[i])
                        moff = 0 if g == 2 * qt else 1024
                        X.op(V, lambda e, ps=ps, mo=moff: e.tensor_tensor(
                            out=ps[:, 0:1024], in0=ps[:, 0:1024],
                            in1=masks_bf[:, mo:mo + 1024], op=ALU.add), "v_ms")
                        X.wait(Sc, "v_ms", X.cnt["v_ms"])
                    X.wait(Sc, "t_mm", rs_end[i - 2] if i >= 2 else sc_end[i])
                    if i == 0:
                        X.wait(Sc, "v_ms", bias_done)
                    X.op(Sc, lambda e, ps=ps, st=slot: e.activation(
                        out=st, in_=ps[:, 0:1024], func=AF.Exp,
                        bias=bias4[:, hl:hl + 1], scale=SCALE), "s_ev")
                    ex_end[i] = X.cnt["s_ev"]
                    exp_hist.append(ex_end[i])
                emit_rsav(len(groups) - 2)
                emit_rsav(len(groups) - 1)
                gidx = gbase + len(groups)
                pair_mm_end[p_idx] = X.cnt["t_mm"]
                if p_idx + 2 < 8:
                    pair_load(p_idx + 2)

            X.wait(G, "st_att", st_att_n)
            X.cc("AllGather", ALU.bypass, [d_attnt_in.ap().opt()],
                 [d_attnt_all.ap().opt()], "cc_att")
            if DEBUG:
                X.wait(W, "st_att", st_att_n)
                X.dma(W, dbg["attnt"][:, :], d_attnt_in[:, :], "st_out")

            X.wait(G, "cc_att", 1)
            X.wait(G, "t_mm", pair_mm_end[7])
            src_wb = t_wbT.rearrange("(dt p) o -> p dt o", p=128)
            dst_wb = wbaseT.rearrange("p (dt o) -> p dt o", o=512)
            for i in range(2):
                X.dma(G, dst_wb[:, i * 16:(i + 1) * 16, :],
                      src_wb[:, i * 16:(i + 1) * 16, :], "ld_g")
            n_g_wbt = X.cnt["ld_g"]

            X.wait(W, "cc_att", 1)
            mm_2a = X.cnt["t_mm"]
            ev_2a = X.cnt["s_ev"]
            vm_2a = X.cnt["v_ms"]
            n_ld_att = 0
            n_ld_xc = 0
            n_st_trk = 0
            for tg2 in range(16):
                parg = tg2 % 2
                if tg2 >= 2:
                    X.wait(W, "t_mm", mm_2a + (tg2 - 1) * 4)
                src = d_attnt_all[:, tg2 * 512:(tg2 + 1) * 512] \
                    .rearrange("(g p) t -> p g t", p=128)
                dst = attbuf[parg].rearrange("p (g t) -> p g t", t=512)
                for q in range(4):
                    X.dma(W, dst[:, q * 8:(q + 1) * 8, :],
                          src[:, q * 8:(q + 1) * 8, :], "ld_att")
                    n_ld_att += 16
                for tt in range(4):
                    tokt = tg2 * 4 + tt
                    t0 = tokt * 128
                    spar = tokt % 2
                    if tokt >= 2:
                        X.wait(W, "v_ms", vm_2a + (tokt - 2) * 2 + 1)
                    X.dma(W, xcbuf[spar], t_x_cols[t0:t0 + 128, :], "ld_xc")
                    n_ld_xc += 16
                    ps = SC4[tokt % 4]
                    X.wait(T, "ld_att", n_ld_att)
                    X.wait(T, "ld_g", n_g_wproj)
                    if tokt >= 4:
                        X.wait(T, "v_ms", vm_2a + (tokt - 4) * 2 + 1)
                    X.mm(ps, onesr_bf[0:1, :], bproj_bf[0:1, :], True, False)
                    for g in range(32):
                        X.mm(ps, attbuf[parg][:, g * 512 + tt * 128:
                                              g * 512 + tt * 128 + 128],
                             wproj_sb[:, g * 512:(g + 1) * 512],
                             False, g == 31, "t_mm" if g == 31 else None)
                    mm_here = X.cnt["t_mm"]
                    X.wait(V, "t_mm", mm_here)
                    X.wait(V, "ld_xc", n_ld_xc)
                    if tokt >= 2:
                        X.wait(V, "st_trk", n_st_trk - 16)
                        X.wait(V, "s_ev", ev_2a + tokt - 1)
                    X.op(V, lambda e, ps=ps, sp=spar: e.tensor_tensor(
                        out=trk_st[sp], in0=ps, in1=xcbuf[sp], op=ALU.add), "v_ms")
                    X.wait(V, "v_ms", X.cnt["v_ms"])
                    X.op(V, lambda e, sp=spar, tk=tokt: e.reduce_sum(
                        stats1[:, tk:tk + 1], trk_st[sp], axis=AX.X))
                    X.wait(Sc, "v_ms", X.cnt["v_ms"])
                    X.op(Sc, lambda e, sp=spar: e.square(sqc, trk_st[sp]), "s_ev")
                    X.wait(V, "s_ev", X.cnt["s_ev"])
                    X.op(V, lambda e, tk=tokt: e.reduce_sum(
                        stats1[:, 64 + tk:64 + tk + 1], sqc, axis=AX.X), "v_ms")
                    X.wait(STQ, "v_ms", X.cnt["v_ms"] - 1)
                    X.dma(STQ, d_trunk[t0:t0 + 128, :], trk_st[spar][:, :], "st_trk")
                    n_st_trk += 16
                if tg2 == 7:
                    X.wait(STQ, "v_ms", X.cnt["v_ms"])
                    X.dma(STQ, d_stA_in[:, 0:32], stats1[:, 0:32], "st_stats")
                    X.dma(STQ, d_stA_in[:, 32:64], stats1[:, 64:96], "st_stats")
                    X.wait(G, "st_stats", 32)
                    X.cc("AllReduce", ALU.add, [d_stA_in.ap().opt()],
                         [d_stA_out.ap().opt()], "cc_st")
            ev_2a_end = X.cnt["s_ev"]
            X.wait(STQ, "v_ms", X.cnt["v_ms"])
            X.dma(STQ, d_stB_in[:, 0:32], stats1[:, 32:64], "st_stats")
            X.dma(STQ, d_stB_in[:, 32:64], stats1[:, 96:128], "st_stats")
            X.wait(G, "st_stats", 64)
            X.cc("AllReduce", ALU.add, [d_stB_in.ap().opt()],
                 [d_stB_out.ap().opt()], "cc_stB")
            if DEBUG:
                X.wait(W, "st_trk", n_st_trk)
                X.dma(W, dbg["trunk"][:, :], d_trunk[:, :], "st_out")

            half_rstd_ev = {}

            def stat_math(h):
                X.wait(W, "cc_st" if h == 0 else "cc_stB", 1)
                src = d_stA_out if h == 0 else d_stB_out
                X.dma(W, stats2[:, h * 64:(h + 1) * 64], src[:, :], "ld_sy")
                n_syh = X.cnt["ld_sy"]
                mean_h = stats1[:, h * 32:(h + 1) * 32]
                rstd_h = stats1[:, 64 + h * 32:64 + (h + 1) * 32]
                sqm_h = sqc[:, h * 96:h * 96 + 32]
                var_h = sqc[:, h * 96 + 32:h * 96 + 64]
                lnv_h = sqc[:, h * 96 + 64:h * 96 + 96]
                X.wait(V, "ld_sy", n_syh)
                if h == 0:
                    X.wait(V, "s_ev", ev_2a_end)
                X.op(V, lambda e: e.tensor_scalar(
                    out=mean_h, in0=stats2[:, h * 64:h * 64 + 32],
                    scalar1=1.0 / D, scalar2=None, op0=ALU.mult), "v_ms")
                X.wait(V, "v_ms", X.cnt["v_ms"])
                X.op(V, lambda e: e.scalar_tensor_tensor(
                    out=sqm_h, in0=mean_h, scalar=1.0, in1=mean_h,
                    op0=ALU.mult, op1=ALU.mult), "v_ms")
                X.wait(V, "v_ms", X.cnt["v_ms"])
                X.op(V, lambda e: e.scalar_tensor_tensor(
                    out=var_h, in0=stats2[:, h * 64 + 32:h * 64 + 64],
                    scalar=1.0 / D, in1=sqm_h,
                    op0=ALU.mult, op1=ALU.subtract), "v_ms")
                X.wait(V, "v_ms", X.cnt["v_ms"])
                X.op(V, lambda e: e.tensor_scalar(
                    out=var_h, in0=var_h, scalar1=EPS, scalar2=None,
                    op0=ALU.add), "v_ms")
                X.wait(Sc, "v_ms", X.cnt["v_ms"])
                X.op(Sc, lambda e: e.activation(out=lnv_h, in_=var_h,
                                                func=AF.Ln), "s_ev")
                X.wait(Sc, "s_ev", X.cnt["s_ev"])
                X.op(Sc, lambda e: e.activation(out=rstd_h, in_=lnv_h,
                                                func=AF.Exp, scale=-0.5), "s_ev")
                half_rstd_ev[h] = X.cnt["s_ev"]

            stat_math(0)
            n_ld_trk2 = 0
            norm_v = {}
            tr_mm = {}
            evac_ev = {}

            def st_ln2(k):
                X.wait(W, "s_ev", evac_ev[k])
                tgt = d_ln2t_inA if k < 32 else d_ln2t_inB
                t0h = (k % 32) * 128
                X.dma(W, tgt[:, t0h:t0h + 128]
                      .rearrange("(j p) t -> p j t", p=128),
                      ln2t_st2[k % 2].rearrange("p (j t) -> p j t", t=128),
                      "st_ln2t")

            for tokt in range(64):
                if tokt == 32:
                    stat_math(1)
                h, loc, q = tokt // 32, tokt % 32, tokt % 2
                t0 = tokt * 128
                if tokt >= 2:
                    X.wait(W, "v_ms", norm_v[tokt - 2])
                X.wait(W, "st_trk", (tokt + 1) * 16)
                X.dma(W, trkb2[q], d_trunk[t0:t0 + 128, :], "ld_trk2")
                n_ld_trk2 += 16
                if tokt >= 1:
                    st_ln2(tokt - 1)
                X.wait(V, "ld_trk2", n_ld_trk2)
                X.wait(V, "s_ev", half_rstd_ev[h])
                if tokt >= 2:
                    X.wait(V, "t_mm", tr_mm[tokt - 2])
                X.op(V, lambda e, q=q, h=h, lc=loc, sp=q: e.tensor_scalar(
                    out=ln2c2[q], in0=trkb2[sp],
                    scalar1=stats1[:, h * 32 + lc:h * 32 + lc + 1],
                    scalar2=stats1[:, 64 + h * 32 + lc:64 + h * 32 + lc + 1],
                    op0=ALU.subtract, op1=ALU.mult), "v_ms")
                norm_v[tokt] = X.cnt["v_ms"]
                X.wait(T, "v_ms", norm_v[tokt])
                if tokt >= 2:
                    X.wait(T, "s_ev", evac_ev[tokt - 2])
                pb = PT if q == 0 else PU
                for j in range(4):
                    X.mm(pb[:, j * 128:(j + 1) * 128],
                         ln2c2[q][:, j * 128:(j + 1) * 128],
                         ident_bf[:], True, True, "t_mm" if j == 3 else None)
                tr_mm[tokt] = X.cnt["t_mm"]
                X.wait(Sc, "t_mm", tr_mm[tokt])
                if tokt >= 2:
                    X.wait(Sc, "st_ln2t", (tokt - 1) * 16)
                X.op(Sc, lambda e, q=q, pb=pb: e.activation(
                    out=ln2t_st2[q], in_=pb[:, 0:512], func=AF.Copy), "s_ev")
                evac_ev[tokt] = X.cnt["s_ev"]
            st_ln2(63)
            X.wait(G, "st_ln2t", 64 * 16)
            X.cc("AllGather", ALU.bypass, [d_ln2t_inA.ap().opt()],
                 [d_ln2t_allA.ap().opt()], "cc_ln2")
            X.cc("AllGather", ALU.bypass, [d_ln2t_inB.ap().opt()],
                 [d_ln2t_allB.ap().opt()], "cc_ln2B")
            if DEBUG:
                X.wait(W, "st_ln2t", 64 * 16)
                X.dma(W, dbg["ln2tA"][:, :], d_ln2t_inA[:, :], "st_out")
                X.dma(W, dbg["ln2tB"][:, :], d_ln2t_inB[:, :], "st_out")
                X.wait(W, "s_ev", X.cnt["s_ev"])
                X.dma(W, dbg["stats"][:, :], stats1[:, :], "st_out")

            X.wait(W, "cc_ln2", 1)
            mm_2c = X.cnt["t_mm"]
            n_ld_l2 = 0
            n_ld_trk3 = 0
            st_out0 = X.cnt["st_out"]
            psb_l = [SC4[0], SC4[1], SC4[2]]
            psu_l = [SC4[3], PS_, PV]
            mm_main = {}
            mm_lora = {}
            mm_tr = {}
            v_hw = {}
            v_out = {}
            ev_ru = {}
            ev_hT = {}

            def close_a(k):
                qk = k % 2
                pbk = PT if qk == 0 else PU
                hT = hT_bf if qk == 0 else hT_bf2
                X.wait(T, "v_ms", v_hw[k])
                if k >= 2:
                    X.wait(T, "s_ev", ev_hT[k - 2])
                X.mm(pbk[:, 0:128], hw_bf[:], ident_bf[:], True, True, "t_mm")
                mm_tr[k] = X.cnt["t_mm"]
                X.wait(Sc, "t_mm", X.cnt["t_mm"])
                X.op(Sc, lambda e, hT=hT, pbk=pbk: e.activation(
                    out=hT[:], in_=pbk[:, 0:128], func=AF.Copy), "s_ev")
                ev_hT[k] = X.cnt["s_ev"]

            def close_b(k):
                hT = hT_bf if k % 2 == 0 else hT_bf2
                psb = psb_l[k % 3]
                X.wait(T, "s_ev", ev_hT[k])
                X.mm(psb, hT[:], bflat[:, 0:512], False, True, "t_mm")
                mm_lora[k] = X.cnt["t_mm"]
                X.wait(V, "t_mm", mm_lora[k])
                X.wait(V, "ld_trk3", (k + 1) * 16)
                if k >= 2:
                    X.wait(V, "st_out", st_out0 + (k - 1) * 16)
                X.op(V, lambda e, ps=psb, sp=k % 2: e.tensor_tensor(
                    out=out_st[sp], in0=ps, in1=trkb3[sp], op=ALU.add), "v_ms")
                v_out[k] = X.cnt["v_ms"]
                X.wait(STQ, "v_ms", v_out[k])
                X.dma(STQ, t_out[k * 128:k * 128 + 128, :],
                      out_st[k % 2][:, :], "st_out")

            for tg2 in range(16):
                parg = tg2 % 2
                if tg2 == 0:
                    X.wait(W, "cc_ln2B", 1)
                    X.wait(W, "t_mm", mm_2c)
                if tg2 >= 2:
                    X.wait(W, "t_mm", mm_main[tg2 * 4 - 5])
                l2src = d_ln2t_allA if tg2 < 8 else d_ln2t_allB
                c0 = (tg2 % 8) * 512
                src = l2src[:, c0:c0 + 512].rearrange("(g p) t -> p g t", p=128)
                dst = l2buf[parg].rearrange("p (g t) -> p g t", t=512)
                for q in range(4):
                    X.dma(W, dst[:, q * 8:(q + 1) * 8, :],
                          src[:, q * 8:(q + 1) * 8, :], "ld_l2")
                    n_ld_l2 += 16
                for tt in range(4):
                    tokt = tg2 * 4 + tt
                    t0 = tokt * 128
                    psb = psb_l[tokt % 3]
                    psu = psu_l[tokt % 3]
                    X.wait(T, "ld_l2", n_ld_l2)
                    X.wait(T, "ld_g", n_g_wbt)
                    if tokt >= 3:
                        X.wait(T, "v_ms", v_out[tokt - 3])
                        X.wait(T, "s_ev", ev_ru[tokt - 3])
                    X.mm(psb, onesr_bf[0:1, :], bbase_bf[0:1, :], True, False)
                    for dt in range(32):
                        lhs = l2buf[parg][:, dt * 512 + tt * 128:
                                          dt * 512 + tt * 128 + 128]
                        X.mm(psb, lhs, wbaseT[:, dt * 512:(dt + 1) * 512],
                             False, False)
                        X.mm(psu[:, 0:136], lhs, wrA[:, dt * 136:(dt + 1) * 136],
                             dt == 0, dt == 31, "t_mm" if dt == 31 else None)
                    mm_main[tokt] = X.cnt["t_mm"]
                    X.wait(Sc, "t_mm", mm_main[tokt])
                    if tokt >= 2:
                        X.wait(Sc, "v_ms", v_hw[tokt - 2])
                    X.op(Sc, lambda e, u=psu, r=ru2[tokt % 2]: e.activation(
                        out=r[:, 0:136], in_=u[:, 0:136], func=AF.Copy), "s_ev")
                    ev_ru[tokt] = X.cnt["s_ev"]
                    if tokt >= 1:
                        close_a(tokt - 1)
                    if tokt >= 2:
                        close_b(tokt - 2)
                    if tokt >= 2:
                        X.wait(G, "v_ms", v_out.get(tokt - 2, 0))
                    X.dma(G, trkb3[tokt % 2], d_trunk[t0:t0 + 128, :], "ld_trk3")
                    n_ld_trk3 += 16
                    ru = ru2[tokt % 2]
                    X.wait(V, "s_ev", ev_ru[tokt])
                    X.op(V, lambda e, u=ru: e.reduce_max(s1[:], u[:, 0:8],
                                                         axis=AX.X), "v_ms")
                    X.wait(V, "v_ms", X.cnt["v_ms"])
                    X.op(V, lambda e, u=ru: e.tensor_scalar(
                        out=tm1[:], in0=u[:, 0:8], scalar1=s1[:, 0:1],
                        scalar2=NEG_BIG, op0=ALU.is_ge, op1=ALU.mult), "v_ms")
                    X.wait(V, "v_ms", X.cnt["v_ms"])
                    X.op(V, lambda e, u=ru: e.tensor_add(tm2[:], u[:, 0:8], tm1[:]),
                         "v_ms")
                    X.wait(V, "v_ms", X.cnt["v_ms"])
                    X.op(V, lambda e: e.reduce_max(s2[:], tm2[:], axis=AX.X), "v_ms")
                    X.wait(V, "v_ms", X.cnt["v_ms"])
                    X.op(V, lambda e: e.tensor_scalar(
                        out=meanb[:], in0=s1[:], scalar1=-1.0, scalar2=None,
                        op0=ALU.mult))
                    X.op(V, lambda e: e.tensor_scalar(
                        out=varb[:], in0=s2[:], scalar1=s1[:, 0:1], scalar2=None,
                        op0=ALU.subtract))
                    X.op(V, lambda e, u=ru: e.tensor_scalar(
                        out=selw[:], in0=u[:, 0:8], scalar1=s2[:, 0:1],
                        scalar2=2.0, op0=ALU.is_ge, op1=ALU.mult), "v_ms")
                    X.wait(Sc, "v_ms", X.cnt["v_ms"])
                    X.op(Sc, lambda e, u=ru: e.activation(
                        out=p8[:], in_=u[:, 0:8], func=AF.Exp, bias=meanb[:, 0:1]),
                        "s_ev")
                    X.wait(Sc, "s_ev", X.cnt["s_ev"])
                    X.op(Sc, lambda e: e.activation(out=dnm[:], in_=varb[:],
                                                    func=AF.Exp), "s_ev")
                    X.wait(V, "s_ev", X.cnt["s_ev"])
                    X.op(V, lambda e: e.tensor_scalar(
                        out=dnm[:], in0=dnm[:], scalar1=1.0, scalar2=None,
                        op0=ALU.add), "v_ms")
                    X.wait(V, "v_ms", X.cnt["v_ms"])
                    X.op(V, lambda e: e.reciprocal(dnm[:], dnm[:]), "v_ms")
                    X.wait(V, "v_ms", X.cnt["v_ms"])
                    X.op(V, lambda e: e.scalar_tensor_tensor(
                        out=cmb[:], in0=p8[:], scalar=dnm[:, 0:1], in1=selw[:],
                        op0=ALU.mult, op1=ALU.mult), "v_ms")
                    X.wait(V, "v_ms", X.cnt["v_ms"])
                    if tokt >= 1:
                        X.wait(V, "t_mm", mm_tr[tokt - 1])
                    X.op(V, lambda e, u=ru: e.tensor_tensor(
                        out=hw_bf.rearrange("p (e r) -> p e r", r=16),
                        in0=u[:, 8:136].rearrange("p (e r) -> p e r", r=16),
                        in1=cmb[:, :, None].broadcast_to([128, 8, 16]),
                        op=ALU.mult), "v_ms")
                    v_hw[tokt] = X.cnt["v_ms"]
            close_a(63)
            close_b(62)
            close_b(63)

            if DEBUG >= 2:
                X.wait(W, "s_ev", X.cnt["s_ev"])
                X.wait(W, "t_mm", X.cnt["t_mm"])
                X.dma(W, dbg["wrA"][:, :], wrA[:, :], "st_out")
                X.dma(W, dbg["wbt"][:, :], wbaseT, "st_out")
                X.dma(W, dbg["bflat"][:, :], bflat[:, :], "st_out")
            X.wait(W, "st_out", X.cnt["st_out"])

        @block.sync
        def _(eng):
            ctx.set_engine("sync", eng)
            script(ctx)
            final_counts.append(dict(ctx.cnt))

        @block.gpsimd
        def _(eng):
            ctx.set_engine("gpsimd", eng)
            script(ctx)
            final_counts.append(dict(ctx.cnt))

        @block.vector
        def _(eng):
            ctx.set_engine("vector", eng)
            script(ctx)
            final_counts.append(dict(ctx.cnt))

        @block.scalar
        def _(eng):
            ctx.set_engine("scalar", eng)
            script(ctx)
            final_counts.append(dict(ctx.cnt))

        @block.tensor
        def _(eng):
            ctx.set_engine("tensor", eng)
            script(ctx)
            final_counts.append(dict(ctx.cnt))

    assert all(fc == final_counts[0] for fc in final_counts[1:]), "pass divergence"
    return nc


_NC_CACHE = {}


def _get_nc():
    key = ("v1", DEBUG)
    if key not in _NC_CACHE:
        _NC_CACHE[key] = build_nc()
    return _NC_CACHE[key]


def kernel(L0_attn_hidden_states, L0_attn_w_attn, L0_attn_b_attn, L0_attn_w_proj,
           L0_attn_b_proj, L0_ffn_W_base, L0_ffn_b_base, L0_ffn_W_router,
           L0_ffn_A_stack, L0_ffn_B_stack):
    x = np.ascontiguousarray(np.asarray(L0_attn_hidden_states, dtype=np.float32)
                             .reshape(NT, D))
    w_attn = np.asarray(L0_attn_w_attn, dtype=np.float32)
    b_attn = np.asarray(L0_attn_b_attn, dtype=np.float32)
    w_proj = np.asarray(L0_attn_w_proj, dtype=np.float32)
    b_proj = np.asarray(L0_attn_b_proj, dtype=np.float32)
    W_base = np.asarray(L0_ffn_W_base, dtype=np.float32)
    b_base = np.asarray(L0_ffn_b_base, dtype=np.float32)
    W_rout = np.asarray(L0_ffn_W_router, dtype=np.float32)
    A_stk = np.asarray(L0_ffn_A_stack, dtype=np.float32).reshape(128, D)
    B_stk = np.asarray(L0_ffn_B_stack, dtype=np.float32)

    ident = np.eye(128, dtype=np.float32)
    ones_sq = np.ones((128, 128), dtype=np.float32)
    ones_row = np.ones((1, 128), dtype=np.float32)
    iota = (np.arange(32, dtype=np.float32)[None, :] * 128
            + np.arange(128, dtype=np.float32)[:, None])
    masks = np.zeros((128, 4 * 512), dtype=np.float32)
    for j in range(4):
        dk = np.arange(128)[:, None]
        dq = np.arange(512)[None, :]
        masks[:, j * 512:(j + 1) * 512] = np.where(j * 128 + dk <= dq, 0.0, -300000.0)
    masks = masks.astype(ml_dtypes.bfloat16)
    slopes = _alibi_slopes(H)

    wrA_host = np.empty((128, 32, 136), dtype=np.float32)
    for dt in range(32):
        wrA_host[:, dt, 0:8] = W_rout[:, dt * 128:(dt + 1) * 128].T
        wrA_host[:, dt, 8:136] = A_stk[:, dt * 128:(dt + 1) * 128].T
    wrA_host = np.ascontiguousarray(wrA_host.reshape(128, 32 * 136))

    def _ksq3(core):
        out = np.empty((3, HL, 4480), dtype=ml_dtypes.bfloat16)
        mm_ = np.arange(4480, dtype=np.float64) - 384.0
        for hl in range(HL):
            sl = float(slopes[core * HL + hl])
            Xv = -128.0 * sl * mm_
            a = Xv.astype(ml_dtypes.bfloat16)
            r = Xv - a.astype(np.float64)
            b2 = r.astype(ml_dtypes.bfloat16)
            c3 = (r - b2.astype(np.float64)).astype(ml_dtypes.bfloat16)
            out[0, hl], out[1, hl], out[2, hl] = a, b2, c3
        return np.ascontiguousarray(out.reshape(3, HL * 4480))

    in_maps = []
    for c in range(8):
        qc = slice(c * OC, (c + 1) * OC)
        w_qkv_slice = np.concatenate(
            [w_attn[:, qc], w_attn[:, D + c * OC: D + (c + 1) * OC],
             w_attn[:, 2 * D + c * OC: 2 * D + (c + 1) * OC]], axis=1)
        b_qkv_slice = np.concatenate(
            [b_attn[qc], b_attn[D + c * OC: D + (c + 1) * OC],
             b_attn[2 * D + c * OC: 2 * D + (c + 1) * OC]])
        b_qkv_pp = np.ascontiguousarray(b_qkv_slice.reshape(12, 128).T)
        in_maps.append({
            "x_rows": np.ascontiguousarray(x[c * TOKB:(c + 1) * TOKB, :]),
            "x_cols": np.ascontiguousarray(x[:, qc]),
            "w_qkv_slice": np.ascontiguousarray(w_qkv_slice),
            "b_qkv_pp": b_qkv_pp,
            "w_proj_cols": np.ascontiguousarray(w_proj[:, qc]),
            "b_proj_row": np.ascontiguousarray(b_proj[qc][None, :]),
            "w_baseT": np.ascontiguousarray(W_base[qc, :].T),
            "b_base_row": np.ascontiguousarray(b_base[qc][None, :]),
            "wrA_host": wrA_host,
            "b_flat_my": np.ascontiguousarray(
                B_stk[:, qc, :].transpose(0, 2, 1).reshape(128, OC)),
            "ident": ident,
            "ones_sq": ones_sq,
            "ones_row": ones_row,
            "iota_k": iota,
            "diag_masks": masks,
            "slopes_row": np.ascontiguousarray(slopes[c * HL:(c + 1) * HL][None, :]),
            "ksq3": _ksq3(c),
        })

    nc = _get_nc()
    res = run_bass_kernel_spmd(nc, in_maps, core_ids=list(range(8)))
    out = np.empty((NT, D), dtype=np.float32)
    for c in range(8):
        out[:, c * OC:(c + 1) * OC] = res.results[c]["out_cols"]
    if DEBUG:
        kernel._dbg = [res.results[c] for c in range(8)]
    return out.reshape(BATCH, S, D)
